# version 02 (final)
# baseline (speedup 1.0000x reference)
import numpy as np
import ml_dtypes
from contextlib import ExitStack
import concourse.bass as bass
import concourse.mybir as mybir
from concourse.bass_utils import run_bass_kernel_spmd

DT = mybir.dt
F32 = DT.float32
BF16 = DT.bfloat16
AF = mybir.ActivationFunctionType
ALU = mybir.AluOpType
AX = mybir.AxisListType.X
BF = ml_dtypes.bfloat16

T = 4096
D = 1024
NTT = T // 128
NDT = D // 128
NEG = -3.0e38


class KB:
    ENGS = ("tensor", "vector", "scalar", "gpsimd", "sync")

    def __init__(self):
        self.q = {e: [] for e in self.ENGS}
        self.cnt = {}
        self.last_wait = {e: {} for e in self.ENGS}
        self.ninstr = 0
        self.pref = ""

    def _s(self, sem):
        if sem.startswith("!"):
            return sem[1:]
        return self.pref + sem

    def now(self, sem):
        return self.cnt.get(self._s(sem), 0)

    def bump(self, sem, n=1):
        sem = self._s(sem)
        self.cnt[sem] = self.cnt.get(sem, 0) + n
        return self.cnt[sem]

    def _filter(self, eng, waits):
        ws = []
        lw = self.last_wait[eng]
        for wv in waits:
            if wv is None:
                continue
            s, v = wv
            s = self._s(s)
            if v <= 0:
                continue
            self.cnt.setdefault(s, 0)
            if v > lw.get(s, 0):
                ws.append((s, v))
                lw[s] = v
        return ws

    def op(self, eng, build, waits=(), inc=None, inc_n=1):
        ws = self._filter(eng, waits)
        incs = []
        if inc is not None:
            inc = self._s(inc)
            self.bump("!" + inc, inc_n)
            incs.append((inc, inc_n))
        import sys as _sys
        _ln = tuple(_sys._getframe(k).f_lineno for k in (1, 2, 3))

        def replay(E, sems, ws=tuple(ws), incs=tuple(incs), build=build, _ln=_ln):
            for s, v in ws:
                E.wait_ge(sems[s], v)
            try:
                ins = build(E)
            except Exception as ex:
                raise RuntimeError(f"op from line {_ln}: {ex}") from ex
            for s, n in incs:
                ins.then_inc(sems[s], n)
        self.q[eng].append(replay)
        self.ninstr += 1 + len(ws)
        return self.cnt.get(inc) if inc else None

    def wait(self, eng, waits):
        ws = self._filter(eng, waits)
        if not ws:
            return

        def replay(E, sems, ws=tuple(ws)):
            for s, v in ws:
                E.wait_ge(sems[s], v)
        self.q[eng].append(replay)
        self.ninstr += len(ws)

    def dma(self, eng, out, in_, sem, waits=()):
        return self.op(eng, lambda E, o=out, i=in_: E.dma_start(out=o, in_=i),
                       waits, inc=sem, inc_n=16)

    def dmat(self, eng, out, in_, sem, waits=()):
        return self.op(eng, lambda E, o=out, i=in_: E.dma_start_transpose(o, i),
                       waits, inc=sem, inc_n=16)

    def finalize(self, nc, stack):
        sems = {name: stack.enter_context(nc.semaphore(name)) for name in self.cnt}
        with nc.Block() as block:
            for e in self.ENGS:
                fns = self.q[e]
                if not fns:
                    continue
                getattr(block, e)(lambda E, fns=fns: [f(E, sems) for f in fns])
        return sems


class Ctx:
    def __init__(self, nc, kb, stack, pref=""):
        self.nc = nc
        self.kb = kb
        self.stack = stack
        self.pref = pref
        self.sb = {}
        self.ps = None
        self.consts = {}
        self.free = {}

    def alloc(self, name, shape, dtype):
        t = self.stack.enter_context(self.nc.sbuf_tensor(self.pref + name, shape, dtype))
        self.sb[name] = t
        return t

    def alloc_ps(self):
        self.ps = self.stack.enter_context(
            self.nc.psum_tensor(self.pref + "PS", [128, 4096], F32))
        return self.ps

    def bank(self, b, rows=128, cols=512):
        return self.ps[0:rows, 512 * b:512 * b + cols]

    def fwait(self, tag):
        return self.free.get(tag)

    def setfree(self, tag, sem, val):
        self.free[tag] = (sem, val)

    def const(self, name, arr):
        name = self.pref + name
        if name in self.consts:
            return self.consts[name][0]
        dt = {np.dtype(np.float32): F32, np.dtype(BF): BF16,
              np.dtype(np.int32): DT.int32}[arr.dtype]
        ap = self.nc.dram_tensor(name, list(arr.shape), dt, kind="ExternalInput").ap()
        self.consts[name] = (ap, np.ascontiguousarray(arr))
        return ap

    def fconst(self, v):
        v = float(v)
        if not hasattr(self, "fc_vals"):
            self.fc_vals = []
            self.fc_tile = self.alloc("FCN", [128, 16], F32)
            ap = self.nc.dram_tensor(self.pref + "c_fconsts", [128, 16], F32,
                                     kind="ExternalInput").ap()
            self.consts[self.pref + "c_fconsts"] = (ap, None)
            self.kb.dma("sync", self.fc_tile[:, :], ap[:, :], "fcl")
        if v not in self.fc_vals:
            self.fc_vals.append(v)
            assert len(self.fc_vals) <= 16
        j = self.fc_vals.index(v)
        return self.fc_tile[:, j:j + 1], ("fcl", 16)

    def const_maps(self):
        m = {n: a for n, (ap, a) in self.consts.items() if a is not None}
        if hasattr(self, "fc_vals"):
            arr = np.zeros((128, 16), np.float32)
            for j, v in enumerate(self.fc_vals):
                arr[:, j] = v
            m[self.pref + "c_fconsts"] = arr
        return m


def mm_acc(C, psum_ap, pairs, inc, waits=(), start=True, stop=True):
    n = len(pairs)
    for i, (w, l, r) in enumerate(pairs):
        allw = (list(waits) + list(w)) if i == 0 else list(w)
        C.kb.op("tensor",
                lambda E, l=l, r=r, s=(start and i == 0), t=(stop and i == n - 1), p=psum_ap:
                    E.matmul(p, l, r, start=s, stop=t),
                waits=allw, inc=inc if i == n - 1 else None)


def as_bf16(a):
    return np.ascontiguousarray(np.asarray(a).astype(BF))


def as_f32(a):
    return np.ascontiguousarray(np.asarray(a).astype(np.float32))


def aug_v_weight(WvT_math, nheads, dhead=64):
    d = WvT_math.shape[0]
    aug = np.zeros((d, nheads * (dhead + 1)), np.float32)
    for h in range(nheads):
        aug[:, (dhead + 1) * h:(dhead + 1) * h + dhead] = \
            WvT_math[:, dhead * h:dhead * h + dhead]
    return aug


def host_consts_common():
    P = {}
    P["c_negI"] = as_bf16(-np.eye(128, dtype=np.float32))
    P["c_ones1x64"] = as_f32(np.ones((1, 64)))
    m = np.zeros((128, 128), np.float32)
    m[np.triu_indices(128, 1)] = NEG
    P["c_mask"] = m
    P["c_maskT"] = np.ascontiguousarray(m.T)
    P["c_onerow"] = as_bf16(np.ones((1, T)))
    return P


def alloc_ln(C):
    C.alloc("LNX", [128, 2048], F32)
    C.alloc("LNSQ", [128, 1024], F32)
    C.alloc("LNST", [128, 8], F32)
    C.alloc("XNTM", [128, 2048], BF16)


def emit_block_ln(C, src_dram, xn_hbm, tag, src_waits=()):
    kb = C.kb
    XT, SQ, ST, XNT = C.sb["LNX"], C.sb["LNSQ"], C.sb["LNST"], C.sb["XNTM"]
    XN = [C.sb[f"XN{c}"] for c in range(NDT)]
    ld, st, vs, sc = (f"{tag}ld", f"{tag}st", f"{tag}vs", f"{tag}sc")
    for i in range(NTT):
        sl = slice((i % 2) * 1024, (i % 2) * 1024 + 1024)
        w = list(src_waits)
        if i >= 2:
            w.append((sc, i - 1))
        lv = kb.dma("sync", XT[:, sl], src_dram[128 * i:128 * (i + 1), :], ld, w)
        kb.op("vector", lambda E, sl=sl: E.reduce_sum(ST[:, 0:1], XT[:, sl], axis=AX),
              waits=[(ld, lv)], inc=vs)
        kb.op("scalar", lambda E, sl=sl: E.activation(SQ[:, :], XT[:, sl], AF.Square,
                                                      accum_out=ST[:, 1:2]),
              waits=[(ld, lv)], inc=vs)
        kb.wait("vector", [(vs, kb.now(vs))])
        kb.op("vector", lambda E: E.tensor_scalar(ST[:, 2:3], ST[:, 0:1], 1.0 / D, None, ALU.mult))
        kb.op("vector", lambda E: E.tensor_scalar(ST[:, 3:4], ST[:, 1:2], 1.0 / D, None, ALU.mult))
        kb.op("vector", lambda E: E.tensor_tensor(ST[:, 4:5], ST[:, 2:3], ST[:, 2:3], ALU.mult), inc=vs)
        kb.op("vector", lambda E: E.tensor_tensor(ST[:, 3:4], ST[:, 3:4], ST[:, 4:5], ALU.subtract),
              waits=[(vs, kb.now(vs))], inc=vs)
        epsap, epsw = C.fconst(1e-5)
        kb.op("scalar", lambda E, epsap=epsap: E.activation(ST[:, 5:6], ST[:, 3:4], AF.Sqrt, bias=epsap),
              waits=[(vs, kb.now(vs)), epsw], inc=vs)
        kb.op("vector", lambda E: E.reciprocal(ST[:, 6:7], ST[:, 5:6]),
              waits=[(vs, kb.now(vs))], inc=vs)
        kb.op("vector", lambda E: E.scalar_tensor_tensor(ST[:, 7:8], ST[:, 2:3], -1.0, ST[:, 6:7],
                                                         ALU.mult, ALU.mult),
              waits=[(vs, kb.now(vs))], inc=vs)
        w2 = [(vs, kb.now(vs))]
        if i >= 2:
            w2.append((st, 16 * (i - 1)))
        kb.op("vector", lambda E, sl=sl: E.tensor_scalar(XNT[:, sl], XT[:, sl], ST[:, 6:7], ST[:, 7:8],
                                                         ALU.mult, ALU.add),
              waits=w2, inc=sc)
        kb.dma("gpsimd", xn_hbm[128 * i:128 * (i + 1), :], XNT[:, sl], st,
               waits=[(sc, kb.now(sc))])
    fin = f"{tag}fmr"
    allst = kb.now(st)
    for c in range(NDT):
        kb.dmat("sync", XN[c][:, :], xn_hbm[:, 128 * c:128 * (c + 1)], fin, waits=[(st, allst)])
    return fin, kb.now(fin)


def emit_xn_from_hbm(C, xn_hbm, tag, waits=()):
    kb = C.kb
    XN = [C.sb[f"XN{c}"] for c in range(NDT)]
    fin = f"{tag}fmr"
    for c in range(NDT):
        kb.dmat("sync", XN[c][:, :], xn_hbm[:, 128 * c:128 * (c + 1)], fin, waits=list(waits))
    return fin, kb.now(fin)


def alloc_mha(C, nof=8, qkvw=True):
    for c in range(nof):
        C.alloc(f"OF{c}", [128, T], BF16)
    C.alloc("QA", [128, T], BF16)
    C.alloc("QB", [128, T], BF16)
    C.alloc("KA", [128, T], BF16)
    C.alloc("KB_", [128, T], BF16)
    C.alloc("VA", [128, 65 * NTT], BF16)
    C.alloc("VB", [128, 65 * NTT], BF16)
    if qkvw:
        C.alloc("QW", [128, 1024], BF16)
        C.alloc("KW", [128, 1024], BF16)
        C.alloc("VW", [128, 1040], BF16)
    C.alloc("PT", [128, 1024], BF16)
    C.alloc("MBc", [128, 2], BF16)
    C.alloc("RL", [1, 512], F32)
    C.alloc("RB", [64, 512], F32)
    C.alloc("negI", [128, 128], BF16)
    C.alloc("ones1x64", [1, 64], F32)
    C.alloc("maskT", [128, 128], F32)
    C.alloc("mask", [128, 128], F32)


def emit_mha_consts(C, P, tag="mhc"):
    kb = C.kb
    s = f"{tag}ld"
    kb.dma("sync", C.sb["negI"][:, :], C.const("c_negI", P["c_negI"]), s)
    kb.dma("sync", C.sb["ones1x64"][0:1, :], C.const("c_ones1x64", P["c_ones1x64"]), s)
    kb.dma("sync", C.sb["maskT"][:, :], C.const("c_maskT", P["c_maskT"]), s)
    kb.dma("sync", C.sb["mask"][:, :], C.const("c_mask", P["c_mask"]), s)
    return s, kb.now(s)


def emit_mha(C, tag, WqT, WkT, WvTa, xn_ready, const_ready, causal=True,
             q_scale=0.125, nheads=16, biases=None, rms=None, pair_hook=None):
    kb = C.kb
    XN = [C.sb[f"XN{c}"] for c in range(NDT)]
    OF = [C.sb[f"OF{c}"] for c in range(nheads // 2)]
    QA, QB, KA, KB_ = C.sb["QA"], C.sb["QB"], C.sb["KA"], C.sb["KB_"]
    V_ = [C.sb["VA"], C.sb["VB"]]
    QW, KW, VW = C.sb["QW"], C.sb["KW"], C.sb["VW"]
    PT, MBc = C.sb["PT"], C.sb["MBc"]
    RL, RB = C.sb["RL"], C.sb["RB"]
    negI, ones64 = C.sb["negI"], C.sb["ones1x64"]
    maskT, mask = C.sb["maskT"], C.sb["mask"]

    wld, mm, ev, vx, pxp, ofs, pd = (f"{tag}wl", f"{tag}mm", f"{tag}ev", f"{tag}vx",
                                     f"{tag}px", f"{tag}of", f"{tag}pd")
    onerow_d = C.const("c_onerow", np.ones((1, T), BF))
    orl = f"{tag}or"

    if biases is not None:
        BQ2, BK2 = C.sb["BQ2"], C.sb["BK2"]
        VBS = C.sb["VBS"]
        kb.dma("sync", BQ2[:, :], biases["bq"][:, :], orl)
        kb.dma("sync", BK2[:, :], biases["bk"][:, :], orl)
        bldv = kb.now(orl)
        if rms is None:
            kb.op("vector", lambda E: E.tensor_scalar(BQ2[:, :], BQ2[:, :], float(q_scale), None,
                                                      ALU.mult), waits=[(orl, bldv)], inc=vx)
        else:
            kb.wait("vector", [(orl, bldv)])
        bqv = kb.now(vx)
    if rms is not None:
        qnw_d, knw_d, rmseps = rms
        QNW, KNW = C.sb["QNW"], C.sb["KNW"]
        SEG, SEGT = C.sb["SEG"], C.sb["SEGT"]
        QS_, QSQ, RSTD = C.sb["QS_"], C.sb["QSQ"], C.sb["RSTD"]
        seg = np.zeros((128, 2), np.float32); seg[0:64, 0] = 1; seg[64:128, 1] = 1
        kb.dma("sync", SEG[:, :], C.const("c_seg", as_bf16(seg)), orl)
        kb.dma("sync", SEGT[0:2, :], C.const("c_segT", as_bf16(seg.T)), orl)
        kb.dma("sync", C.sb["SEGTF"][0:2, :], C.const("c_segTf", as_f32(seg.T)), orl)
        kb.dma("sync", QNW[:, 0:1], qnw_d[:, :], orl)
        kb.dma("sync", KNW[:, 0:1], knw_d[:, :], orl)
        rldv = kb.now(orl)
        kb.op("vector", lambda E: E.tensor_scalar(QNW[:, 1:2], QNW[:, 0:1], float(q_scale), None,
                                                  ALU.mult), waits=[(orl, rldv)], inc=vx)
        rmv = kb.now(vx)
        epsap, epsw = C.fconst(float(rmseps))

    npairs = nheads // 2
    for p in range(npairs):
        wv = [(pd, p), xn_ready, const_ready]
        kb.dma("sync", KA[64:65, :], onerow_d[0:1, :], orl, wv)
        kb.dma("sync", KB_[64:65, :], onerow_d[0:1, :], orl)
        kb.wait("tensor", [(orl, kb.now(orl))])
        for c in range(NDT):
            kb.dma("sync", QW[:, 128 * c:128 * (c + 1)],
                   WqT[128 * c:128 * (c + 1), 128 * p:128 * (p + 1)], wld,
                   wv if c == 0 else ())
            kb.dma("sync", KW[:, 128 * c:128 * (c + 1)],
                   WkT[128 * c:128 * (c + 1), 128 * p:128 * (p + 1)], wld)
            kb.dma("sync", VW[:, 130 * c:130 * (c + 1)],
                   WvTa[128 * c:128 * (c + 1), 130 * p:130 * (p + 1)], wld)
        if biases is not None:
            kb.dma("sync", VBS[:, 130 * (p % 2):130 * (p % 2) + 130], biases["bv"][p, :, :],
                   wld, [C.fwait(("vbs", p % 2))])
        wldv = kb.now(wld)

        for ch in range(T // 512):
            sl = slice(512 * ch, 512 * (ch + 1))
            bq, bk = ch % 2, 2 + ch % 2
            mm_acc(C, C.bank(bq),
                   [((), QW[:, 128 * c:128 * (c + 1)], XN[c][:, sl]) for c in range(NDT)],
                   inc=mm, waits=[(wld, wldv), C.fwait(("bk", bq))])
            mq = kb.now(mm)
            mm_acc(C, C.bank(bk),
                   [((), KW[:, 128 * c:128 * (c + 1)], XN[c][:, sl]) for c in range(NDT)],
                   inc=mm, waits=[C.fwait(("bk", bk))])
            mk = kb.now(mm)
            if rms is None and biases is None:
                kb.op("scalar", lambda E, sl=sl, b=bq: E.activation(
                    QA[0:64, sl], C.ps[0:64, 512 * b:512 * b + 512], AF.Copy, scale=float(q_scale)),
                    waits=[(mm, mq)], inc=ev)
                kb.op("scalar", lambda E, sl=sl, b=bq: E.activation(
                    QB[0:64, sl], C.ps[64:128, 512 * b:512 * b + 512], AF.Copy, scale=float(q_scale)),
                    inc=ev)
                C.setfree(("bk", bq), ev, kb.now(ev))
                kb.op("scalar", lambda E, sl=sl, b=bk: E.copy(KA[0:64, sl],
                                                              C.ps[0:64, 512 * b:512 * b + 512]),
                      waits=[(mm, mk)], inc=ev)
                kb.op("scalar", lambda E, sl=sl, b=bk: E.copy(KB_[0:64, sl],
                                                              C.ps[64:128, 512 * b:512 * b + 512]),
                      inc=ev)
                C.setfree(("bk", bk), ev, kb.now(ev))
            elif rms is None:
                kb.op("vector", lambda E, sl=sl, b=bq, p=p: E.tensor_scalar(
                    QA[0:64, sl], C.ps[0:64, 512 * b:512 * b + 512], float(q_scale),
                    BQ2[0:64, p:p + 1], ALU.mult, ALU.add), waits=[(mm, mq), (vx, bqv)], inc=vx)
                kb.op("vector", lambda E, sl=sl, b=bq, p=p: E.tensor_scalar(
                    QB[0:64, sl], C.ps[64:128, 512 * b:512 * b + 512], float(q_scale),
                    BQ2[64:128, p:p + 1], ALU.mult, ALU.add), inc=vx)
                C.setfree(("bk", bq), vx, kb.now(vx))
                kb.op("vector", lambda E, sl=sl, b=bk, p=p: E.tensor_scalar(
                    KA[0:64, sl], C.ps[0:64, 512 * b:512 * b + 512], 1.0,
                    BK2[0:64, p:p + 1], ALU.mult, ALU.add), waits=[(mm, mk)], inc=vx)
                kb.op("vector", lambda E, sl=sl, b=bk, p=p: E.tensor_scalar(
                    KB_[0:64, sl], C.ps[64:128, 512 * b:512 * b + 512], 1.0,
                    BK2[64:128, p:p + 1], ALU.mult, ALU.add), inc=vx)
                C.setfree(("bk", bk), vx, kb.now(vx))
            else:
                for (src_b, dst, bcol, wcol, mmv) in (
                        (bq, QA, "bq", QNW[:, 1:2], mq), (bk, KA, "bk", KNW[:, 0:1], mk)):
                    dstB = QB if dst is QA else KB_
                    bc = BQ2 if bcol == "bq" else BK2
                    kb.op("vector", lambda E, b=src_b, bc=bc, p=p: E.tensor_scalar(
                        QS_[:, 0:512], C.ps[0:128, 512 * b:512 * b + 512], 1.0,
                        bc[:, p:p + 1], ALU.mult, ALU.add),
                        waits=[(mm, mmv), (vx, bqv), C.fwait("qs")], inc=vx)
                    qrv = kb.now(vx)
                    kb.op("scalar", lambda E: E.activation(QSQ[:, :], QS_[:, 0:512],
                                                           AF.Square),
                          waits=[(vx, qrv), C.fwait("qsq")], inc=ev)
                    mm_acc(C, C.ps[0:2, 512 * 6:512 * 6 + 512],
                           [([(ev, kb.now(ev)), (orl, rldv)], SEG[:, :],
                             QSQ[:, :])], inc=mm, waits=[C.fwait(("bk", 6))])
                    C.setfree("qsq", mm, kb.now(mm))
                    kb.op("vector", lambda E: E.tensor_scalar(RSTD[0:2, 0:512],
                                                              C.ps[0:2, 512 * 6:512 * 6 + 512],
                                                              1.0 / 64, None, ALU.mult),
                          waits=[(mm, kb.now(mm))], inc=vx)
                    C.setfree(("bk", 6), vx, kb.now(vx))
                    kb.op("scalar", lambda E, epsap=epsap: E.activation(
                        RSTD[0:2, 0:512], RSTD[0:2, 0:512], AF.Sqrt, bias=epsap[0:2, :]),
                        waits=[(vx, kb.now(vx)), epsw], inc=ev)
                    kb.wait("vector", [(ev, kb.now(ev))])
                    kb.op("vector", lambda E: E.reciprocal(RSTD[0:2, 0:512],
                                                           RSTD[0:2, 0:512]),
                          inc=vx)
                    mm_acc(C, C.ps[0:128, 512 * 7:512 * 7 + 512],
                           [([(vx, kb.now(vx))], C.sb["SEGTF"][0:2, :], RSTD[0:2, 0:512])],
                           inc=mm, waits=[C.fwait(("bk", 7))])
                    bcv = kb.now(mm)
                    kb.op("vector", lambda E, sl=sl, wcol=wcol: E.scalar_tensor_tensor(
                        QS_[:, 0:512], QS_[:, 0:512], wcol, C.ps[0:128, 512 * 7:512 * 7 + 512],
                        ALU.mult, ALU.mult),
                        waits=[(mm, bcv), (vx, rmv)], inc=vx)
                    kb.op("vector", lambda E, sl=sl, dst=dst: E.tensor_copy(dst[0:64, sl],
                                                                            QS_[0:64, 0:512]),
                          inc=vx)
                    kb.op("vector", lambda E, sl=sl, dstB=dstB: E.tensor_copy(dstB[0:64, sl],
                                                                              QS_[64:128, 0:512]),
                          inc=vx)
                    C.setfree(("bk", 7), vx, kb.now(vx))
                    C.setfree(("bk", src_b), vx, kb.now(vx))
                    C.setfree("qs", vx, kb.now(vx))
        for i in range(NTT):
            b = 4 + i % 2
            mm_acc(C, C.ps[0:128, 512 * b:512 * b + 130],
                   [((), XN[c][:, 128 * i:128 * (i + 1)], VW[:, 130 * c:130 * (c + 1)])
                    for c in range(NDT)], inc=mm, waits=[C.fwait(("bk", b))])
            mv = kb.now(mm)
            if biases is not None:
                kb.op("vector", lambda E, i=i, b=b, p=p: E.tensor_tensor(
                    V_[0][:, 65 * i:65 * i + 65], C.ps[0:128, 512 * b:512 * b + 65],
                    VBS[:, 130 * (p % 2):130 * (p % 2) + 65], ALU.add),
                    waits=[(mm, mv), (wld, wldv)], inc=vx)
                kb.op("vector", lambda E, i=i, b=b, p=p: E.tensor_tensor(
                    V_[1][:, 65 * i:65 * i + 65], C.ps[0:128, 512 * b + 65:512 * b + 130],
                    VBS[:, 130 * (p % 2) + 65:130 * (p % 2) + 130], ALU.add), inc=vx)
                C.setfree(("bk", b), vx, kb.now(vx))
            else:
                kb.op("scalar", lambda E, i=i, b=b: E.copy(V_[0][:, 65 * i:65 * i + 65],
                                                           C.ps[0:128, 512 * b:512 * b + 65]),
                      waits=[(mm, mv)], inc=ev)
                kb.op("scalar", lambda E, i=i, b=b: E.copy(V_[1][:, 65 * i:65 * i + 65],
                                                           C.ps[0:128, 512 * b + 65:512 * b + 130]),
                      inc=ev)
                C.setfree(("bk", b), ev, kb.now(ev))
        if biases is not None:
            C.setfree(("vbs", p % 2), mm, kb.now(mm))
        else:
            for hh in range(2):
                kb.op("scalar", lambda E, hh=hh: E.activation(
                    V_[hh][:, 64::65], V_[hh][:, 64::65], AF.Copy, bias=1.0),
                    waits=[(ev, kb.now(ev)), (vx, kb.now(vx))], inc=ev)
        projv_e, projv_v = kb.now(ev), kb.now(vx)

        for hh in range(2):
            Q = (QA, QB)[hh]
            K = (KA, KB_)[hh]
            V = V_[hh]
            for q in range(NTT):
                W = 128 * (q + 1) if causal else T
                nch = (W + 511) // 512
                fw = [(ev, projv_e), (vx, projv_v), C.fwait("p1banks")]
                for chk in range(nch):
                    cw = min(512, W - 512 * chk)
                    mm_acc(C, C.ps[0:128, 512 * chk:512 * chk + cw],
                           [((), Q[0:64, 128 * q:128 * (q + 1)],
                             K[0:64, 512 * chk:512 * chk + cw])],
                           inc=mm, waits=fw if chk == 0 else ())
                ms = kb.now(mm)
                rw = [(mm, ms)]
                if causal:
                    kb.op("vector", lambda E, W=W: E.tensor_tensor(
                        C.ps[0:128, W - 128:W], C.ps[0:128, W - 128:W], mask[:, :], ALU.add),
                        waits=[(mm, ms)], inc=vx)
                    rw.append((vx, kb.now(vx)))
                kb.op("vector", lambda E, W=W, hh=hh: E.reduce_max(
                    MBc[:, hh:hh + 1], C.ps[0:128, 0:W], axis=AX), waits=rw, inc=vx)
                mm_acc(C, C.ps[0:1, 0:128],
                       [([(vx, kb.now(vx))], MBc[:, hh:hh + 1], negI[:, :])], inc=mm)
                kb.op("scalar", lambda E, q=q, Q=Q: E.copy(Q[64:65, 128 * q:128 * (q + 1)],
                                                           C.ps[0:1, 0:128]),
                      waits=[(mm, kb.now(mm))], inc=ev)
                C.setfree("p1banks", ev, kb.now(ev))
            qaugv = kb.now(ev)
            p1vx = kb.now(vx)

            ptfree = {}
            for g in range(T // 512):
                jmax = (4 * g + 3) if causal else (NTT - 1)
                for j in range(jmax + 1):
                    qlo = max(512 * g, 128 * j) if causal else 512 * g
                    qhi = 512 * (g + 1)
                    N = qhi - qlo
                    sb_ = 4 + j % 2
                    mm_acc(C, C.ps[0:128, 512 * sb_:512 * sb_ + N],
                           [((), K[0:65, 128 * j:128 * (j + 1)], Q[0:65, qlo:qhi])],
                           inc=mm,
                           waits=[(ev, qaugv), (vx, p1vx), C.fwait(("st", sb_)),
                                  C.fwait("ob")])
                    mst = kb.now(mm)
                    exp_w = [(mm, mst), ptfree.get(j % 2)]
                    if causal and qlo == 128 * j:
                        kb.op("vector", lambda E, sb_=sb_: E.tensor_tensor(
                            C.ps[0:128, 512 * sb_:512 * sb_ + 128],
                            C.ps[0:128, 512 * sb_:512 * sb_ + 128], maskT[:, :], ALU.add),
                            waits=[(mm, mst)], inc=vx)
                        exp_w.append((vx, kb.now(vx)))
                    pts = 512 * (j % 2)
                    kb.op("scalar", lambda E, sb_=sb_, pts=pts, N=N: E.activation(
                        PT[:, pts:pts + N], C.ps[0:128, 512 * sb_:512 * sb_ + N], AF.Exp),
                        waits=exp_w, inc=pxp)
                    C.setfree(("st", sb_), pxp, kb.now(pxp))
                    co = qlo - 512 * g
                    mm_acc(C, C.ps[0:65, 512 * 7 + co:512 * 7 + co + N],
                           [([(pxp, kb.now(pxp))], V[:, 65 * j:65 * j + 65], PT[:, pts:pts + N])],
                           inc=mm, start=(j == 0), stop=(j == jmax))
                    ptfree[j % 2] = (mm, kb.now(mm))
                pvd = kb.now(mm)
                kb.op("vector", lambda E: E.reciprocal(RL[0:1, :], C.ps[64:65, 512 * 7:512 * 8]),
                      waits=[(mm, pvd), C.fwait("rl")], inc=vx)
                mm_acc(C, C.ps[0:64, 512 * 6:512 * 6 + 512],
                       [([(vx, kb.now(vx))], ones64[0:1, :], RL[0:1, :])], inc=mm,
                       waits=[C.fwait("b6")])
                kb.op("scalar", lambda E: E.copy(RB[:, :], C.ps[0:64, 512 * 6:512 * 6 + 512]),
                      waits=[(mm, kb.now(mm)), C.fwait("rb")], inc=ev)
                C.setfree("b6", ev, kb.now(ev))
                C.setfree("rl", ev, kb.now(ev))
                po = 64 * hh
                kb.op("vector", lambda E, g=g, po=po, p=p: E.tensor_tensor(
                    OF[p][po:po + 64, 512 * g:512 * (g + 1)],
                    C.ps[0:64, 512 * 7:512 * 8], RB[:, :], ALU.mult),
                    waits=[(ev, kb.now(ev))], inc=ofs)
                C.setfree("ob", ofs, kb.now(ofs))
                C.setfree("rb", ofs, kb.now(ofs))
        hw_ = pair_hook(p) if pair_hook is not None else []
        kb.op("sync", lambda E: E.nop(),
              waits=[(ofs, kb.now(ofs)), (ev, kb.now(ev)), (vx, kb.now(vx))] + list(hw_),
              inc=pd)
    return ofs, kb.now(ofs)


def alloc_mha_bias(C):
    C.alloc("BQ2", [128, 8], F32)
    C.alloc("BK2", [128, 8], F32)
    C.alloc("VBS", [128, 260], F32)


def alloc_mha_rms(C):
    C.alloc("QNW", [128, 2], F32)
    C.alloc("KNW", [128, 1], F32)
    C.alloc("SEG", [128, 2], BF16)
    C.alloc("SEGT", [2, 128], BF16)
    C.alloc("QS_", [128, 512], F32)
    C.alloc("QSQ", [128, 512], BF16)
    C.alloc("RSTD", [2, 512], F32)
    C.alloc("SEGTF", [2, 128], F32)


def emit_out_proj(C, tag, WT_dram, of_ready, sink, nin=1024, nout=1024, ofname="OF",
                  bias=None):
    kb = C.kb
    OF = [C.sb[f"{ofname}{c}"] for c in range(nin // 128)]
    WOSTA, WOSTB = C.sb["WOSTA"], C.sb["WOSTB"]
    DELTA = C.sb["DELTA"]
    wld, mm, ev, dc = f"{tag}wl", f"{tag}mm", f"{tag}ev", f"{tag}dc"
    dcb = C.kb.now(dc)
    nko = nin // 128
    nno = nout // 512

    def wost(c, n):
        flat = 1024 * c + 512 * n
        t = WOSTA if flat < 4096 else WOSTB
        o = flat % 4096
        return t[:, o:o + 512]
    if bias is not None:
        OPB = C.sb["OPB"]
        opl = f"{tag}opb"
        kb.dma("gpsimd", OPB[:, 0:nout], bias[:, 0:nout], opl, [of_ready])
    for c in range(nko):
        for n in range(nno):
            kb.dma("sync", wost(c, n), WT_dram[128 * c:128 * (c + 1), 512 * n:512 * n + 512],
                   wld, [of_ready])
    wldv = kb.now(wld)
    for i in range(NTT):
        for n in range(nno):
            b = (2 * i + n) % 4
            mm_acc(C, C.bank(b),
                   [((), OF[c][:, 128 * i:128 * (i + 1)], wost(c, n))
                    for c in range(nko)],
                   inc=mm, waits=[(wld, wldv), of_ready, C.fwait(("opk", b))])
            mv = kb.now(mm)
            if bias is None:
                kb.op("vector", lambda E, b=b: E.tensor_copy(
                    DELTA[:, 0:512], C.bank(b)),
                    waits=[(mm, mv), (dc, dcb + 16 * (nno * i + n))], inc=ev)
            else:
                kb.op("vector", lambda E, b=b, o=512 * n: E.tensor_tensor(
                    DELTA[:, 0:512], C.bank(b), OPB[:, o:o + 512], ALU.add),
                    waits=[(mm, mv), (dc, dcb + 16 * (nno * i + n)), (wld, wldv),
                           (f"{tag}opb", 16)], inc=ev)
            C.setfree(("opk", b), ev, kb.now(ev))
            sink(i, n, DELTA[:, 0:512], [(ev, kb.now(ev))])
    return dc, kb.now(dc)



def _env_open(env):
    if env is None:
        return (bass.Bass("TRN2", target_bir_lowering=False), KB(), ExitStack(), "", True)
    return env["nc"], env["kb"], env["stack"], env["pref"], False


def build_l2_attn_standalone(env=None):
    nc, kb, stack, pref, own = _env_open(env)
    if env is None:
        xn_hbm = nc.dram_tensor("xn", [T, D], BF16, kind="ExternalInput").ap()
        out = nc.dram_tensor("out", [T, D], F32, kind="ExternalOutput").ap()
    else:
        xn_hbm, out = env["xn"], env["out"]
    WqT = nc.dram_tensor("l2_wqt", [D, D], BF16, kind="ExternalInput").ap()
    WkT = nc.dram_tensor("l2_wkt", [D, D], BF16, kind="ExternalInput").ap()
    WvTa = nc.dram_tensor("l2_wvta", [D, 16 * 65], BF16, kind="ExternalInput").ap()
    WoT = nc.dram_tensor("l2_wot", [D, D], BF16, kind="ExternalInput").ap()

    P = host_consts_common()
    C = Ctx(nc, kb, stack, pref)
    for c in range(NDT):
        C.alloc(f"XN{c}", [128, T], BF16)
    alloc_mha(C)
    if T >= 4096:
        C.sb["WOSTA"] = C.sb["QA"]
        C.sb["WOSTB"] = C.sb["QB"]
        C.sb["DELTA"] = C.sb["KA"].bitcast(F32)
    else:
        C.sb["WOSTA"] = C.alloc("WOSTAd", [128, 4096], BF16)
        C.sb["WOSTB"] = C.alloc("WOSTBd", [128, 4096], BF16)
        C.sb["DELTA"] = C.alloc("DELTAd", [128, 512], F32)
    C.alloc_ps()

    cs = emit_mha_consts(C, P)
    xr = emit_xn_from_hbm(C, xn_hbm, "x0")
    ofs = emit_mha(C, "a", WqT, WkT, WvTa, xr, cs, causal=True, q_scale=0.125)

    def sink(i, n, ap, waits):
        kb.dma("gpsimd", out[128 * i:128 * (i + 1), 512 * n:512 * (n + 1)], ap, "odc", waits)

    emit_out_proj(C, "o", WoT, ofs, sink)
    kb.wait("gpsimd", [("odc", kb.now("odc"))])
    if own:
        kb.finalize(nc, stack)
    return nc, C, stack


def prep_l2_attn_inputs(I):
    return {
        "l2_wqt": as_bf16(I["L2_attn_Wq"].T),
        "l2_wkt": as_bf16(I["L2_attn_Wk"].T),
        "l2_wvta": as_bf16(aug_v_weight(I["L2_attn_Wv"].T.astype(np.float32), 16)),
        "l2_wot": as_bf16(I["L2_attn_Wo"].T),
    }


def build_l0_attn_standalone(env=None):
    nc, kb, stack, pref, own = _env_open(env)
    if env is None:
        xn_hbm = nc.dram_tensor("xn", [T, D], BF16, kind="ExternalInput").ap()
        out = nc.dram_tensor("out", [T, D], F32, kind="ExternalOutput").ap()
    else:
        xn_hbm, out = env["xn"], env["out"]
    WqT = nc.dram_tensor("wqt", [D, D], BF16, kind="ExternalInput").ap()
    WkT = nc.dram_tensor("wkt", [D, D], BF16, kind="ExternalInput").ap()
    WvTa = nc.dram_tensor("wvta", [D, 16 * 65], BF16, kind="ExternalInput").ap()
    WoT = nc.dram_tensor("wot", [D, D], BF16, kind="ExternalInput").ap()
    BQd = nc.dram_tensor("bqd", [128, 8], F32, kind="ExternalInput").ap()
    BKd = nc.dram_tensor("bkd", [128, 8], F32, kind="ExternalInput").ap()
    BVd = nc.dram_tensor("bvd", [8, 128, 130], F32, kind="ExternalInput").ap()
    QNd = nc.dram_tensor("qnd", [128, 1], F32, kind="ExternalInput").ap()
    KNd = nc.dram_tensor("knd", [128, 1], F32, kind="ExternalInput").ap()
    BPd = nc.dram_tensor("bpd", [128, 1024], F32, kind="ExternalInput").ap()

    P = host_consts_common()
    C = Ctx(nc, kb, stack, pref)
    for c in range(NDT):
        C.alloc(f"XN{c}", [128, T], BF16)
    alloc_mha(C)
    alloc_mha_bias(C)
    alloc_mha_rms(C)
    if T >= 4096:
        C.sb["WOSTA"] = C.sb["QA"]
        C.sb["WOSTB"] = C.sb["QB"]
        C.sb["DELTA"] = C.sb["KA"].bitcast(F32)
        C.sb["OPB"] = C.sb["KB_"]
    else:
        C.sb["WOSTA"] = C.alloc("WOSTAd", [128, 4096], BF16)
        C.sb["WOSTB"] = C.alloc("WOSTBd", [128, 4096], BF16)
        C.sb["DELTA"] = C.alloc("DELTAd", [128, 512], F32)
        C.sb["OPB"] = C.alloc("OPBd", [128, 1024], BF16)
    C.alloc_ps()

    cs = emit_mha_consts(C, P)
    xr = emit_xn_from_hbm(C, xn_hbm, "x0")
    ofs = emit_mha(C, "a", WqT, WkT, WvTa, xr, cs, causal=False, q_scale=0.125,
                   biases=dict(bq=BQd, bk=BKd, bv=BVd), rms=(QNd, KNd, 1e-6))

    def sink(i, n, ap, waits):
        kb.dma("gpsimd", out[128 * i:128 * (i + 1), 512 * n:512 * (n + 1)], ap, "odc", waits)

    emit_out_proj(C, "o", WoT, ofs, sink, bias=BPd)
    kb.wait("gpsimd", [("odc", kb.now("odc"))])
    if own:
        kb.finalize(nc, stack)
    return nc, C, stack


def prep_l0_attn_inputs(I):
    WT = I["L0_attn_Wqkv"].T.astype(np.float32)
    b = I["L0_attn_bqkv"].astype(np.float32)
    bv = b[2048:3072]
    bvs = np.zeros((8, 128, 130), np.float32)
    for p_ in range(8):
        for hh in range(2):
            h = 2 * p_ + hh
            bvs[p_, :, 65 * hh:65 * hh + 64] = bv[64 * h:64 * h + 64][None, :]
            bvs[p_, :, 65 * hh + 64] = 1.0
    return {
        "wqt": as_bf16(WT[:, 0:1024]),
        "wkt": as_bf16(WT[:, 1024:2048]),
        "wvta": as_bf16(aug_v_weight(WT[:, 2048:3072], 16)),
        "wot": as_bf16(I["L0_attn_Wproj"].T),
        "bqd": as_f32(b[0:1024].reshape(8, 128).T),
        "bkd": as_f32(b[1024:2048].reshape(8, 128).T),
        "bvd": as_f32(bvs),
        "qnd": as_f32(np.concatenate([I["L0_attn_qn_w"]] * 2)[:, None]),
        "knd": as_f32(np.concatenate([I["L0_attn_kn_w"]] * 2)[:, None]),
        "bpd": as_f32(np.tile(I["L0_attn_bproj"][None, :], (128, 1))),
    }


def build_ln_standalone(nadd, env=None):
    nc, kb, stack, pref, own = _env_open(env)
    if env is None:
        xa = nc.dram_tensor("xa", [T, D], F32, kind="ExternalInput").ap()
        if nadd:
            ab = nc.dram_tensor("ab", [T, D], F32, kind="ExternalInput").ap()
            fb = nc.dram_tensor("fb", [T, D], F32, kind="ExternalInput").ap()
            xout = nc.dram_tensor("xout", [T, D], F32, kind="ExternalOutput").ap()
        xn_hbm = nc.dram_tensor("xn", [T, D], BF16, kind="ExternalOutput").ap()
    else:
        xa, xn_hbm = env["xa"], env["xn"]
        if nadd:
            ab, fb, xout = env["ab"], env["fb"], env["xout"]
    C = Ctx(nc, kb, stack, pref)
    XT = C.alloc("LNX", [128, 2048], F32)
    if nadd:
        AT = C.alloc("LNA", [128, 2048], F32)
        FT = C.alloc("LNF", [128, 2048], F32)
    C.alloc("LNSQ", [128, 1024], F32)
    ST = C.alloc("LNST", [128, 8], F32)
    XNT = C.alloc("XNTM", [128, 2048], BF16)
    ld, vs, ad, sc, st, so = "ld", "vs", "ad", "sc", "st", "so"
    scb, stb, sob = kb.now(sc), kb.now(st), kb.now(so)
    for i in range(NTT):
        b = i % 2
        sl = slice(1024 * b, 1024 * (b + 1))
        w = []
        if i >= 2:
            w.append((sc, scb + i - 1))
            if nadd:
                w.append((so, sob + 16 * (i - 1)))
        kb.dma("sync", XT[:, sl], xa[128 * i:128 * (i + 1), :], ld, w)
        if nadd:
            kb.dma("sync", AT[:, sl], ab[128 * i:128 * (i + 1), :], ld)
            kb.dma("sync", FT[:, sl], fb[128 * i:128 * (i + 1), :], ld)
        lv = kb.now(ld)
        if nadd:
            kb.op("vector", lambda E, sl=sl: E.tensor_tensor(
                XT[:, sl], XT[:, sl], AT[:, sl], ALU.add), waits=[(ld, lv)], inc=ad)
            kb.op("vector", lambda E, sl=sl: E.tensor_tensor(
                XT[:, sl], XT[:, sl], FT[:, sl], ALU.add), inc=ad)
            adv = kb.now(ad)
            kb.dma("gpsimd", xout[128 * i:128 * (i + 1), :], XT[:, sl], so, [(ad, adv)])
            stats_w = [(ad, adv)]
        else:
            stats_w = [(ld, lv)]
        ret = ln_stats(C, XT[:, sl], ST, 1e-5, vs, waits=stats_w)
        w2 = [(vs, ret)]
        if i >= 2:
            w2.append((st, stb + 16 * (i - 1)))
        kb.op("vector", lambda E, sl=sl: E.tensor_scalar(
            XNT[:, sl], XT[:, sl], ST[:, 6:7], ST[:, 7:8], ALU.mult, ALU.add),
            waits=w2, inc=sc)
        kb.dma("gpsimd", xn_hbm[128 * i:128 * (i + 1), :], XNT[:, sl], st,
               [(sc, kb.now(sc))])
    fw = [(st, kb.now(st))]
    if nadd:
        fw.append((so, kb.now(so)))
    kb.wait("gpsimd", fw)
    if own:
        kb.finalize(nc, stack)
    return nc, C, stack


_PROGS = {}


def _get_prog(key, builder):
    if key not in _PROGS:
        _PROGS[key] = builder()
    return _PROGS[key]


def build_fused():
    nc = bass.Bass("TRN2", target_bir_lowering=False)
    kb = KB()
    gstack = ExitStack()
    xin = nc.dram_tensor("xa0", [T, D], F32, kind="ExternalInput").ap()
    xoutd = nc.dram_tensor("xout", [T, D], F32, kind="ExternalOutput").ap()
    xs = [xin] + [nc.dram_tensor(f"x{i}_s", [T, D], F32, kind="Internal").ap()
                  for i in (1, 2, 3)] + [xoutd]
    xns = [nc.dram_tensor(f"xn{i}_s", [T, D], BF16, kind="Internal").ap() for i in range(5)]
    a_s = nc.dram_tensor("a_s", [T, D], F32, kind="Internal").ap()
    f_s = nc.dram_tensor("f_s", [T, D], F32, kind="Internal").ap()
    Cs = []
    state = {"n": 0, "snap": {}}

    def run_phase(pref, fn):
        pstack = ExitStack()
        env = {"nc": nc, "kb": kb, "stack": pstack, "pref": pref}
        _, C, _ = fn(env)
        pstack.close()
        state["n"] += 1
        changed = [("!" + sem, v) for sem, v in kb.cnt.items()
                   if v > state["snap"].get(sem, 0) and sem != "phb"]
        kb.op("sync", lambda E: E.nop(), waits=changed, inc="!phb")
        state["snap"] = dict(kb.cnt)
        for e in KB.ENGS:
            kb.wait(e, [("!phb", state["n"])])
        Cs.append(C)

    attn_b = {0: build_l0_attn_standalone, 1: build_l1_attn_standalone,
              2: build_l2_attn_standalone, 3: build_l3_attn_standalone}
    ffn_b = {2: build_l2_ffn_standalone, 3: build_l3_ffn_standalone}
    for blk in range(4):
        if blk == 0:
            run_phase("q0.", lambda env: build_ln_standalone(
                0, env=dict(env, xa=xs[0], xn=xns[0])))
        else:
            run_phase(f"q{blk}.", lambda env, blk=blk: build_ln_standalone(
                2, env=dict(env, xa=xs[blk - 1], ab=a_s, fb=f_s, xout=xs[blk],
                            xn=xns[blk])))
        run_phase(f"a{blk}.", lambda env, blk=blk: attn_b[blk](
            env=dict(env, xn=xns[blk], out=a_s)))
        if blk in ffn_b:
            run_phase(f"f{blk}.", lambda env, blk=blk: ffn_b[blk](
                env=dict(env, xn=xns[blk], out=f_s)))
        else:
            run_phase(f"f{blk}.", lambda env, blk=blk: build_ffn_standalone(
                env=dict(env, xn=xns[blk], out=f_s), pr=f"f{blk}_"))
    run_phase("q4.", lambda env: build_ln_standalone(
        2, env=dict(env, xa=xs[3], ab=a_s, fb=f_s, xout=xs[4], xn=xns[4])))
    kb.finalize(nc, gstack)
    return nc, Cs, gstack


def _fused_weights(I):
    W = {}
    W.update(prep_l0_attn_inputs(I))
    W.update(prep_l1_attn_inputs(I))
    W.update(prep_l2_attn_inputs(I))
    W.update(prep_l3_attn_inputs(I))
    W.update(prep_l2_ffn_inputs(I))
    W.update(prep_l3_ffn_inputs(I))
    W.update(prep_ffn_inputs(I, "L0_ffn_", "f0_"))
    W.update(prep_ffn_inputs(I, "L1_ffn_", "f1_"))
    return W


def kernel(**inputs):
    I = inputs
    B = 8
    nc, Cs, _ = _get_prog("fused", build_fused)
    W = _fused_weights(I)
    for C in Cs:
        W.update(C.const_maps())
    in_maps = [dict(W, xa0=np.ascontiguousarray(I["L0_attn_x"][b].astype(np.float32)))
               for b in range(B)]
    res = run_bass_kernel_spmd(nc, in_maps, core_ids=list(range(B)))
    return np.stack([np.asarray(res.results[b]["xout"], np.float32) for b in range(B)])


def kernel_unfused(**inputs):
    I = inputs
    B = 8
    cores = list(range(B))

    def launch(progkey, builder, W, percore, outs):
        nc, C, stack = _get_prog(progkey, builder)
        base = dict(W)
        base.update(C.const_maps())
        in_maps = []
        for b in range(B):
            m = dict(base)
            for k, v in percore.items():
                m[k] = v[b]
            in_maps.append(m)
        res = run_bass_kernel_spmd(nc, in_maps, core_ids=cores)
        return tuple([res.results[b][o] for b in range(B)] for o in outs)

    x = [np.ascontiguousarray(I["L0_attn_x"][b].astype(np.float32)) for b in range(B)]

    ffnW = {0: prep_ffn_inputs(I, "L0_ffn_", "f0_"), 1: prep_ffn_inputs(I, "L1_ffn_", "f0_")}
    attn_launch = {
        0: ("l0a", build_l0_attn_standalone, prep_l0_attn_inputs),
        1: ("l1a", build_l1_attn_standalone, prep_l1_attn_inputs),
        2: ("l2a", build_l2_attn_standalone, prep_l2_attn_inputs),
        3: ("l3a", build_l3_attn_standalone, prep_l3_attn_inputs),
    }
    ffn_launch = {
        2: ("l2f", build_l2_ffn_standalone, prep_l2_ffn_inputs),
        3: ("l3f", build_l3_ffn_standalone, prep_l3_ffn_inputs),
    }

    for blk in range(4):
        if blk == 0:
            (xn,) = launch("ln0", lambda: build_ln_standalone(0), {},
                           {"xa": x}, ["xn"])
        else:
            xn, x = launch("ln2", lambda: build_ln_standalone(2), {},
                           {"xa": x, "ab": a, "fb": f}, ["xn", "xout"])
        xn = [np.ascontiguousarray(t) for t in xn]
        akey, abuild, aprep = attn_launch[blk]
        (a,) = launch(akey, abuild, aprep(I), {"xn": xn}, ["out"])
        if blk in ffn_launch:
            fkey, fbuild, fprep = ffn_launch[blk]
            (f,) = launch(fkey, fbuild, fprep(I), {"xn": xn}, ["out"])
        else:
            (f,) = launch("ffn", build_ffn_standalone, ffnW[blk], {"xn": xn}, ["out"])
    _, x = launch("ln2", lambda: build_ln_standalone(2), {},
                  {"xa": x, "ab": a, "fb": f}, ["xn", "xout"])
    return np.stack([np.asarray(t, dtype=np.float32) for t in x])


def alloc_moe_l2(C):
    C.alloc("RWS", [128, 64], BF16)
    C.alloc("EBt", [128, 8], F32)
    C.alloc("LGt", [128, 16], F32)
    C.alloc("SPt", [128, 256], F32)
    C.alloc("ACTt", [128, 256], F32)
    C.alloc("CW", [128, 288], F32)
    C.alloc("CWT", [9, 4096], BF16)
    C.alloc("TKS", [128, 8], F32)
    C.alloc("W1E", [128, 8192], BF16)
    C.alloc("W2E", [128, 8192], BF16)
    C.alloc("B1E", [128, 8], F32)
    C.alloc("B2S", [9, 1024], BF16)
    C.alloc("H1m", [128, 2048], BF16)
    C.alloc("ACCm", [128, 4096], F32)
    C.alloc("IF32", [128, 128], F32)
    C.alloc("ONES32", [128, 32], F32)


def emit_moe_l2(C, tag, RWT, EB, W1s, B1s, W2s, B2s, xn_ready, sink):
    kb = C.kb
    XN = [C.sb[f"XN{c}"] for c in range(NDT)]
    RWS, EBt, LGt = C.sb["RWS"], C.sb["EBt"], C.sb["LGt"]
    SPt, ACTt, CW, CWT, TKS = C.sb["SPt"], C.sb["ACTt"], C.sb["CW"], C.sb["CWT"], C.sb["TKS"]
    W1E, W2E, B1E, B2S = C.sb["W1E"], C.sb["W2E"], C.sb["B1E"], C.sb["B2S"]
    H1, ACC = C.sb["H1m"], C.sb["ACCm"]
    IF32, ONES32 = C.sb["IF32"], C.sb["ONES32"]
    ld, mm, ev, vx, dc = f"{tag}ld", f"{tag}mm", f"{tag}ev", f"{tag}vx", f"{tag}dc"
    dcb = C.kb.now(dc)
    wl = f"{tag}wl"

    kb.dma("sync", IF32[:, :], C.const("c_IF32", as_f32(np.eye(128))), ld)
    kb.dma("sync", ONES32[:, :], C.const("c_ones32", as_f32(np.ones((128, 32)))), ld)
    for c in range(NDT):
        kb.dma("sync", RWS[:, 8 * c:8 * c + 8], RWT[128 * c:128 * (c + 1), :], ld)
    kb.dma("sync", EBt[:, :], EB[:, :], ld)
    ldv = kb.now(ld)

    for i in range(NTT):
        b = i % 2
        mm_acc(C, C.ps[0:128, 512 * b:512 * b + 8],
               [((), XN[c][:, 128 * i:128 * (i + 1)], RWS[:, 8 * c:8 * c + 8])
                for c in range(NDT)],
               inc=mm, waits=[(ld, ldv), xn_ready, C.fwait(("bk", b))])
        mv = kb.now(mm)
        sl = slice(8 * (i % 2), 8 * (i % 2) + 8)
        kb.op("vector", lambda E, b=b, sl=sl: E.tensor_tensor(
            LGt[:, sl], C.ps[0:128, 512 * b:512 * b + 8], EBt[:, :], ALU.add),
            waits=[(mv and (mm, mv)), C.fwait(("lg", i % 2))], inc=vx)
        kb.op("scalar", lambda E, i=i, sl=sl: E.activation(SPt[:, 8 * i:8 * i + 8], LGt[:, sl], AF.Exp),
              waits=[(vx, kb.now(vx))], inc=ev)
        kb.op("scalar", lambda E, i=i: E.activation(SPt[:, 8 * i:8 * i + 8], SPt[:, 8 * i:8 * i + 8],
                                                    AF.Ln, bias=1.0), inc=ev)
        C.setfree(("bk", b), ev, kb.now(ev))
        C.setfree(("lg", i % 2), ev, kb.now(ev))
    kb.op("scalar", lambda E: E.activation(ACTt[:, 0:8 * NTT], SPt[:, 0:8 * NTT], AF.Sqrt),
          waits=[(ev, kb.now(ev))], inc=ev)
    sq = kb.now(ev)
    for i in range(NTT):
        a = ACTt[:, 8 * i:8 * i + 8]
        kb.op("vector", lambda E, a=a: E.reduce_max(TKS[:, 0:1], a, axis=AX),
              waits=[(ev, sq)], inc=vx)
        kb.wait("vector", [(vx, kb.now(vx))])
        kb.op("vector", lambda E, a=a, i=i: E.tensor_scalar(CW[:, 9 * i:9 * i + 8], a, TKS[:, 0:1],
                                                            None, ALU.is_ge), inc=vx)
        kb.wait("vector", [(vx, kb.now(vx))])
        kb.op("vector", lambda E, a=a, i=i: E.scalar_tensor_tensor(
            SPt[:, 8 * i:8 * i + 8], CW[:, 9 * i:9 * i + 8], -1e30, a, ALU.mult, ALU.add), inc=vx)
        kb.wait("vector", [(vx, kb.now(vx))])
        kb.op("vector", lambda E, i=i: E.reduce_max(TKS[:, 1:2], SPt[:, 8 * i:8 * i + 8], axis=AX),
              inc=vx)
        kb.wait("vector", [(vx, kb.now(vx))])
        kb.op("vector", lambda E, a=a, i=i: E.tensor_scalar(CW[:, 9 * i:9 * i + 8], a, TKS[:, 1:2],
                                                            None, ALU.is_ge), inc=vx)
        kb.wait("vector", [(vx, kb.now(vx))])
        kb.op("vector", lambda E, a=a, i=i: E.tensor_tensor(CW[:, 9 * i:9 * i + 8],
                                                            CW[:, 9 * i:9 * i + 8], a, ALU.mult), inc=vx)
    kb.op("vector", lambda E: E.tensor_copy(CW[:, 8::9], ONES32[:, :]),
          waits=[(ld, ldv)], inc=vx)
    cwv = kb.now(vx)
    for i in range(NTT):
        kb.op("tensor", lambda E, i=i: E.transpose(C.ps[0:9, 3584:3712], CW[:, 9 * i:9 * i + 9],
                                                   IF32[:, :]),
              waits=[(vx, cwv), C.fwait(("bk", 7)), (ld, ldv)], inc=mm)
        kb.op("scalar", lambda E, i=i: E.copy(CWT[0:9, 128 * i:128 * (i + 1)], C.ps[0:9, 3584:3712]),
              waits=[(mm, kb.now(mm))], inc=ev)
        C.setfree(("bk", 7), ev, kb.now(ev))
    cwtv = kb.now(ev)

    for g in range(T // 512):
        gsl = slice(512 * g, 512 * (g + 1))
        for e in range(9):
            par = e % 2
            wo = 8192 // 2 * par
            wv = [C.fwait(("we", par)), (ld, ldv)]
            kb.dma("sync", W1E[:, wo:wo + 4096], W1s[e, :, :], wl, wv)
            kb.dma("sync", W2E[:, wo:wo + 4096], W2s[e, :, :], wl)
            kb.dma("sync", B1E[:, 4 * par:4 * par + 4], B1s[e, :, :], wl)
            if g == 0 and e == 0:
                kb.dma("sync", B2S[0:9, :], B2s[:, :], wl)
            wldv = kb.now(wl)
            for hch in range(4):
                b = hch % 2
                mm_acc(C, C.bank(b),
                       [((), W1E[:, wo + 512 * c + 128 * hch:wo + 512 * c + 128 * hch + 128],
                         XN[c][:, gsl]) for c in range(NDT)],
                       inc=mm, waits=[(wl, wldv), xn_ready, C.fwait(("bk", b))])
                mv = kb.now(mm)
                kb.op("scalar", lambda E, b=b, hch=hch, par=par: E.activation(
                    H1[:, 512 * hch:512 * (hch + 1)], C.bank(b), AF.Silu,
                    bias=B1E[:, 4 * par + hch:4 * par + hch + 1]),
                    waits=[(mm, mv), C.fwait(("h1s", hch))], inc=ev)
                C.setfree(("bk", b), ev, kb.now(ev))
            h1v = kb.now(ev)
            for ti in range(4):
                for n in range(2):
                    b = 4 + (2 * ti + n) % 2
                    mm_acc(C, C.bank(b),
                           [((), H1[:, 512 * hch + 128 * ti:512 * hch + 128 * ti + 128],
                             W2E[:, wo + 1024 * hch + 512 * n:wo + 1024 * hch + 512 * n + 512])
                            for hch in range(4)],
                           inc=mm, waits=[(ev, h1v), C.fwait(("bk", b))])
                    mv = kb.now(mm)
                    o = 1024 * ti + 512 * n
                    cwc = CW[:, 9 * (4 * g + ti) + e:9 * (4 * g + ti) + e + 1]
                    if e == 0:
                        kb.op("vector", lambda E, b=b, o=o, cwc=cwc: E.tensor_scalar(
                            ACC[:, o:o + 512], C.bank(b), cwc, None, ALU.mult),
                            waits=[(mm, mv), (vx, cwv), C.fwait(("acc", o))], inc=vx)
                    else:
                        kb.op("vector", lambda E, b=b, o=o, cwc=cwc: E.scalar_tensor_tensor(
                            ACC[:, o:o + 512], C.bank(b), cwc, ACC[:, o:o + 512], ALU.mult, ALU.add),
                            waits=[(mm, mv)], inc=vx)
                    C.setfree(("bk", b), vx, kb.now(vx))
            C.setfree(("we", par), mm, kb.now(mm))
            C.setfree(("h1s", 0), mm, kb.now(mm))
            C.setfree(("h1s", 1), mm, kb.now(mm))
            C.setfree(("h1s", 2), mm, kb.now(mm))
            C.setfree(("h1s", 3), mm, kb.now(mm))
        for ti in range(4):
            for n in range(2):
                b = 6
                mm_acc(C, C.bank(b),
                       [([(ev, cwtv)], CWT[0:9, 512 * g + 128 * ti:512 * g + 128 * (ti + 1)],
                         B2S[0:9, 512 * n:512 * (n + 1)])],
                       inc=mm, waits=[C.fwait(("bk", 6))])
                mv = kb.now(mm)
                o = 1024 * ti + 512 * n
                kb.op("vector", lambda E, o=o: E.tensor_tensor(
                    ACC[:, o:o + 512], ACC[:, o:o + 512], C.bank(6), ALU.add),
                    waits=[(mm, mv), (vx, kb.now(vx))], inc=vx)
                C.setfree(("bk", 6), vx, kb.now(vx))
        accv = kb.now(vx)
        for ti in range(4):
            i = 4 * g + ti
            sink(i, ACC[:, 1024 * ti:1024 * (ti + 1)], [(vx, accv)])
            C.setfree(("acc", 1024 * ti), dc, dcb + 16 * (i + 1))
            C.setfree(("acc", 1024 * ti + 512), dc, 16 * (i + 1))
    return dc, kb.now(dc)


def build_l2_ffn_standalone(env=None):
    nc, kb, stack, pref, own = _env_open(env)
    if env is None:
        xn_hbm = nc.dram_tensor("xn", [T, D], BF16, kind="ExternalInput").ap()
        out = nc.dram_tensor("out", [T, D], F32, kind="ExternalOutput").ap()
    else:
        xn_hbm, out = env["xn"], env["out"]
    RWT = nc.dram_tensor("l2_rwt", [D, 8], BF16, kind="ExternalInput").ap()
    EB = nc.dram_tensor("l2_eb", [128, 8], F32, kind="ExternalInput").ap()
    W1s = nc.dram_tensor("l2_w1s", [9, 128, 4096], BF16, kind="ExternalInput").ap()
    B1s = nc.dram_tensor("l2_b1s", [9, 128, 4], F32, kind="ExternalInput").ap()
    W2s = nc.dram_tensor("l2_w2s", [9, 128, 4096], BF16, kind="ExternalInput").ap()
    B2s = nc.dram_tensor("l2_b2s", [9, 1024], BF16, kind="ExternalInput").ap()
    C = Ctx(nc, kb, stack, pref)
    for c in range(NDT):
        C.alloc(f"XN{c}", [128, T], BF16)
    alloc_moe_l2(C)
    C.alloc_ps()
    xr = emit_xn_from_hbm(C, xn_hbm, "x0")

    def sink(i, ap, waits):
        kb.dma("gpsimd", out[128 * i:128 * (i + 1), :], ap, "mdc", waits)

    emit_moe_l2(C, "m", RWT, EB, W1s, B1s, W2s, B2s, xr, sink)
    kb.wait("gpsimd", [("mdc", kb.now("mdc"))])
    if own:
        kb.finalize(nc, stack)
    return nc, C, stack


def prep_l2_ffn_inputs(I):
    w1 = np.concatenate([I["L2_ffn_rW1"], I["L2_ffn_sW1"][None]], 0)
    w2 = np.concatenate([I["L2_ffn_rW2"], I["L2_ffn_sW2"][None]], 0)
    b1 = np.concatenate([I["L2_ffn_rb1"], I["L2_ffn_sb1"][None]], 0)
    b2 = np.concatenate([I["L2_ffn_rb2"], I["L2_ffn_sb2"][None]], 0)
    return {
        "l2_rwt": as_bf16(I["L2_ffn_routing_W"].T),
        "l2_eb": as_f32(np.tile(I["L2_ffn_expert_bias"][None, :], (128, 1))),
        "l2_w1s": as_bf16(w1.reshape(9, 8, 128, 512).transpose(0, 2, 1, 3).reshape(9, 128, 4096)),
        "l2_b1s": as_f32(b1.reshape(9, 4, 128).transpose(0, 2, 1)),
        "l2_w2s": as_bf16(w2.reshape(9, 4, 128, 1024).transpose(0, 2, 1, 3).reshape(9, 128, 4096)),
        "l2_b2s": as_bf16(b2),
    }


def ln_stats(C, x_ap, ST, eps, vs, waits=()):
    kb = C.kb
    Dn = 1024
    kb.op("vector", lambda E: E.reduce_sum(ST[:, 0:1], x_ap, axis=AX),
          waits=list(waits), inc=vs)
    kb.op("scalar", lambda E: E.activation(C.sb["LNSQ"][:, :], x_ap, AF.Square,
                                           accum_out=ST[:, 1:2]),
          waits=list(waits), inc=vs)
    kb.wait("vector", [(vs, kb.now(vs))])
    kb.op("vector", lambda E: E.tensor_scalar(ST[:, 2:3], ST[:, 0:1], 1.0 / Dn, None, ALU.mult))
    kb.op("vector", lambda E: E.tensor_scalar(ST[:, 3:4], ST[:, 1:2], 1.0 / Dn, None, ALU.mult))
    kb.op("vector", lambda E: E.tensor_tensor(ST[:, 4:5], ST[:, 2:3], ST[:, 2:3], ALU.mult), inc=vs)
    kb.op("vector", lambda E: E.tensor_tensor(ST[:, 3:4], ST[:, 3:4], ST[:, 4:5], ALU.subtract),
          waits=[(vs, kb.now(vs))], inc=vs)
    epsap, epsw = C.fconst(eps)
    kb.op("scalar", lambda E, epsap=epsap: E.activation(ST[:, 5:6], ST[:, 3:4], AF.Sqrt, bias=epsap),
          waits=[(vs, kb.now(vs)), epsw], inc=vs)
    kb.op("vector", lambda E: E.reciprocal(ST[:, 6:7], ST[:, 5:6]),
          waits=[(vs, kb.now(vs))], inc=vs)
    kb.op("vector", lambda E: E.scalar_tensor_tensor(ST[:, 7:8], ST[:, 2:3], -1.0, ST[:, 6:7],
                                                     ALU.mult, ALU.mult),
          waits=[(vs, kb.now(vs))], inc=vs)
    return kb.now(vs)


def alloc_moe_l3(C):
    C.alloc("RWS", [128, 64], BF16)
    C.alloc("RBt", [128, 8], F32)
    C.alloc("LGt", [128, 256], F32)
    C.alloc("SGt", [128, 256], F32)
    C.alloc("CW", [128, 256], F32)
    C.alloc("CWT", [8, 4096], BF16)
    C.alloc("TKS", [128, 8], F32)
    C.alloc("W1C", [128, 2048], BF16)
    C.alloc("B1L", [128, 64], F32)
    C.alloc("W2R", [128, 32768], BF16)
    C.alloc("H1C", [128, 1024], BF16)
    C.alloc("H1G", [128, 16384], BF16)
    C.alloc("YTl", [128, 1024], F32)
    C.alloc("B2S", [8, 1024], BF16)
    C.alloc("ACF", [128, 1024], F32)
    C.alloc("XNL", [128, 1024], BF16)
    C.alloc("OT", [128, 1024], F32)
    C.alloc("LNW", [128, 1024], F32)
    C.alloc("LNB", [128, 1024], F32)
    C.alloc("IF32", [128, 128], F32)
    C.alloc("ONES32", [128, 32], F32)
    C.alloc("LNSQ", [128, 1024], F32)
    C.alloc("LNST", [128, 8], F32)


def emit_moe_l3(C, tag, RWT, RBb, W1s, B1s, W2s, B2s, LNWd, LNBd,
                xn_hbm, h1_hbm, acc_hbm, xn_ready, sink):
    kb = C.kb
    XN = [C.sb[f"XN{c}"] for c in range(NDT)]
    RWS, RBt, LGt, SGt = C.sb["RWS"], C.sb["RBt"], C.sb["LGt"], C.sb["SGt"]
    CW, CWT, TKS = C.sb["CW"], C.sb["CWT"], C.sb["TKS"]
    W1C, B1L, W2R = C.sb["W1C"], C.sb["B1L"], C.sb["W2R"]
    H1C, H1G, YT = C.sb["H1C"], C.sb["H1G"], C.sb["YTl"]
    B2S, ACF, XNL, OT = C.sb["B2S"], C.sb["ACF"], C.sb["XNL"], C.sb["OT"]
    LNW, LNB, IF32 = C.sb["LNW"], C.sb["LNB"], C.sb["IF32"]
    ST = C.sb["LNST"]
    ld, mm, ev, vx, wl = f"{tag}ld", f"{tag}mm", f"{tag}ev", f"{tag}vx", f"{tag}wl"
    sp, ac, dc = f"{tag}sp", f"{tag}ac", f"{tag}dc"
    dcb = C.kb.now(dc)

    kb.dma("sync", IF32[:, :], C.const("c_IF32", as_f32(np.eye(128))), ld)
    for c in range(NDT):
        kb.dma("sync", RWS[:, 8 * c:8 * c + 8], RWT[128 * c:128 * (c + 1), :], ld)
    kb.dma("sync", RBt[:, :], RBb[:, :], ld)
    kb.dma("sync", B2S[0:8, :], B2s[:, :], ld)
    kb.dma("sync", LNW[:, :], LNWd[:, :], ld)
    kb.dma("sync", LNB[:, :], LNBd[:, :], ld)
    ldv = kb.now(ld)

    for i in range(NTT):
        b = i % 2
        mm_acc(C, C.ps[0:128, 512 * b:512 * b + 8],
               [((), XN[c][:, 128 * i:128 * (i + 1)], RWS[:, 8 * c:8 * c + 8])
                for c in range(NDT)],
               inc=mm, waits=[(ld, ldv), xn_ready, C.fwait(("bk", b))])
        mv = kb.now(mm)
        kb.op("vector", lambda E, b=b, i=i: E.tensor_tensor(
            LGt[:, 8 * i:8 * i + 8], C.ps[0:128, 512 * b:512 * b + 8], RBt[:, :], ALU.add),
            waits=[(mm, mv)], inc=vx)
        C.setfree(("bk", b), vx, kb.now(vx))
    for i in range(NTT):
        lg = LGt[:, 8 * i:8 * i + 8]
        kb.op("vector", lambda E, lg=lg: E.reduce_max(TKS[:, 0:1], lg, axis=AX), inc=vx)
        kb.wait("vector", [(vx, kb.now(vx))])
        kb.op("vector", lambda E, lg=lg, i=i: E.tensor_scalar(CW[:, 8 * i:8 * i + 8], lg, TKS[:, 0:1],
                                                              None, ALU.is_ge), inc=vx)
        kb.wait("vector", [(vx, kb.now(vx))])
        kb.op("vector", lambda E, lg=lg, i=i: E.scalar_tensor_tensor(
            SGt[:, 8 * i:8 * i + 8], CW[:, 8 * i:8 * i + 8], -1e30, lg, ALU.mult, ALU.add), inc=vx)
        kb.wait("vector", [(vx, kb.now(vx))])
        kb.op("vector", lambda E, i=i: E.reduce_max(TKS[:, 1:2], SGt[:, 8 * i:8 * i + 8], axis=AX),
              inc=vx)
        kb.wait("vector", [(vx, kb.now(vx))])
        kb.op("vector", lambda E: E.tensor_tensor(TKS[:, 2:3], TKS[:, 0:1], TKS[:, 1:2], ALU.add),
              inc=vx)
        kb.wait("vector", [(vx, kb.now(vx))])
        kb.op("vector", lambda E, lg=lg, i=i: E.tensor_scalar(SGt[:, 8 * i:8 * i + 8], lg, 2.0,
                                                              TKS[:, 2:3], ALU.mult, ALU.subtract),
              inc=vx)
        kb.op("scalar", lambda E, i=i: E.activation(SGt[:, 8 * i:8 * i + 8], SGt[:, 8 * i:8 * i + 8],
                                                    AF.Sigmoid),
              waits=[(vx, kb.now(vx))], inc=ev)
        kb.op("vector", lambda E, lg=lg, i=i: E.tensor_scalar(CW[:, 8 * i:8 * i + 8], lg, TKS[:, 1:2],
                                                              None, ALU.is_ge),
              waits=[(ev, kb.now(ev))], inc=vx)
        kb.wait("vector", [(vx, kb.now(vx))])
        kb.op("vector", lambda E, i=i: E.tensor_tensor(CW[:, 8 * i:8 * i + 8], CW[:, 8 * i:8 * i + 8],
                                                       SGt[:, 8 * i:8 * i + 8], ALU.mult), inc=vx)
    cwv = kb.now(vx)
    for i in range(NTT):
        kb.op("tensor", lambda E, i=i: E.transpose(C.ps[0:8, 3584:3712], CW[:, 8 * i:8 * i + 8],
                                                   IF32[:, :]),
              waits=[(vx, cwv), C.fwait(("bk", 7)), (ld, ldv)], inc=mm)
        kb.op("scalar", lambda E, i=i: E.copy(CWT[0:8, 128 * i:128 * (i + 1)], C.ps[0:8, 3584:3712]),
              waits=[(mm, kb.now(mm))], inc=ev)
        C.setfree(("bk", 7), ev, kb.now(ev))
    cwtv = kb.now(ev)

    for e in range(8):
        w2w = [C.fwait("w2r")] if e > 0 else []
        for hch in range(32):
            kb.dma("gpsimd", W2R[:, 1024 * hch:1024 * (hch + 1)], W2s[e, hch, :, :], sp,
                   w2w if hch == 0 else ())
        w2v = kb.now(sp)
        kb.dma("sync", B1L[:, 32 * (e % 2):32 * (e % 2) + 32], B1s[e, :, :], wl,
               [C.fwait(("b1", e % 2))])
        b1v = kb.now(wl)
        for hch in range(32):
            par = hch % 2
            kb.dma("sync", W1C[:, 1024 * par:1024 * (par + 1)], W1s[e, hch, :, :], wl,
                   [C.fwait(("w1c", par)), C.fwait("h1r")])
            wv = kb.now(wl)
            for g in range(T // 512):
                b = g % 2
                mm_acc(C, C.bank(b),
                       [((), W1C[:, 1024 * par + 128 * c:1024 * par + 128 * c + 128],
                         XN[c][:, 512 * g:512 * (g + 1)]) for c in range(NDT)],
                       inc=mm, waits=[(wl, wv), xn_ready, C.fwait(("bk", b))])
                mv = kb.now(mm)
                sl = (g % 2) * 512
                kb.op("scalar", lambda E, b=b, sl=sl, e=e, hch=hch: E.activation(
                    H1C[:, sl:sl + 512], C.bank(b), AF.Gelu,
                    bias=B1L[:, 32 * (e % 2) + hch:32 * (e % 2) + hch + 1]),
                    waits=[(mm, mv), (wl, b1v), C.fwait(("h1c", sl))], inc=ev)
                C.setfree(("bk", b), ev, kb.now(ev))
                kb.dma("gpsimd", h1_hbm[hch, :, 512 * g:512 * (g + 1)], H1C[:, sl:sl + 512], sp,
                       [(ev, kb.now(ev))])
                C.setfree(("h1c", sl), sp, kb.now(sp))
            C.setfree(("w1c", par), mm, kb.now(mm))
        C.setfree(("b1", e % 2), ev, kb.now(ev))
        h1done = kb.now(sp)
        for g in range(T // 512):
            for hch in range(32):
                kb.dma("sync", H1G[:, 512 * hch:512 * (hch + 1)],
                       h1_hbm[hch, :, 512 * g:512 * (g + 1)], wl,
                       [(sp, h1done), C.fwait("h1g")] if hch == 0 else ())
            hgv = kb.now(wl)
            for ti in range(4):
                for n in range(2):
                    b = 4 + (2 * ti + n) % 2
                    mm_acc(C, C.bank(b),
                           [((), H1G[:, 512 * hch + 128 * ti:512 * hch + 128 * ti + 128],
                             W2R[:, 1024 * hch + 512 * n:1024 * hch + 512 * n + 512])
                            for hch in range(32)],
                           inc=mm, waits=[(wl, hgv), (sp, w2v), C.fwait(("bk", b))])
                    mv = kb.now(mm)
                    i = 4 * g + ti
                    cwc = CW[:, 8 * i + e:8 * i + e + 1]
                    sl = (2 * ti + n) % 2 * 512
                    kb.op("vector", lambda E, b=b, sl=sl, cwc=cwc: E.tensor_scalar(
                        YT[:, sl:sl + 512], C.bank(b), cwc, None, ALU.mult),
                        waits=[(mm, mv), (vx, cwv), C.fwait(("yt", sl))], inc=vx)
                    C.setfree(("bk", b), vx, kb.now(vx))
                    if e == 0:
                        kb.dma("gpsimd", acc_hbm[128 * i:128 * (i + 1), 512 * n:512 * (n + 1)],
                               YT[:, sl:sl + 512], ac, [(vx, kb.now(vx))])
                    else:
                        kb.op("gpsimd", lambda E, i=i, n=n, sl=sl: E.dma_start(
                            out=acc_hbm[128 * i:128 * (i + 1), 512 * n:512 * (n + 1)],
                            in_=YT[:, sl:sl + 512], accum_op=ALU.add),
                            waits=[(vx, kb.now(vx))], inc=ac, inc_n=16)
                    C.setfree(("yt", sl), ac, kb.now(ac))
            C.setfree("h1g", mm, kb.now(mm))
        C.setfree("w2r", mm, kb.now(mm))
        C.setfree("h1r", wl, kb.now(wl))
    accv = kb.now(ac)

    for i in range(NTT):
        lv = kb.dma("sync", ACF[:, :], acc_hbm[128 * i:128 * (i + 1), :], ld,
                    [(ac, accv), C.fwait("acf")])
        xv = kb.dma("sync", XNL[:, :], xn_hbm[128 * i:128 * (i + 1), :], ld)
        for n in range(2):
            mm_acc(C, C.ps[0:128, 512 * 6 + 512 * 0:512 * 6 + 512],
                   [([(ev, cwtv)], CWT[0:8, 128 * i:128 * (i + 1)], B2S[0:8, 512 * n:512 * (n + 1)])],
                   inc=mm, waits=[C.fwait(("bk", 6))])
            mv = kb.now(mm)
            kb.op("vector", lambda E, n=n: E.tensor_tensor(
                ACF[:, 512 * n:512 * (n + 1)], ACF[:, 512 * n:512 * (n + 1)],
                C.ps[0:128, 512 * 6:512 * 6 + 512], ALU.add),
                waits=[(mm, mv), (ld, lv)], inc=vx)
            C.setfree(("bk", 6), vx, kb.now(vx))
        kb.op("vector", lambda E: E.tensor_tensor(ACF[:, :], ACF[:, :], XNL[:, :], ALU.add),
              waits=[(ld, xv)], inc=vx)
        vs = ln_stats(C, ACF[:, :], ST, 1e-12, vx, waits=[(vx, kb.now(vx))])
        kb.op("vector", lambda E: E.tensor_scalar(OT[:, :], ACF[:, :], ST[:, 6:7], ST[:, 7:8],
                                                  ALU.mult, ALU.add),
              waits=[(vx, vs), C.fwait("ot")], inc=ev)
        kb.op("vector", lambda E: E.tensor_tensor(OT[:, :], OT[:, :], LNW[:, :], ALU.mult),
              waits=[(ev, kb.now(ev)), (ld, ldv)], inc=vx)
        kb.op("vector", lambda E: E.tensor_tensor(OT[:, :], OT[:, :], LNB[:, :], ALU.add), inc=vx)
        sink(i, OT[:, :], [(vx, kb.now(vx))])
        C.setfree("ot", dc, dcb + 16 * (i + 1))
        C.setfree("acf", dc, dcb + 16 * (i + 1))
    return dc, kb.now(dc)


def build_l3_ffn_standalone(env=None):
    nc, kb, stack, pref, own = _env_open(env)
    if env is None:
        xn_hbm = nc.dram_tensor("xn", [T, D], BF16, kind="ExternalInput").ap()
        out = nc.dram_tensor("out", [T, D], F32, kind="ExternalOutput").ap()
    else:
        xn_hbm, out = env["xn"], env["out"]
    RWT = nc.dram_tensor("rwt", [D, 8], BF16, kind="ExternalInput").ap()
    RBb = nc.dram_tensor("rbb", [128, 8], F32, kind="ExternalInput").ap()
    W1s = nc.dram_tensor("w1s", [8, 32, 128, 1024], BF16, kind="ExternalInput").ap()
    B1s = nc.dram_tensor("b1s", [8, 128, 32], F32, kind="ExternalInput").ap()
    W2s = nc.dram_tensor("w2s", [8, 32, 128, 1024], BF16, kind="ExternalInput").ap()
    B2s = nc.dram_tensor("b2s", [8, 1024], BF16, kind="ExternalInput").ap()
    LNWd = nc.dram_tensor("lnw", [128, 1024], F32, kind="ExternalInput").ap()
    LNBd = nc.dram_tensor("lnb", [128, 1024], F32, kind="ExternalInput").ap()
    h1_hbm = nc.dram_tensor(pref + "h1s", [32, 128, 4096], BF16, kind="Internal").ap()
    acc_hbm = nc.dram_tensor(pref + "accs", [T, D], F32, kind="Internal").ap()
    C = Ctx(nc, kb, stack, pref)
    for c in range(NDT):
        C.alloc(f"XN{c}", [128, T], BF16)
    alloc_moe_l3(C)
    C.alloc_ps()
    xr = emit_xn_from_hbm(C, xn_hbm, "x0")

    def sink(i, ap, waits):
        kb.dma("gpsimd", out[128 * i:128 * (i + 1), :], ap, "fdc", waits)

    emit_moe_l3(C, "f", RWT, RBb, W1s, B1s, W2s, B2s, LNWd, LNBd,
                xn_hbm, h1_hbm, acc_hbm, xr, sink)
    kb.wait("gpsimd", [("fdc", kb.now("fdc"))])
    if own:
        kb.finalize(nc, stack)
    return nc, C, stack


def prep_l3_ffn_inputs(I):
    w1 = I["L3_ffn_w1"].astype(np.float32)
    w1T = w1.transpose(0, 2, 1)
    w1s = w1T.reshape(8, 8, 128, 32, 128).transpose(0, 3, 2, 1, 4).reshape(8, 32, 128, 1024)
    w2 = I["L3_ffn_w2"].astype(np.float32)
    w2s = w2.transpose(0, 2, 1).reshape(8, 32, 128, 1024)
    return {
        "rwt": as_bf16(I["L3_ffn_router_w"].T),
        "rbb": as_f32(np.tile(I["L3_ffn_router_b"][None, :], (128, 1))),
        "w1s": as_bf16(w1s),
        "b1s": as_f32(I["L3_ffn_b1"].reshape(8, 32, 128).transpose(0, 2, 1)),
        "w2s": as_bf16(w2s),
        "b2s": as_bf16(I["L3_ffn_b2"]),
        "lnw": as_f32(np.tile(I["L3_ffn_ln_w"][None, :], (128, 1))),
        "lnb": as_f32(np.tile(I["L3_ffn_ln_b"][None, :], (128, 1))),
    }


def alloc_ssm(C):
    C.alloc("WGB", [128, 1024], BF16)
    C.alloc("BGB", [128, 64], F32)
    C.alloc("GTs", [128, 128], F32)
    C.alloc("UTs", [128, 128], F32)
    C.alloc("UF", [128, T], BF16)
    C.alloc("ABC", [64, T], F32)
    C.alloc("STF", [64, T], BF16)
    C.alloc("ALC", [64, 2], F32)
    C.alloc("WCS", [64, 1024], BF16)
    C.alloc("HG", [128, 4096], F32)
    C.alloc("HSQ2", [128, 4096], F32)
    C.alloc("HN", [128, 4096], BF16)
    C.alloc("WOS", [128, 8192], BF16)
    C.alloc("RSB", [128, 512], F32)
    C.alloc("MRB", [128, 512], F32)
    C.alloc("STR", [1, 2048], F32)
    C.alloc("GAM", [128, 8], F32)
    C.alloc("BET", [128, 8], F32)
    C.alloc("DC1", [128, 8], F32)
    C.alloc("BOUT", [128, 1024], F32)
    C.alloc("DELTA3", [128, 1024], F32)
    C.alloc("ONESC", [128, 1], F32)
    C.alloc("ONESR", [1, 128], F32)


def emit_ssm(C, tag, WGBd, BGBd, ALd, WCTd, Dd, WOTd, BOUTd, GAMd, BETd,
             u_hbm, xn_ready, sink):
    kb = C.kb
    XN = [C.sb[f"XN{c}"] for c in range(NDT)]
    WGB, BGB, GT, UT = C.sb["WGB"], C.sb["BGB"], C.sb["GTs"], C.sb["UTs"]
    UF, ABC, STF, ALC = C.sb["UF"], C.sb["ABC"], C.sb["STF"], C.sb["ALC"]
    WCS, HG, HSQ, HN = C.sb["WCS"], C.sb["HG"], C.sb["HSQ2"], C.sb["HN"]
    WOS, RSB, MRB, STR = C.sb["WOS"], C.sb["RSB"], C.sb["MRB"], C.sb["STR"]
    GAM, BET, DC1, BOUT = C.sb["GAM"], C.sb["BET"], C.sb["DC1"], C.sb["BOUT"]
    ONESC, ONESR = C.sb["ONESC"], C.sb["ONESR"]
    dcb = C.kb.now(f"{tag}dc")
    ld, mm, ev, vx, st, dc = (f"{tag}ld", f"{tag}mm", f"{tag}ev", f"{tag}vx",
                              f"{tag}st", f"{tag}dc")
    for c in range(NDT):
        kb.dma("sync", WGB[:, 128 * c:128 * (c + 1)], WGBd[128 * c:128 * (c + 1), :], ld)
    kb.dma("sync", BGB[:, :], BGBd[:, :], ld)
    kb.dma("sync", ALC[0:64, 0:1], ALd[:, :], ld)
    kb.dma("sync", WCS[0:64, :], WCTd[:, :], ld)
    kb.dma("sync", C.sb["DC1"][:, :], Dd[:, :], ld)
    for c in range(NDT):
        kb.dma("sync", WOS[:, 1024 * c:1024 * (c + 1)],
               WOTd[128 * c:128 * (c + 1), :], ld)
    kb.dma("sync", BOUT[:, :], BOUTd[:, :], ld)
    kb.dma("sync", GAM[:, :], GAMd[:, :], ld)
    kb.dma("sync", BET[:, :], BETd[:, :], ld)
    kb.dma("sync", ONESC[:, :], C.const("c_onescol", as_f32(np.ones((128, 1)))), ld)
    kb.dma("sync", ONESR[0:1, :], C.const("c_onesrow", as_f32(np.ones((1, 128)))), ld)
    ldv = kb.now(ld)
    kb.op("vector", lambda E: E.tensor_scalar(DC1[:, :], DC1[:, :], 1.0, None, ALU.add),
          waits=[(ld, ldv)], inc=vx)
    d1v = kb.now(vx)

    for i in range(NTT):
        b = i % 2
        sl = (i % 2) * 64
        mm_acc(C, C.ps[0:128, 512 * b:512 * b + 128],
               [((), XN[c][:, 128 * i:128 * (i + 1)], WGB[:, 128 * c:128 * (c + 1)])
                for c in range(NDT)],
               inc=mm, waits=[(ld, ldv), xn_ready, C.fwait(("bk", b))])
        mv = kb.now(mm)
        kb.op("vector", lambda E, b=b, sl=sl: E.tensor_tensor(
            GT[:, sl:sl + 64], C.ps[0:128, 512 * b:512 * b + 64], BGB[:, :], ALU.add),
            waits=[(mm, mv), C.fwait(("gt", sl))], inc=vx)
        kb.op("scalar", lambda E, sl=sl: E.activation(GT[:, sl:sl + 64], GT[:, sl:sl + 64], AF.Sigmoid),
              waits=[(vx, kb.now(vx))], inc=ev)
        kb.op("vector", lambda E, b=b, sl=sl: E.tensor_tensor(
            UT[:, sl:sl + 64], GT[:, sl:sl + 64], C.ps[0:128, 512 * b + 64:512 * b + 128], ALU.mult),
            waits=[(ev, kb.now(ev)), C.fwait(("ut", sl))], inc=vx)
        C.setfree(("bk", b), vx, kb.now(vx))
        C.setfree(("gt", sl), vx, kb.now(vx))
        kb.dma("gpsimd", u_hbm[128 * i:128 * (i + 1), 0:64], UT[:, sl:sl + 64], st,
               [(vx, kb.now(vx))])
        C.setfree(("ut", sl), st, kb.now(st))
    ustv = kb.now(st)
    kb.dmat("sync", UF[0:128, :], u_hbm[:, :], ld, [(st, ustv)])
    ufv = kb.now(ld)
    kb.op("scalar", lambda E: E.activation(ALC[0:64, 1:2], ALC[0:64, 0:1], AF.Exp),
          waits=[(ld, ldv)], inc=ev)
    kb.op("vector", lambda E: E.tensor_scalar(ABC[0:64, :], UF[0:64, :], 0.0,
                                              ALC[0:64, 1:2], ALU.mult, ALU.add),
          waits=[(ev, kb.now(ev)), (ld, ufv)], inc=vx)
    kb.wait("vector", [(vx, kb.now(vx))])
    kb.op("vector", lambda E: E.tensor_tensor_scan(STF[0:64, :], ABC[0:64, :], UF[0:64, :],
                                                   0.0, ALU.mult, ALU.add), inc=vx)
    scv = kb.now(vx)

    for tc in range(T // 512):
        tsl = slice(512 * tc, 512 * (tc + 1))
        for hc in range(NDT):
            b = hc % 2
            mm_acc(C, C.bank(b),
                   [((), WCS[0:64, 128 * hc:128 * (hc + 1)], STF[0:64, tsl])],
                   inc=mm, waits=[(vx, scv), (ld, ldv), C.fwait(("bk", b))])
            mv = kb.now(mm)
            kb.op("vector", lambda E, b=b, hc=hc, tsl=tsl: E.scalar_tensor_tensor(
                HG[:, 512 * hc:512 * (hc + 1)], XN[hc][:, tsl], DC1[:, hc:hc + 1],
                C.bank(b), ALU.mult, ALU.add),
                waits=[(mm, mv), (vx, d1v), C.fwait(("hg", hc))], inc=vx)
            C.setfree(("bk", b), vx, kb.now(vx))
        hgv = kb.now(vx)
        kb.op("scalar", lambda E: E.activation(HSQ[:, :], HG[:, :], AF.Square),
              waits=[(vx, hgv), C.fwait("hsq")], inc=ev)
        sqv = kb.now(ev)
        mm_acc(C, C.ps[0:1, 512 * 4:512 * 4 + 512],
               [((), ONESC[:, :], HG[:, 512 * hc:512 * (hc + 1)]) for hc in range(NDT)],
               inc=mm, waits=[(vx, hgv), C.fwait(("bk", 4))])
        mm_acc(C, C.ps[0:1, 512 * 5:512 * 5 + 512],
               [((), ONESC[:, :], HSQ[:, 512 * hc:512 * (hc + 1)]) for hc in range(NDT)],
               inc=mm, waits=[(ev, sqv), C.fwait(("bk", 5))])
        smv = kb.now(mm)
        kb.op("vector", lambda E: E.tensor_scalar(STR[0:1, 0:512], C.ps[0:1, 512 * 4:512 * 4 + 512],
                                                  1.0 / D, None, ALU.mult),
              waits=[(mm, smv)], inc=vx)
        kb.op("vector", lambda E: E.tensor_scalar(STR[0:1, 512:1024], C.ps[0:1, 512 * 5:512 * 5 + 512],
                                                  1.0 / D, None, ALU.mult), inc=vx)
        C.setfree(("bk", 4), vx, kb.now(vx))
        C.setfree(("bk", 5), vx, kb.now(vx))
        kb.op("vector", lambda E: E.tensor_tensor(STR[0:1, 1024:1536], STR[0:1, 0:512],
                                                  STR[0:1, 0:512], ALU.mult), inc=vx)
        kb.wait("vector", [(vx, kb.now(vx))])
        kb.op("vector", lambda E: E.tensor_tensor(STR[0:1, 512:1024], STR[0:1, 512:1024],
                                                  STR[0:1, 1024:1536], ALU.subtract), inc=vx)
        epsap, epsw = C.fconst(1e-5)
        kb.op("scalar", lambda E, epsap=epsap: E.activation(STR[0:1, 1024:1536], STR[0:1, 512:1024],
                                                            AF.Sqrt, bias=epsap[0:1, :]),
              waits=[(vx, kb.now(vx)), epsw], inc=ev)
        kb.op("vector", lambda E: E.reciprocal(STR[0:1, 512:1024], STR[0:1, 1024:1536]),
              waits=[(ev, kb.now(ev))], inc=vx)
        kb.wait("vector", [(vx, kb.now(vx))])
        kb.op("vector", lambda E: E.scalar_tensor_tensor(STR[0:1, 0:512], STR[0:1, 0:512], -1.0,
                                                         STR[0:1, 512:1024], ALU.mult, ALU.mult),
              inc=vx)
        stv = kb.now(vx)
        mm_acc(C, C.ps[0:128, 512 * 6:512 * 6 + 512],
               [([(vx, stv)], ONESR[0:1, :], STR[0:1, 512:1024])], inc=mm,
               waits=[C.fwait(("bk", 6))])
        mm_acc(C, C.ps[0:128, 512 * 7:512 * 7 + 512],
               [((), ONESR[0:1, :], STR[0:1, 0:512])], inc=mm,
               waits=[C.fwait(("bk", 7))])
        bcv = kb.now(mm)
        kb.op("scalar", lambda E: E.copy(RSB[:, :], C.ps[0:128, 512 * 6:512 * 6 + 512]),
              waits=[(mm, bcv), C.fwait("rsb")], inc=ev)
        kb.op("scalar", lambda E: E.copy(MRB[:, :], C.ps[0:128, 512 * 7:512 * 7 + 512]), inc=ev)
        bcev = kb.now(ev)
        C.setfree(("bk", 6), ev, bcev)
        C.setfree(("bk", 7), ev, bcev)
        for hc in range(NDT):
            hsl = slice(512 * hc, 512 * (hc + 1))
            kb.op("vector", lambda E, hsl=hsl: E.tensor_tensor(HSQ[:, hsl], HG[:, hsl], RSB[:, :],
                                                               ALU.mult),
                  waits=[(ev, bcev), (ev, sqv)], inc=vx)
            kb.op("vector", lambda E, hsl=hsl: E.tensor_tensor(HSQ[:, hsl], HSQ[:, hsl], MRB[:, :],
                                                               ALU.add), inc=vx)
            kb.op("vector", lambda E, hsl=hsl, hc=hc: E.tensor_scalar(
                HN[:, hsl], HSQ[:, hsl], GAM[:, hc:hc + 1], BET[:, hc:hc + 1], ALU.mult, ALU.add),
                waits=[(ld, ldv), (vx, kb.now(vx))], inc=vx)
            C.setfree(("hg", hc), vx, kb.now(vx))
        hnv = kb.now(vx)
        C.setfree("hsq", vx, hnv)
        C.setfree("rsb", vx, hnv)
        for tt in range(4):
            i = 4 * tc + tt
            for n in range(2):
                b = n
                mm_acc(C, C.bank(b),
                       [((), HN[:, 512 * hc + 128 * tt:512 * hc + 128 * tt + 128],
                         WOS[:, 1024 * hc + 512 * n:1024 * hc + 512 * n + 512])
                        for hc in range(NDT)],
                       inc=mm, waits=[(vx, hnv), (ld, ldv), C.fwait(("bk", b))])
            mv = kb.now(mm)
            kb.op("vector", lambda E: E.tensor_tensor(C.sb["DELTA3"][:, 0:512],
                                                      C.ps[0:128, 0:512], BOUT[:, 0:512], ALU.add),
                  waits=[(mm, mv), (dc, dcb + 16 * i)], inc=vx)
            kb.op("vector", lambda E: E.tensor_tensor(C.sb["DELTA3"][:, 512:1024],
                                                      C.ps[0:128, 512:1024], BOUT[:, 512:1024],
                                                      ALU.add), inc=vx)
            C.setfree(("bk", 0), vx, kb.now(vx))
            C.setfree(("bk", 1), vx, kb.now(vx))
            sink(i, C.sb["DELTA3"][:, 0:1024], [(vx, kb.now(vx))])
    return dc, kb.now(dc)


def build_l3_attn_standalone(env=None):
    nc, kb, stack, pref, own = _env_open(env)
    if env is None:
        xn_hbm = nc.dram_tensor("xn", [T, D], BF16, kind="ExternalInput").ap()
        out = nc.dram_tensor("out", [T, D], F32, kind="ExternalOutput").ap()
    else:
        xn_hbm, out = env["xn"], env["out"]
    WGBd = nc.dram_tensor("wgb", [D, 128], BF16, kind="ExternalInput").ap()
    BGBd = nc.dram_tensor("bgb", [128, 64], F32, kind="ExternalInput").ap()
    ALd = nc.dram_tensor("alog", [64, 1], F32, kind="ExternalInput").ap()
    WCTd = nc.dram_tensor("wct", [64, D], BF16, kind="ExternalInput").ap()
    Dd = nc.dram_tensor("dvec", [128, 8], F32, kind="ExternalInput").ap()
    WOTd = nc.dram_tensor("wot3", [D, D], BF16, kind="ExternalInput").ap()
    BOUTd = nc.dram_tensor("bout", [128, 1024], F32, kind="ExternalInput").ap()
    GAMd = nc.dram_tensor("gam", [128, 8], F32, kind="ExternalInput").ap()
    BETd = nc.dram_tensor("bet", [128, 8], F32, kind="ExternalInput").ap()
    u_hbm = nc.dram_tensor(pref + "u_s", [T, 128], BF16, kind="Internal").ap()
    C = Ctx(nc, kb, stack, pref)
    for c in range(NDT):
        C.alloc(f"XN{c}", [128, T], BF16)
    alloc_ssm(C)
    C.alloc_ps()
    xr = emit_xn_from_hbm(C, xn_hbm, "x0")

    def sink(i, ap, waits):
        kb.dma("gpsimd", out[128 * i:128 * (i + 1), :], ap, "sdc", waits)

    emit_ssm(C, "s", WGBd, BGBd, ALd, WCTd, Dd, WOTd, BOUTd, GAMd, BETd,
             u_hbm, xr, sink)
    kb.wait("gpsimd", [("sdc", kb.now("sdc"))])
    if own:
        kb.finalize(nc, stack)
    return nc, C, stack


def prep_l3_attn_inputs(I):
    return {
        "wgb": as_bf16(np.concatenate([I["L3_attn_Wg"].T, I["L3_attn_WB"].T], axis=1)),
        "bgb": as_f32(np.tile(I["L3_attn_bg"][None, :], (128, 1))),
        "alog": as_f32(I["L3_attn_A_log"][:, None]),
        "wct": as_bf16(I["L3_attn_WC"].T),
        "dvec": as_f32(I["L3_attn_D"].reshape(8, 128).T),
        "wot3": as_bf16(I["L3_attn_Wout"].T),
        "bout": as_f32(np.tile(I["L3_attn_bout"][None, :], (128, 1))),
        "gam": as_f32(I["L3_attn_gamma"].reshape(8, 128).T),
        "bet": as_f32(I["L3_attn_beta"].reshape(8, 128).T),
    }

W1M = ([(128 * j, 128 if j < 21 else 42) for j in range(22)] +
       [(2730 + 128 * j, 128 if j < 21 else 42) for j in range(22)])
W2K = [(128 * j, 128 if j < 21 else 42) for j in range(22)]


class TV:
    def __init__(self, h, coff=0, dtype=None, ncols=None):
        self.h = h.bitcast(dtype) if dtype is not None else h
        self.coff = coff
        self.ncols = ncols if ncols is not None else (self.h.shape[1] - coff)

    def __getitem__(self, idx):
        pp, cc = idx
        st = (cc.start or 0) + self.coff
        sp = (self.ncols if cc.stop is None else cc.stop) + self.coff
        return self.h[pp, st:sp]


def vap(tile, off, dims):
    if isinstance(tile, TV):
        off = off + tile.coff
        tile = tile.h
    t = tile[:, 0:1]
    fw = t.ap[0][0]
    return bass.AP(t.tensor, off, [[fw, 128]] + dims)


def alloc_ffn(C):
    C.alloc("TBL", [128, T], BF16)
    bfa = C.alloc("BFA", [128, 4096], BF16)
    C.sb["XIB"] = TV(bfa, 0)
    C.sb["QR"] = TV(bfa, 1024, F32, 512)
    C.sb["Q1"] = TV(bfa, 1536, F32, 512)
    C.sb["LNH"] = TV(bfa, 0)
    XNs = [C.sb[f"XN{c}"] for c in range(NDT)]
    if T >= 4096:
        C.alloc("WS", [128, 4096], BF16)
        C.sb["SQB"] = TV(C.sb["OF7"], 0)
        C.sb["W2SA"] = TV(XNs[0], 2080, None, 1920)
        C.sb["W2SB"] = TV(XNs[1], 2080, None, 896)
        C.sb["ST0"] = TV(XNs[2], 1040, F32, 512)
        C.sb["ST1"] = TV(XNs[3], 1040, F32, 512)
        C.sb["ST2"] = TV(XNs[4], 1040, F32, 512)
        C.sb["RSB0"] = TV(XNs[5], 1040, F32, 512)
        C.sb["RSB1"] = TV(XNs[6], 1040, F32, 512)
        C.sb["RW0"] = TV(XNs[5], 1552, F32, 256)
        C.sb["RW1v"] = TV(XNs[6], 1552, F32, 256)
        C.sb["AGT"] = TV(XNs[7], 2080, None, 1024)
        C.sb["QKS"] = TV(XNs[7], 3104, None, 512)
    else:
        C.alloc("WS", [128, 4096], BF16)
        C.sb["SQB"] = TV(C.alloc("SQBd", [128, 4096], BF16), 0)
        C.alloc("H4d", [128, 11264], BF16)
        w2d = C.alloc("W2SD", [128, 2816], BF16)
        C.sb["W2SA"] = TV(w2d, 0, None, 1920)
        C.sb["W2SB"] = TV(w2d, 1920, None, 896)
        st2 = C.alloc("ST2D", [1, 1536], F32)
        C.sb["ST0"] = TV(st2, 0, None, 512)
        C.sb["ST1"] = TV(st2, 512, None, 512)
        C.sb["ST2"] = TV(st2, 1024, None, 512)
        rsb = C.alloc("RSB2D", [128, 1024], F32)
        C.sb["RSB0"] = TV(rsb, 0, None, 512)
        C.sb["RSB1"] = TV(rsb, 512, None, 512)
        rw = C.alloc("RWD", [128, 512], F32)
        C.sb["RW0"] = TV(rw, 0, None, 256)
        C.sb["RW1v"] = TV(rw, 256, None, 256)
        C.alloc("AGTD", [128, 1024], BF16)
        C.sb["AGT"] = TV(C.sb["AGTD"], 0)
        C.alloc("QKSD", [128, 512], BF16)
        C.sb["QKS"] = TV(C.sb["QKSD"], 0)
    C.alloc("RSc", [128, 32], F32)
    C.alloc("XIF", [128, 512], F32)
    C.alloc("OUTT", [128, 1024], F32)
    C.alloc("B1C", [128, 44], F32)
    C.alloc("B2C", [128, 8], F32)
    C.alloc("ONESCB", [128, 1], BF16)
    C.alloc("ONESCF", [128, 1], F32)
    C.alloc("ONESR2", [1, 128], F32)
    C.alloc("LNGB", [128, 32], F32)
    C.alloc("ZCOL", [128, 1], BF16)
    C.alloc("IDF2", [128, 128], F32)


def emit_ffn(C, tag, Wqd, Wkd, Wvd, Wod, W1d, B1d, W2d, B2d, LNGd, LNBd,
             QSd, QSRd, KSd, KSRd, CQd, SQd, CKd, SKd,
             xi_hbm, qk_hbm, xn_ready, const_ready, sink, nlayers=2, phases="ab"):
    kb = C.kb
    XN = [C.sb[f"XN{c}"] for c in range(NDT)]
    OF = [C.sb[f"OF{c}"] for c in range(4)]
    QP, KP = C.sb["OF4"], C.sb["OF5"]
    TB1, TB2 = C.sb["OF6"], C.sb["OF7"]
    TQC, TQS = C.sb["VA"], C.sb["VB"]
    TBL = C.sb["TBL"]
    VH = XN
    QA, QB, KA, KB_ = C.sb["QA"], C.sb["QB"], C.sb["KA"], C.sb["KB_"]
    WS, XIB, QKS = C.sb["WS"], C.sb["XIB"], C.sb["QKS"]
    QR, Q1, RS = C.sb["QR"], C.sb["Q1"], C.sb["RSc"]
    RW0, RW1v = C.sb["RW0"], C.sb["RW1v"]
    W2SA, W2SB = C.sb["W2SA"], C.sb["W2SB"]

    def w2sl(kk, rows):
        if kk < 15:
            return W2SA[rows, 128 * kk:128 * (kk + 1)]
        return W2SB[rows, 128 * (kk - 15):128 * (kk - 14)]
    LNH, SQB, AGT = C.sb["LNH"], C.sb["SQB"], C.sb["AGT"]
    if "H4d" in C.sb:
        H4T = [TV(C.sb["H4d"], 4096 * z_) for z_ in range(3)]
    else:
        H4T = [TV(QP, 0), TV(KP, 0), TV(TB1, 0)]

    def h4sl(j, rows, c0=0, c1=512):
        g = 512 * j
        return H4T[g // 4096][rows, g % 4096 + c0:g % 4096 + c1]
    RSB0, RSB1 = C.sb["RSB0"], C.sb["RSB1"]
    XIF, OUTT = C.sb["XIF"], C.sb["OUTT"]
    ST0, ST1, ST2 = C.sb["ST0"], C.sb["ST1"], C.sb["ST2"]
    B1C, B2C = C.sb["B1C"], C.sb["B2C"]
    ONESCB, ONESR2, LNGB, ZCOL = C.sb["ONESCB"], C.sb["ONESR2"], C.sb["LNGB"], C.sb["ZCOL"]
    IDF = C.sb["IDF2"]
    PT, MBc = C.sb["PT"], C.sb["MBc"]
    RL, RB = C.sb["RL"], C.sb["RB"]
    negI, ones64 = C.sb["negI"], C.sb["ones1x64"]
    maskT, mask = C.sb["maskT"], C.sb["mask"]

    ld, mm, ev, vx, pxp = f"{tag}ld", f"{tag}mm", f"{tag}ev", f"{tag}vx", f"{tag}px"
    wl, st, dc_, ofs = f"{tag}wl", f"{tag}st", f"{tag}dc", f"{tag}of"

    TW = T // 2
    kb.dma("sync", ONESCB[:, :], C.const("c_onescol_bf", as_bf16(np.ones((128, 1)))), ld)
    kb.dma("sync", ONESR2[0:1, :], C.const("c_onesrow", as_f32(np.ones((1, 128)))), ld)
    kb.dma("sync", ZCOL[:, :], C.const("c_zcol", as_bf16(np.zeros((128, 1)))), ld)
    kb.dma("sync", IDF[:, :], C.const("c_IF32", as_f32(np.eye(128))), ld)
    kb.dma("sync", LNGB[:, 0:8], LNGd[0, :, :], ld)
    kb.dma("sync", LNGB[:, 8:16], LNBd[0, :, :], ld)
    kb.dma("sync", LNGB[:, 16:24], LNGd[1, :, :], ld)
    kb.dma("sync", LNGB[:, 24:32], LNBd[1, :, :], ld)
    onerow_d = C.const("c_onerow", np.ones((1, T), BF))
    kb.dma("sync", KA[64:65, :], onerow_d[0:1, :], ld)
    kb.dma("sync", KB_[64:65, :], onerow_d[0:1, :], ld)
    ldv = kb.now(ld)
    kb.wait("tensor", [(ld, ldv)])

    for c in range(NDT):
        kb.dma("gpsimd", xi_hbm[c, :, :], XN[c][:, :], st, [xn_ready])
    xiv = kb.now(st)
    xi_ready = xiv

    for l in range(nlayers):
        qsld = f"{tag}q{l}"
        kb.dma("sync", TB1[:, 0:TW], CQd[:, :], qsld,
               [xn_ready, const_ready, C.fwait("h4of"), (ld, kb.now(ld))])
        kb.dma("sync", TB1[:, TW:2 * TW], SQd[:, :], qsld)
        kb.dma("sync", TB2[:, 0:TW], CKd[:, :], qsld)
        kb.dma("sync", TB2[:, TW:2 * TW], SKd[:, :], qsld)
        kb.dma("sync", QR[:, 0:64], QSd[l, :, :], qsld, [C.fwait("qr")])
        kb.dma("sync", QR[:, 64:128], QSRd[l, :, :], qsld)
        kb.dma("sync", QR[:, 128:192], KSd[l, :, :], qsld)
        kb.dma("sync", QR[:, 192:256], KSRd[l, :, :], qsld)
        qslv = kb.now(qsld)
        for (dst, doff, src, soff, sco) in (
                (TQC, 0, TB1, 0, 0), (TQS, 0, TB1, TW, 64),
                (TBL, 0, TB2, 0, 128), (TBL, TW, TB2, TW, 192)):
            kb.op("vector", lambda E, dst=dst, doff=doff, src=src, soff=soff, sco=sco:
                  E.tensor_tensor(
                      vap(dst, doff, [[64, NTT], [1, 64]]),
                      vap(src, soff, [[64, NTT], [1, 64]]),
                      vap(QR, sco, [[0, NTT], [1, 64]]),
                      ALU.mult),
                  waits=[(qsld, qslv), (ld, ldv), C.fwait(("tbl", l))], inc=vx)
        tblv = kb.now(vx)
        C.setfree("qr", vx, tblv)

        for fi, (fw_, dfw, doff, fdram, hbase) in enumerate((
                (512, 512, 0, Wqd, 0), (512, 512, 0, Wkd, 0),
                (260, 520, 0, Wvd, 0), (260, 520, 260, Wvd, 4))):
            for c in range(NDT):
                kb.dma("sync", WS[:, fw_ * c:fw_ * c + fw_],
                       fdram[l, :, dfw * c + doff:dfw * c + doff + fw_], wl,
                       [C.fwait("ws")] if c == 0 else [])
            wlv = kb.now(wl)
            for i in range(NTT):
                par = i % 2
                o = 1024 * par
                gl = f"{tag}gl"
                xw = [C.fwait(("xib", par)), (st, xi_ready)]
                for c in range(NDT):
                    if c < 4:
                        kb.dma("gpsimd", XIB[:, o + 128 * c:o + 128 * c + 128],
                               xi_hbm[c, :, 128 * i:128 * (i + 1)], gl, xw if c == 0 else [])
                    elif i > 0:
                        kb.dma("gpsimd", XIB[:, o + 128 * c:o + 128 * c + 128],
                               xi_hbm[c, :, 128 * i - 1:128 * (i + 1) - 1], gl)
                    else:
                        kb.op("scalar", lambda E, o=o, c=c: E.copy(
                            XIB[:, o + 128 * c:o + 128 * c + 1], ZCOL[:, :]),
                            waits=[(ld, ldv), C.fwait(("xib", par))], inc=ev)
                        kb.dma("gpsimd", XIB[:, o + 128 * c + 1:o + 128 * c + 128],
                               xi_hbm[c, :, 0:127], gl)
                xlv = kb.now(gl)
                if i == 0:
                    kb.wait("tensor", [(ev, kb.now(ev))])
                b = par
                mm_acc(C, C.ps[0:128, 512 * b:512 * b + fw_],
                       [((), XIB[:, o + 128 * c:o + 128 * c + 128],
                         WS[:, fw_ * c:fw_ * c + fw_]) for c in range(NDT)],
                       inc=mm, waits=[(gl, xlv), (wl, wlv), C.fwait(("bk", b))])
                mv = kb.now(mm)
                C.setfree(("xib", par), mm, mv)
                if fi < 2:
                    kb.op("vector", lambda E, b=b: E.tensor_copy(
                        QR[:, :], C.ps[0:128, 512 * b:512 * b + 512]),
                        waits=[(mm, mv), C.fwait("qr")], inc=vx)
                    C.setfree(("bk", b), vx, kb.now(vx))
                    kb.op("scalar", lambda E: E.activation(Q1[:, :], QR[:, :], AF.Square),
                          waits=[(vx, kb.now(vx)), C.fwait("q1")], inc=ev)
                    kb.op("vector", lambda E: E.reduce_sum(
                        RS[:, 0:8], Q1[:, :].rearrange("p (s d) -> p s d", d=64), axis=AX),
                        waits=[(ev, kb.now(ev))], inc=vx)
                    kb.wait("vector", [(vx, kb.now(vx))])
                    kb.op("vector", lambda E: E.tensor_scalar(RS[:, 8:16], RS[:, 0:8], 1e-24,
                                                              None, ALU.max), inc=vx)
                    kb.wait("scalar", [(vx, kb.now(vx))])
                    kb.op("scalar", lambda E: E.activation(RS[:, 16:24], RS[:, 8:16], AF.Sqrt),
                          inc=ev)
                    kb.wait("vector", [(ev, kb.now(ev))])
                    kb.op("vector", lambda E: E.reciprocal(RS[:, 24:32], RS[:, 16:24]), inc=vx)
                    kb.wait("vector", [(vx, kb.now(vx))])
                    kb.op("vector", lambda E: E.tensor_tensor(
                        Q1[:, :].rearrange("p (s d) -> p s d", d=64),
                        QR[:, :].rearrange("p (s d) -> p s d", d=64),
                        vap(RS, 24, [[1, 8], [0, 64]]), ALU.mult), inc=vx)
                    kb.wait("vector", [(vx, kb.now(vx))])
                    ct, cto = (TQC, 0) if fi == 0 else (TBL, 0)
                    stt, sto = (TQS, 0) if fi == 0 else (TBL, TW)
                    kb.op("vector", lambda E, ct=ct, cto=cto, i=i: E.tensor_tensor(
                        QR[:, :].rearrange("p (s d) -> p s d", d=64),
                        Q1[:, :].rearrange("p (s d) -> p s d", d=64),
                        vap(ct, cto + 64 * i, [[0, 8], [1, 64]]), ALU.mult),
                        waits=[(vx, tblv)], inc=vx)
                    kb.op("vector", lambda E, stt=stt, sto=sto, i=i: E.tensor_tensor(
                        vap(RW0, 0, [[32, 8], [1, 32]]),
                        vap(Q1, 32, [[64, 8], [1, 32]]),
                        vap(stt, sto + 64 * i, [[0, 8], [1, 32]]), ALU.mult),
                        waits=[C.fwait("rw")], inc=vx)
                    kb.op("vector", lambda E, stt=stt, sto=sto, i=i: E.tensor_tensor(
                        vap(RW1v, 0, [[32, 8], [1, 32]]),
                        vap(Q1, 0, [[64, 8], [1, 32]]),
                        vap(stt, sto + 64 * i + 32, [[0, 8], [1, 32]]), ALU.mult), inc=vx)
                    kb.wait("vector", [(vx, kb.now(vx))])
                    kb.op("vector", lambda E: E.tensor_tensor(
                        vap(QKS, 0, [[64, 8], [1, 32]]),
                        vap(QR, 0, [[64, 8], [1, 32]]),
                        vap(RW0, 0, [[32, 8], [1, 32]]), ALU.subtract),
                        waits=[C.fwait("qks")], inc=vx)
                    kb.op("vector", lambda E: E.tensor_tensor(
                        vap(QKS, 32, [[64, 8], [1, 32]]),
                        vap(QR, 32, [[64, 8], [1, 32]]),
                        vap(RW1v, 0, [[32, 8], [1, 32]]), ALU.add), inc=vx)
                    kb.wait("vector", [(vx, kb.now(vx))])
                    C.setfree("qr", vx, kb.now(vx))
                    C.setfree("q1", vx, kb.now(vx))
                    C.setfree("rw", vx, kb.now(vx))
                    kb.dma("gpsimd", qk_hbm[128 * i:128 * (i + 1), 512 * fi:512 * (fi + 1)],
                           QKS[:, :], st, [(vx, kb.now(vx))])
                    C.setfree("qks", st, kb.now(st))
                else:
                    for hh2 in range(4):
                        kb.op("scalar", lambda E, hh2=hh2, hbase=hbase, i=i, b=b: E.copy(
                            VH[hbase + hh2][:, 65 * i:65 * i + 65],
                            C.ps[0:128, 512 * b + 65 * hh2:512 * b + 65 * hh2 + 65]),
                            waits=[(mm, mv), (st, xi_ready)] if hh2 == 0 else [], inc=ev)
                    C.setfree(("bk", b), ev, kb.now(ev))
            C.setfree("ws", mm, kb.now(mm))
        qkst = kb.now(st)
        for h in range(8):
            kb.op("scalar", lambda E, h=h: E.activation(
                VH[h][:, 64::65], VH[h][:, 64::65], AF.Copy, bias=1.0),
                waits=[(ev, kb.now(ev))], inc=ev)
        vdone = kb.now(ev)

        for p in range(4):
            pr = f"{tag}pr{l}"
            kb.dmat("sync", QP[:, :], qk_hbm[:, 128 * p:128 * (p + 1)], pr,
                    [(st, qkst), C.fwait("qp"), C.fwait("h4of")])
            kb.dmat("sync", KP[:, :], qk_hbm[:, 512 + 128 * p:512 + 128 * (p + 1)], pr)
            prv = kb.now(pr)
            kb.op("vector", lambda E: E.tensor_copy(QA[0:64, :], QP[0:64, :]),
                  waits=[(pr, prv), C.fwait("qa")], inc=vx)
            kb.op("vector", lambda E: E.tensor_copy(QB[0:64, :], QP[64:128, :]), inc=vx)
            kb.op("vector", lambda E: E.tensor_copy(KA[0:64, :], KP[0:64, :]), inc=vx)
            kb.op("vector", lambda E: E.tensor_copy(KB_[0:64, :], KP[64:128, :]), inc=vx)
            C.setfree("qp", vx, kb.now(vx))
            qkv_v = kb.now(vx)
            for hh in range(2):
                h = 2 * p + hh
                Q = (QA, QB)[hh]
                K = (KA, KB_)[hh]
                V = VH[h]
                for q in range(NTT):
                    klo = max(0, q - 4)
                    Wd = 128 * (q + 1) - 128 * klo
                    fwt = [(vx, qkv_v), (ev, vdone), C.fwait("p1banks")]
                    for chk in range((Wd + 511) // 512):
                        cw = min(512, Wd - 512 * chk)
                        mm_acc(C, C.ps[0:128, 512 * chk:512 * chk + cw],
                               [((), Q[0:64, 128 * q:128 * (q + 1)],
                                 K[0:64, 128 * klo + 512 * chk:128 * klo + 512 * chk + cw])],
                               inc=mm, waits=fwt if chk == 0 else ())
                    ms = kb.now(mm)
                    rw_ = [(mm, ms)]
                    if q >= 4:
                        kb.op("vector", lambda E: E.tensor_tensor(
                            C.ps[0:128, 0:128], C.ps[0:128, 0:128], maskT[:, :], ALU.add),
                            waits=[(mm, ms)], inc=vx)
                    kb.op("vector", lambda E, Wd=Wd: E.tensor_tensor(
                        C.ps[0:128, Wd - 128:Wd], C.ps[0:128, Wd - 128:Wd], mask[:, :], ALU.add),
                        waits=[(mm, ms)], inc=vx)
                    rw_.append((vx, kb.now(vx)))
                    kb.op("vector", lambda E, Wd=Wd, hh=hh: E.reduce_max(
                        MBc[:, hh:hh + 1], C.ps[0:128, 0:Wd], axis=AX), waits=rw_, inc=vx)
                    mm_acc(C, C.ps[0:1, 0:128],
                           [([(vx, kb.now(vx))], MBc[:, hh:hh + 1], negI[:, :])], inc=mm)
                    kb.op("scalar", lambda E, q=q, Q=Q: E.copy(
                        Q[64:65, 128 * q:128 * (q + 1)], C.ps[0:1, 0:128]),
                        waits=[(mm, kb.now(mm))], inc=ev)
                    C.setfree("p1banks", ev, kb.now(ev))
                qaugv = kb.now(ev)
                p1vx = kb.now(vx)
                ptfree = {}
                for g in range(T // 512):
                    jlo = max(0, 4 * g - 4)
                    for j in range(jlo, 4 * g + 4):
                        qlo = max(512 * g, 128 * j)
                        qhi = min(512 * (g + 1), 128 * j + 640)
                        N = qhi - qlo
                        sb_ = 4 + j % 2
                        mm_acc(C, C.ps[0:128, 512 * sb_:512 * sb_ + N],
                               [((), K[0:65, 128 * j:128 * (j + 1)], Q[0:65, qlo:qhi])],
                               inc=mm, waits=[(ev, qaugv), (vx, p1vx),
                                              C.fwait(("st", sb_)), C.fwait("ob")])
                        mst = kb.now(mm)
                        exp_w = [(mm, mst), ptfree.get(j % 2)]
                        nvx = 0
                        if qlo == 128 * j:
                            kb.op("vector", lambda E, sb_=sb_: E.tensor_tensor(
                                C.ps[0:128, 512 * sb_:512 * sb_ + 128],
                                C.ps[0:128, 512 * sb_:512 * sb_ + 128], maskT[:, :], ALU.add),
                                waits=[(mm, mst)], inc=vx)
                            nvx = 1
                        if qhi == 128 * j + 640:
                            kb.op("vector", lambda E, sb_=sb_, N=N: E.tensor_tensor(
                                C.ps[0:128, 512 * sb_ + N - 128:512 * sb_ + N],
                                C.ps[0:128, 512 * sb_ + N - 128:512 * sb_ + N], mask[:, :],
                                ALU.add), waits=[(mm, mst)], inc=vx)
                            nvx = 1
                        if nvx:
                            exp_w.append((vx, kb.now(vx)))
                        pts = 512 * (j % 2)
                        kb.op("scalar", lambda E, sb_=sb_, pts=pts, N=N: E.activation(
                            PT[:, pts:pts + N], C.ps[0:128, 512 * sb_:512 * sb_ + N], AF.Exp),
                            waits=exp_w, inc=pxp)
                        C.setfree(("st", sb_), pxp, kb.now(pxp))
                        co = qlo - 512 * g
                        mm_acc(C, C.ps[0:65, 512 * 7 + co:512 * 7 + co + N],
                               [([(pxp, kb.now(pxp))], V[:, 65 * j:65 * j + 65],
                                 PT[:, pts:pts + N])],
                               inc=mm, start=(j == jlo), stop=(j == 4 * g + 3))
                        ptfree[j % 2] = (mm, kb.now(mm))
                    pvd = kb.now(mm)
                    kb.op("vector", lambda E: E.reciprocal(RL[0:1, :], C.ps[64:65, 512 * 7:512 * 8]),
                          waits=[(mm, pvd), C.fwait("rl")], inc=vx)
                    mm_acc(C, C.ps[0:64, 512 * 6:512 * 6 + 512],
                           [([(vx, kb.now(vx))], ones64[0:1, :], RL[0:1, :])], inc=mm,
                           waits=[C.fwait("b6")])
                    kb.op("scalar", lambda E: E.copy(RB[:, :], C.ps[0:64, 512 * 6:512 * 6 + 512]),
                          waits=[(mm, kb.now(mm)), C.fwait("rb")], inc=ev)
                    C.setfree("b6", ev, kb.now(ev))
                    C.setfree("rl", ev, kb.now(ev))
                    kb.op("vector", lambda E, g=g, hh=hh, p=p: E.tensor_tensor(
                        OF[p][64 * hh:64 * hh + 64, 512 * g:512 * (g + 1)],
                        C.ps[0:64, 512 * 7:512 * 8], RB[:, :], ALU.mult),
                        waits=[(ev, kb.now(ev))], inc=ofs)
                    C.setfree("ob", ofs, kb.now(ofs))
                    C.setfree("rb", ofs, kb.now(ofs))
            C.setfree("qa", ofs, kb.now(ofs))
        atdone = kb.now(ofs)

        for c in range(4):
            kb.dma("sync", WS[:, 1024 * c:1024 * (c + 1)], Wod[l, :, 1024 * c:1024 * (c + 1)],
                   wl, [C.fwait("ws")] if c == 0 else [])
        wov = kb.now(wl)
        for dcc in range(NDT):
            for tc in range(T // 512):
                b = (8 * dcc + tc) % 2
                xv = kb.dma("sync", XIF[:, :], xi_hbm[dcc, :, 512 * tc:512 * (tc + 1)], ld,
                            [C.fwait("xif")])
                mm_acc(C, C.bank(b),
                       [((), WS[:, 1024 * kc + 128 * dcc:1024 * kc + 128 * dcc + 128],
                         OF[kc][:, 512 * tc:512 * (tc + 1)]) for kc in range(4)],
                       inc=mm, waits=[(wl, wov), (ofs, atdone), C.fwait(("bk", b))])
                mv = kb.now(mm)
                kb.op("vector", lambda E, b=b: E.tensor_tensor(XIF[:, :], XIF[:, :],
                                                               C.bank(b), ALU.add),
                      waits=[(mm, mv), (ld, xv)], inc=vx)
                C.setfree(("bk", b), vx, kb.now(vx))
                kb.dma("gpsimd", xi_hbm[dcc, :, 512 * tc:512 * (tc + 1)], XIF[:, :], st,
                       [(vx, kb.now(vx))])
                C.setfree("xif", st, kb.now(st))
        wodone = kb.now(st)
        C.setfree("ws", mm, kb.now(mm))

        final = (l == nlayers - 1)
        kb.dma("sync", B1C[:, :], B1d[l, :, :], wl, [(st, wodone)])
        kb.dma("sync", B2C[:, :], B2d[l, :, :], wl)
        bcv = kb.now(wl)
        for tc in range(T // 512):
            gl = f"{tag}gl"
            xw = [(st, wodone), C.fwait("xib2")]
            for c in range(NDT):
                if c < 4:
                    kb.dma("gpsimd", LNH[:, 512 * c:512 * c + 512],
                           xi_hbm[c, :, 512 * tc:512 * (tc + 1)], gl, xw if c == 0 else [])
                elif tc > 0:
                    kb.dma("gpsimd", LNH[:, 512 * c:512 * c + 512],
                           xi_hbm[c, :, 512 * tc - 1:512 * (tc + 1) - 1], gl)
                else:
                    kb.op("scalar", lambda E, c=c: E.copy(
                        LNH[:, 512 * c:512 * c + 1], ZCOL[:, :]),
                        waits=[(ld, ldv), C.fwait("sqb")], inc=ev)
                    kb.dma("gpsimd", LNH[:, 512 * c + 1:512 * c + 512],
                           xi_hbm[c, :, 0:511], gl)
            xlv = kb.now(gl)
            if tc == 0:
                kb.wait("scalar", [(ev, kb.now(ev))])
                kb.wait("tensor", [(ev, kb.now(ev))])
            kb.op("scalar", lambda E: E.activation(SQB[:, :], LNH[:, :], AF.Square),
                  waits=[(f"{tag}gl", xlv), C.fwait("sqb")], inc=ev)
            sqv = kb.now(ev)
            mm_acc(C, C.ps[0:1, 512 * 2:512 * 2 + 512],
                   [([(f"{tag}gl", xlv)], ONESCB[:, :], LNH[:, 512 * c:512 * (c + 1)])
                    for c in range(NDT)],
                   inc=mm, waits=[C.fwait(("bk", 2))])
            mm_acc(C, C.ps[0:1, 512 * 3:512 * 3 + 512],
                   [([(ev, sqv)], ONESCB[:, :], SQB[:, 512 * c:512 * (c + 1)])
                    for c in range(NDT)],
                   inc=mm, waits=[C.fwait(("bk", 3))])
            smv = kb.now(mm)
            C.setfree("sqb", mm, smv)
            kb.op("vector", lambda E: E.tensor_scalar(ST0[0:1, 0:512],
                                                      C.ps[0:1, 512 * 2:512 * 2 + 512],
                                                      1.0 / D, None, ALU.mult),
                  waits=[(mm, smv)], inc=vx)
            kb.op("vector", lambda E: E.tensor_scalar(ST1[0:1, 0:512],
                                                      C.ps[0:1, 512 * 3:512 * 3 + 512],
                                                      1.0 / D, None, ALU.mult), inc=vx)
            C.setfree(("bk", 2), vx, kb.now(vx))
            C.setfree(("bk", 3), vx, kb.now(vx))
            kb.op("vector", lambda E: E.tensor_tensor(ST2[0:1, 0:512], ST0[0:1, 0:512],
                                                      ST0[0:1, 0:512], ALU.mult), inc=vx)
            kb.wait("vector", [(vx, kb.now(vx))])
            kb.op("vector", lambda E: E.tensor_tensor(ST1[0:1, 0:512], ST1[0:1, 0:512],
                                                      ST2[0:1, 0:512], ALU.subtract), inc=vx)
            epsap, epsw = C.fconst(1e-5)
            kb.op("scalar", lambda E, epsap=epsap: E.activation(
                ST2[0:1, 0:512], ST1[0:1, 0:512], AF.Sqrt, bias=epsap[0:1, :]),
                waits=[(vx, kb.now(vx)), epsw], inc=ev)
            kb.op("vector", lambda E: E.reciprocal(ST1[0:1, 0:512], ST2[0:1, 0:512]),
                  waits=[(ev, kb.now(ev))], inc=vx)
            kb.wait("vector", [(vx, kb.now(vx))])
            kb.op("vector", lambda E: E.scalar_tensor_tensor(
                ST0[0:1, 0:512], ST0[0:1, 0:512], -1.0, ST1[0:1, 0:512],
                ALU.mult, ALU.mult), inc=vx)
            stv = kb.now(vx)
            mm_acc(C, C.ps[0:128, 512 * 2:512 * 2 + 512],
                   [([(vx, stv)], ONESR2[0:1, :], ST1[0:1, 0:512])], inc=mm,
                   waits=[C.fwait(("bk", 2))])
            mm_acc(C, C.ps[0:128, 512 * 3:512 * 3 + 512],
                   [((), ONESR2[0:1, :], ST0[0:1, 0:512])], inc=mm,
                   waits=[C.fwait(("bk", 3))])
            bmv = kb.now(mm)
            kb.op("scalar", lambda E: E.copy(RSB0[:, 0:512], C.ps[0:128, 512 * 2:512 * 2 + 512]),
                  waits=[(mm, bmv), C.fwait("rsb2")], inc=ev)
            kb.op("scalar", lambda E: E.copy(RSB1[:, 0:512], C.ps[0:128, 512 * 3:512 * 3 + 512]),
                  inc=ev)
            rbv = kb.now(ev)
            C.setfree(("bk", 2), ev, rbv)
            C.setfree(("bk", 3), ev, rbv)
            for c in range(NDT):
                kb.op("vector", lambda E, c=c: E.tensor_tensor(
                    SQB[:, 512 * c:512 * (c + 1)], LNH[:, 512 * c:512 * (c + 1)],
                    RSB0[:, 0:512], ALU.mult), waits=[(ev, rbv)], inc=vx)
                kb.op("vector", lambda E, c=c: E.tensor_tensor(
                    SQB[:, 512 * c:512 * (c + 1)], SQB[:, 512 * c:512 * (c + 1)],
                    RSB1[:, 0:512], ALU.add), inc=vx)
                kb.op("vector", lambda E, c=c, l=l: E.tensor_scalar(
                    LNH[:, 512 * c:512 * (c + 1)], SQB[:, 512 * c:512 * (c + 1)],
                    LNGB[:, 16 * l + c:16 * l + c + 1], LNGB[:, 16 * l + 8 + c:16 * l + 8 + c + 1],
                    ALU.mult, ALU.add), waits=[(wl, bcv), (ld, ldv)], inc=vx)
            lnv = kb.now(vx)
            C.setfree("sqb", vx, lnv)

            for j in range(22 if "b2" in phases or phases == "ab" else 0):
                parj = j % 2
                (a0, asz) = W1M[j]
                (g0, gsz) = W1M[22 + j]
                wj = f"{tag}w1"
                kb.dma("sync", WS[:, 2048 * parj:2048 * parj + 1024],
                       W1d[l, j, :, :], wj, [C.fwait(("w1s", parj)), C.fwait("ws")])
                kb.dma("sync", WS[:, 2048 * parj + 1024:2048 * parj + 2048],
                       W1d[l, 22 + j, :, :], wj)
                wjv = kb.now(wj)
                for (which, m0, msz, bcol) in (("a", a0, asz, j), ("g", g0, gsz, 22 + j)):
                    woff = 2048 * parj + (0 if which == "a" else 1024)
                    bnk = 4 if which == "a" else 5
                    mm_acc(C, C.ps[0:msz, 512 * bnk:512 * bnk + 512],
                           [((), WS[:, woff + 128 * c:woff + 128 * c + msz],
                             LNH[:, 512 * c:512 * (c + 1)]) for c in range(NDT)],
                           inc=mm, waits=[(wj, wjv), (vx, lnv), C.fwait(("bk", bnk))])
                    mv = kb.now(mm)
                    if which == "a":
                        kb.op("vector", lambda E, msz=msz, bcol=bcol, bnk=bnk: E.tensor_scalar(
                            AGT[0:msz, 0:512], C.ps[0:msz, 512 * bnk:512 * bnk + 512], 1.0,
                            B1C[0:msz, bcol:bcol + 1], ALU.mult, ALU.add),
                            waits=[(mm, mv), (wl, bcv), C.fwait("agta")], inc=vx)
                        C.setfree(("bk", bnk), vx, kb.now(vx))
                    else:
                        kb.op("scalar", lambda E, msz=msz, bcol=bcol, bnk=bnk: E.activation(
                            AGT[0:msz, 512:1024], C.ps[0:msz, 512 * bnk:512 * bnk + 512],
                            AF.Gelu, bias=B1C[0:msz, bcol:bcol + 1]),
                            waits=[(mm, mv), (wl, bcv), C.fwait("agtg")], inc=ev)
                        C.setfree(("bk", bnk), ev, kb.now(ev))
                C.setfree(("w1s", parj), mm, kb.now(mm))
                kb.op("vector", lambda E, j=j, asz=asz: E.tensor_tensor(
                    h4sl(j, slice(0, asz)), AGT[0:asz, 0:512], AGT[0:asz, 512:1024],
                    ALU.mult), waits=[(ev, kb.now(ev)), (vx, kb.now(vx)), C.fwait(("h4", j))],
                    inc=vx)
                C.setfree("agta", vx, kb.now(vx))
                C.setfree("agtg", vx, kb.now(vx))
            h4v = kb.now(vx)

            for dcc in range(NDT if "b2" in phases or phases == "ab" else 0):
                w2j = f"{tag}w2"
                kb.dma("sync", W2SA[0:128, 0:1920], W2d[l, dcc, :, 0:1920], w2j,
                       [C.fwait("w2s"), (st, xiv)])
                kb.dma("sync", W2SB[0:128, 0:896], W2d[l, dcc, :, 1920:2816], w2j)
                w2v = kb.now(w2j)
                if not final:
                    xv = kb.dma("sync", XIF[:, :], xi_hbm[dcc, :, 512 * tc:512 * (tc + 1)], ld,
                                [C.fwait("xif")])
                mm_acc(C, C.ps[0:128, 0:512],
                       [((), w2sl(kk, slice(0, W2K[kk][1])),
                         h4sl(kk, slice(0, W2K[kk][1])))
                        for kk in range(22)],
                       inc=mm, waits=[(w2j, w2v), (vx, h4v), C.fwait(("bk", 0))])
                mv = kb.now(mm)
                C.setfree("w2s", mm, mv)
                if not final:
                    kb.op("vector", lambda E, dcc=dcc: E.tensor_scalar(
                        QR[:, :], C.ps[0:128, 0:512], 1.0, B2C[:, dcc:dcc + 1],
                        ALU.mult, ALU.add),
                        waits=[(mm, mv), (wl, bcv), C.fwait("qr")], inc=vx)
                    C.setfree(("bk", 0), vx, kb.now(vx))
                    kb.op("vector", lambda E: E.tensor_tensor(XIF[:, :], XIF[:, :], QR[:, :],
                                                              ALU.add),
                          waits=[(ld, xv)], inc=vx)
                    C.setfree("qr", vx, kb.now(vx))
                    kb.dma("gpsimd", xi_hbm[dcc, :, 512 * tc:512 * (tc + 1)], XIF[:, :], st,
                           [(vx, kb.now(vx))])
                    C.setfree("xif", st, kb.now(st))
                else:
                    kb.op("vector", lambda E, dcc=dcc: E.tensor_scalar(
                        XIF[:, :], C.ps[0:128, 0:512], 1.0, B2C[:, dcc:dcc + 1],
                        ALU.mult, ALU.add), waits=[(mm, mv), (wl, bcv), C.fwait("xif")], inc=vx)
                    xv2 = kb.dma("sync", QR[:, :], xi_hbm[dcc, :, 512 * tc:512 * (tc + 1)],
                                 ld, [C.fwait("qr"), (vx, h4v)])
                    kb.op("vector", lambda E: E.tensor_tensor(XIF[:, :], XIF[:, :], QR[:, :],
                                                              ALU.add),
                          waits=[(ld, xv2), (vx, kb.now(vx))], inc=vx)
                    xfv = kb.now(vx)
                    C.setfree(("bk", 0), vx, xfv)
                    C.setfree("qr", vx, xfv)
                    for tt in range(4):
                        kb.op("tensor", lambda E, tt=tt: E.transpose(
                            C.ps[0:128, 512 * 5:512 * 5 + 128], XIF[:, 128 * tt:128 * (tt + 1)],
                            IDF[:, :]),
                            waits=[(vx, xfv), C.fwait(("bk", 5)), (ld, ldv)], inc=mm)
                        kb.op("vector", lambda E, tt=tt, dcc=dcc: E.tensor_copy(
                            OUTT[:, 128 * dcc:128 * (dcc + 1)],
                            C.ps[0:128, 512 * 5:512 * 5 + 128]),
                            waits=[(mm, kb.now(mm)), C.fwait(("outt", dcc))], inc=vx)
                        C.setfree(("bk", 5), vx, kb.now(vx))
                        sink(4 * tc + tt, dcc, OUTT[:, 128 * dcc:128 * (dcc + 1)],
                             [(vx, kb.now(vx))])
                        C.setfree(("outt", dcc), dc_, kb.now(dc_))
                    C.setfree("xif", mm, kb.now(mm))
            for jj in range(22):
                C.setfree(("h4", jj), mm, kb.now(mm))
            C.setfree("xib2", vx, kb.now(vx))
        C.setfree(("tbl", 1 - l), vx, kb.now(vx))
        C.setfree("ws", mm, kb.now(mm))
        C.setfree("h4of", mm, kb.now(mm))
        for jj in range(22):
            C.setfree(("h4", jj), mm, kb.now(mm))
        xi_ready = kb.now(st)
    return dc_, kb.now(dc_)


def rotary_tables(Tn=T):
    DIM_HEAD, WINDOW = 64, 512
    half_idx = np.arange(0, DIM_HEAD, 2, dtype=np.float32)
    inv_freq = 1.0 / 10000.0 ** (half_idx / DIM_HEAD)
    t = np.arange(Tn, dtype=np.float32)
    freqs = t[:, None] * inv_freq[None, :]
    freqs = np.concatenate([freqs, freqs], axis=-1).astype(np.float32)
    base = ((half_idx + 0.4 * DIM_HEAD) / (1.4 * DIM_HEAD)).astype(np.float32)
    power = ((t - Tn // 2) / float(WINDOW // 2)).astype(np.float32)
    scale = base[None, :] ** power[:, None]
    scale = np.concatenate([scale, scale], axis=-1).astype(np.float32)
    cos, sin = np.cos(freqs), np.sin(freqs)
    CQ = cos * scale * 8.0
    SQ = sin * scale * 8.0
    CK = cos / scale
    SK = sin / scale
    def tl(tab):
        return as_bf16(tab.reshape(Tn // 128, 128, 64).transpose(1, 0, 2).reshape(128, -1))
    return tl(CQ), tl(SQ), tl(CK), tl(SK)


def prep_ffn_inputs(I, pref, pr):
    wqkv = I[pref + "wqkv"].astype(np.float32)
    wq, wk, wv = wqkv[:, :, 0:512], wqkv[:, :, 512:1024], wqkv[:, :, 1024:1536]
    wva = np.zeros((2, 1024, 520), np.float32)
    for h in range(8):
        wva[:, :, 65 * h:65 * h + 64] = wv[:, :, 64 * h:64 * h + 64]

    def cm(w):
        fw = w.shape[2]
        return as_bf16(w.reshape(2, 8, 128, fw).transpose(0, 2, 1, 3)
                       .reshape(2, 128, 8 * fw))
    wo = I[pref + "wo"].astype(np.float32)
    ffwo = np.ascontiguousarray(wo.reshape(2, 4, 128, 1024).transpose(0, 2, 1, 3)
                                .reshape(2, 128, 4096))
    w1 = I[pref + "w1"].astype(np.float32)
    ffw1 = np.zeros((2, 44, 128, 1024), np.float32)
    for m, (c0, sz) in enumerate(W1M):
        blk = w1[:, :, c0:c0 + sz]
        tmp = blk.reshape(2, 8, 128, sz).transpose(0, 2, 1, 3)
        for c in range(8):
            ffw1[:, m, :, 128 * c:128 * c + sz] = tmp[:, :, c, :]
    b1 = I[pref + "b1"].astype(np.float32)
    ffb1 = np.zeros((2, 128, 44), np.float32)
    for m, (c0, sz) in enumerate(W1M):
        ffb1[:, 0:sz, m] = b1[:, c0:c0 + sz]
    w2 = I[pref + "w2"].astype(np.float32)
    ffw2 = np.zeros((2, 8, 128, 2816), np.float32)
    for kk, (k0, szk) in enumerate(W2K):
        blk = w2[:, k0:k0 + szk, :]
        for dcc in range(8):
            ffw2[:, dcc, 0:szk, 128 * kk:128 * (kk + 1)] = blk[:, :, 128 * dcc:128 * (dcc + 1)]
    b2 = I[pref + "b2"].astype(np.float32)
    ffb2 = np.ascontiguousarray(b2.reshape(2, 8, 128).transpose(0, 2, 1))
    lng = np.ascontiguousarray(I[pref + "ln_g"].astype(np.float32).reshape(2, 8, 128)
                               .transpose(0, 2, 1))
    lnb = np.ascontiguousarray(I[pref + "ln_b"].astype(np.float32).reshape(2, 8, 128)
                               .transpose(0, 2, 1))
    qs = I[pref + "q_scale"].astype(np.float32)
    ks = I[pref + "k_scale"].astype(np.float32)
    CQ, SQ, CK, SK = rotary_tables(T)
    return {
        pr + "wq": cm(wq), pr + "wk": cm(wk), pr + "wv": cm(wva),
        pr + "wo": as_bf16(ffwo),
        pr + "w1": as_bf16(ffw1), pr + "b1": as_f32(ffb1),
        pr + "w2": as_bf16(ffw2), pr + "b2": as_f32(ffb2),
        pr + "lng": as_f32(lng), pr + "lnb": as_f32(lnb),
        pr + "qs": as_f32(np.tile(qs[:, None, :], (1, 128, 1))),
        pr + "qsr": as_f32(np.tile(np.roll(qs, -32, axis=-1)[:, None, :], (1, 128, 1))),
        pr + "ks": as_f32(np.tile(ks[:, None, :], (1, 128, 1))),
        pr + "ksr": as_f32(np.tile(np.roll(ks, -32, axis=-1)[:, None, :], (1, 128, 1))),
        pr + "cq": CQ, pr + "sq": SQ, pr + "ck": CK, pr + "sk": SK,
    }


def ffn_drams(nc, pr):
    d = {}
    d["Wqd"] = nc.dram_tensor(pr + "wq", [2, 128, 4096], BF16, kind="ExternalInput").ap()
    d["Wkd"] = nc.dram_tensor(pr + "wk", [2, 128, 4096], BF16, kind="ExternalInput").ap()
    d["Wvd"] = nc.dram_tensor(pr + "wv", [2, 128, 4160], BF16, kind="ExternalInput").ap()
    d["Wod"] = nc.dram_tensor(pr + "wo", [2, 128, 4096], BF16, kind="ExternalInput").ap()
    d["W1d"] = nc.dram_tensor(pr + "w1", [2, 44, 128, 1024], BF16, kind="ExternalInput").ap()
    d["B1d"] = nc.dram_tensor(pr + "b1", [2, 128, 44], F32, kind="ExternalInput").ap()
    d["W2d"] = nc.dram_tensor(pr + "w2", [2, 8, 128, 2816], BF16, kind="ExternalInput").ap()
    d["B2d"] = nc.dram_tensor(pr + "b2", [2, 128, 8], F32, kind="ExternalInput").ap()
    d["LNGd"] = nc.dram_tensor(pr + "lng", [2, 128, 8], F32, kind="ExternalInput").ap()
    d["LNBd"] = nc.dram_tensor(pr + "lnb", [2, 128, 8], F32, kind="ExternalInput").ap()
    d["QSd"] = nc.dram_tensor(pr + "qs", [2, 128, 64], F32, kind="ExternalInput").ap()
    d["QSRd"] = nc.dram_tensor(pr + "qsr", [2, 128, 64], F32, kind="ExternalInput").ap()
    d["KSd"] = nc.dram_tensor(pr + "ks", [2, 128, 64], F32, kind="ExternalInput").ap()
    d["KSRd"] = nc.dram_tensor(pr + "ksr", [2, 128, 64], F32, kind="ExternalInput").ap()
    d["CQd"] = nc.dram_tensor(pr + "cq", [128, T // 2], BF16, kind="ExternalInput").ap()
    d["SQd"] = nc.dram_tensor(pr + "sq", [128, T // 2], BF16, kind="ExternalInput").ap()
    d["CKd"] = nc.dram_tensor(pr + "ck", [128, T // 2], BF16, kind="ExternalInput").ap()
    d["SKd"] = nc.dram_tensor(pr + "sk", [128, T // 2], BF16, kind="ExternalInput").ap()
    return d


def build_ffn_standalone(env=None, pr="f0_"):
    nc, kb, stack, pref, own = _env_open(env)
    if env is None:
        xn_hbm = nc.dram_tensor("xn", [T, D], BF16, kind="ExternalInput").ap()
        out = nc.dram_tensor("out", [T, D], F32, kind="ExternalOutput").ap()
    else:
        xn_hbm, out = env["xn"], env["out"]
    xi_hbm = nc.dram_tensor(pref + "xi_s", [8, 128, T], F32, kind="Internal").ap()
    qk_hbm = nc.dram_tensor(pref + "qk_s", [T, 1024], BF16, kind="Internal").ap()
    dr = ffn_drams(nc, pr)
    P = host_consts_common()
    C = Ctx(nc, kb, stack, pref)
    for c in range(NDT):
        C.alloc(f"XN{c}", [128, T], BF16)
    alloc_mha(C, qkvw=False)
    alloc_ffn(C)
    C.alloc_ps()
    cs = emit_mha_consts(C, P)
    xr = emit_xn_from_hbm(C, xn_hbm, "x0")

    def sink(ti, dcc, ap, waits):
        kb.dma("gpsimd", out[128 * ti:128 * (ti + 1), 128 * dcc:128 * (dcc + 1)],
               ap, "fdc", waits)

    emit_ffn(C, "f", dr["Wqd"], dr["Wkd"], dr["Wvd"], dr["Wod"], dr["W1d"], dr["B1d"],
             dr["W2d"], dr["B2d"], dr["LNGd"], dr["LNBd"], dr["QSd"], dr["QSRd"],
             dr["KSd"], dr["KSRd"], dr["CQd"], dr["SQd"], dr["CKd"], dr["SKd"],
             xi_hbm, qk_hbm, xr, cs, sink)
    kb.wait("gpsimd", [("fdc", kb.now("fdc"))])
    if own:
        kb.finalize(nc, stack)
    return nc, C, stack


def alloc_l1(C):
    C.alloc("CWV", [128, 2048], BF16)
    C.alloc("CKA", [65, 512], BF16)
    C.alloc("CVA", [128, 130], BF16)
    C.alloc("CST", [128, 288], F32)
    C.alloc("COMPST", [64, 512], BF16)
    NBv = (T - 32) // 16 + 1
    C.alloc("CM0", [128, NBv], BF16)
    C.alloc("CM1", [128, NBv], BF16)
    C.alloc("CMT0", [128, 256], BF16)
    C.alloc("CMT1", [128, 256], BF16)
    C.alloc("IDF3", [128, 128], F32)
    C.alloc("ONE64", [64, 1], F32)
    C.alloc("M2R", [1, 256], F32)
    C.alloc("CSQ", [64, 256], F32)
    C.alloc("BQK2", [128, 16], F32)
    C.alloc("CGL", [64, 256], F32)


def emit_l1_comp(C, tag, CWd, CBd, CLNd, comp_hbm, conv_ready):
    kb = C.kb
    XN = [C.sb[f"XN{c}"] for c in range(NDT)]
    QA, QB, KA, KB_ = C.sb["QA"], C.sb["QB"], C.sb["KA"], C.sb["KB_"]
    QW, KW = C.sb["QW"], C.sb["KW"]
    CWV, CKA, CVA, CST = C.sb["CWV"], C.sb["CKA"], C.sb["CVA"], C.sb["CST"]
    COMPST = C.sb["COMPST"]
    PT, MBc = C.sb["PT"], C.sb["MBc"]
    RL, RB = C.sb["RL"], C.sb["RB"]
    negI, ones64 = C.sb["negI"], C.sb["ones1x64"]
    CM0, CM1, CMT0, CMT1 = C.sb["CM0"], C.sb["CM1"], C.sb["CMT0"], C.sb["CMT1"]
    IDF3, ONE64 = C.sb["IDF3"], C.sb["ONE64"]
    mm, ev, vx, pxp = f"{tag}cm", f"{tag}ce", f"{tag}cv", f"{tag}cp"
    wld, st = f"{tag}cw", f"{tag}cs"
    NB = (T - 32) // 16 + 1
    NTK = (NB + 127) // 128

    def hook(p):
        wv = [conv_ready]
        for (dstb, wsrc, woff) in ((KA, KW, 1024), (KB_, KW, 2048)):
            for c in range(NDT):
                wvc = []
                if c == 0 and dstb is KA:
                    wvc = [(f"{tag}of", kb.now(f"{tag}of"))]
                elif c == 0:
                    wvc = [(mm, kb.now(mm))]
                kb.dma("sync", wsrc[:, 128 * c:128 * (c + 1)],
                       C.l1_cattn[128 * c:128 * (c + 1), woff + 128 * p:woff + 128 * (p + 1)],
                       wld, wvc)
            wlv = kb.now(wld)
            for ch in range(T // 512):
                sl = slice(512 * ch, 512 * (ch + 1))
                b = ch % 2
                mm_acc(C, C.bank(b),
                       [((), wsrc[:, 128 * c:128 * (c + 1)], XN[c][:, sl]) for c in range(NDT)],
                       inc=mm, waits=[(wld, wlv), C.fwait(("ck", b))])
                mv = kb.now(mm)
                kb.op("vector", lambda E, sl=sl, b=b, woff=woff, p=p, dstb=dstb: E.tensor_scalar(
                    dstb[0:128, sl], C.ps[0:128, 512 * b:512 * b + 512], 1.0,
                    C.sb["BQK2"][:, (woff // 1024 - 1) * 8 + p:(woff // 1024 - 1) * 8 + p + 1],
                    ALU.mult, ALU.add), waits=[(mm, mv)], inc=vx)
                C.setfree(("ck", b), vx, kb.now(vx))
        kvv = kb.now(vx)
        for hh in range(2):
            h = 2 * p + hh
            Q = (QA, QB)[hh]
            KH = KA
            VH_ = KB_
            pb = 64 * hh
            for (isv, srcb) in ((0, KH), (1, VH_)):
                t0 = srcb[pb:pb + 64, 0:1]
                for tau in range(32):
                    src = bass.AP(t0.tensor, t0.offset + tau, [[t0.ap[0][0], 64], [16, NB]])
                    kb.op("tensor", lambda E, tau=tau, src=src, pb=pb: E.matmul(
                        C.ps[0:64, 512 * 4:512 * 4 + NB], CWV[pb:pb + 64, 64 * tau:64 * (tau + 1)],
                        src, start=(tau == 0), stop=(tau == 31)),
                        waits=([(vx, kvv), C.fwait(("ck", 4))] if tau == 0 else ()),
                        inc=mm if tau == 31 else None)
                cmv = kb.now(mm)
                kb.op("vector", lambda E: E.tensor_scalar(
                    CST[0:64, 0:NB], C.ps[0:64, 512 * 4:512 * 4 + NB], 1.0,
                    C.sb["CBC"][0:64, 0:1], ALU.mult, ALU.add),
                    waits=[(mm, cmv), conv_ready, (ev, kb.now(ev))], inc=vx)
                C.setfree(("ck", 4), vx, kb.now(vx))
                kb.op("scalar", lambda E: E.activation(C.sb["CSQ"][0:64, 0:NB],
                                                       CST[0:64, 0:NB], AF.Square),
                      waits=[(vx, kb.now(vx))], inc=ev)
                mm_acc(C, C.ps[0:1, 512 * 5:512 * 5 + NB],
                       [([(vx, kb.now(vx))], ONE64[0:64, 0:1], CST[0:64, 0:NB])],
                       inc=mm, waits=[C.fwait(("ck", 5))])
                mm_acc(C, C.ps[0:1, 512 * 5 + 256:512 * 5 + 256 + NB],
                       [([(ev, kb.now(ev))], ONE64[0:64, 0:1], C.sb["CSQ"][0:64, 0:NB])],
                       inc=mm)
                smv = kb.now(mm)
                kb.op("vector", lambda E: E.tensor_scalar(RL[0:1, 0:NB],
                                                          C.ps[0:1, 512 * 5:512 * 5 + NB],
                                                          1.0 / 64, None, ALU.mult),
                      waits=[(mm, smv), C.fwait("rl")], inc=vx)
                kb.op("vector", lambda E: E.tensor_scalar(CST[0:1, 287:287 + NB]
                                                          if False else RL[0:1, 256:256 + NB],
                                                          C.ps[0:1, 512 * 5 + 256:512 * 5 + 256 + NB],
                                                          1.0 / 64, None, ALU.mult), inc=vx)
                C.setfree(("ck", 5), vx, kb.now(vx))
                kb.wait("vector", [(vx, kb.now(vx))])
                kb.op("vector", lambda E: E.tensor_tensor(C.sb["M2R"][0:1, 0:NB], RL[0:1, 0:NB],
                                                          RL[0:1, 0:NB], ALU.mult), inc=vx)
                kb.wait("vector", [(vx, kb.now(vx))])
                kb.op("vector", lambda E: E.tensor_tensor(RL[0:1, 256:256 + NB],
                                                          RL[0:1, 256:256 + NB],
                                                          C.sb["M2R"][0:1, 0:NB], ALU.subtract), inc=vx)
                epsap, epsw = C.fconst(1e-5)
                kb.op("scalar", lambda E, epsap=epsap: E.activation(
                    RL[0:1, 256:256 + NB], RL[0:1, 256:256 + NB], AF.Sqrt, bias=epsap[0:1, :]),
                    waits=[(vx, kb.now(vx)), epsw], inc=ev)
                kb.wait("vector", [(ev, kb.now(ev))])
                kb.op("vector", lambda E: E.reciprocal(RL[0:1, 256:256 + NB],
                                                       RL[0:1, 256:256 + NB]), inc=vx)
                kb.wait("vector", [(vx, kb.now(vx))])
                kb.op("vector", lambda E: E.scalar_tensor_tensor(
                    RL[0:1, 0:NB], RL[0:1, 0:NB], -1.0, RL[0:1, 256:256 + NB],
                    ALU.mult, ALU.mult), inc=vx)
                stv = kb.now(vx)
                mm_acc(C, C.ps[0:64, 512 * 5:512 * 5 + NB],
                       [([(vx, stv)], ones64[0:1, :], RL[0:1, 256:256 + NB])], inc=mm,
                       waits=[C.fwait(("ck", 5))])
                mm_acc(C, C.ps[0:64, 512 * 5 + 256:512 * 5 + 256 + NB],
                       [((), ones64[0:1, :], RL[0:1, 0:NB])], inc=mm)
                bcv = kb.now(mm)
                kb.op("vector", lambda E: E.tensor_tensor(CST[0:64, 0:NB], CST[0:64, 0:NB],
                                                          C.ps[0:64, 512 * 5:512 * 5 + NB],
                                                          ALU.mult),
                      waits=[(mm, bcv)], inc=vx)
                kb.wait("vector", [(vx, kb.now(vx))])
                kb.op("vector", lambda E: E.tensor_tensor(CST[0:64, 0:NB], CST[0:64, 0:NB],
                                                          C.ps[0:64, 512 * 5 + 256:512 * 5 + 256 + NB],
                                                          ALU.add), inc=vx)
                C.setfree(("ck", 5), vx, kb.now(vx))
                kb.wait("vector", [(vx, kb.now(vx))])
                kb.op("vector", lambda E: E.tensor_scalar(
                    CST[0:64, 0:NB], CST[0:64, 0:NB], C.sb["CLW"][0:64, 0:1],
                    C.sb["CLW"][0:64, 1:2], ALU.mult, ALU.add),
                    waits=[conv_ready], inc=vx)
                kb.wait("scalar", [(vx, kb.now(vx))])
                if isv == 0:
                    kb.op("scalar", lambda E, hh=hh: E.activation(
                        CKA[0:64, 256 * hh:256 * hh + NB], CST[0:64, 0:NB], AF.Gelu),
                        waits=[C.fwait(("cka", hh))], inc=ev)
                    kb.op("scalar", lambda E, hh=hh: E.copy(
                        CKA[64:65, 256 * hh:256 * hh + 256],
                        C.sb["onerow_c"][0:1, 0:256]), inc=ev)
                else:
                    kb.op("scalar", lambda E: E.activation(
                        C.sb["CGL"][0:64, 0:NB], CST[0:64, 0:NB], AF.Gelu), inc=ev)
                    gv = kb.now(ev)
                    for tt2 in range(NTK):
                        ww = min(128, NB - 128 * tt2)
                        kb.op("tensor", lambda E, tt2=tt2, ww=ww: E.transpose(
                            C.ps[0:ww, 512 * 6:512 * 6 + 64],
                            C.sb["CGL"][0:64, 128 * tt2:128 * tt2 + ww],
                            IDF3[0:64, 0:64]),
                            waits=[(ev, gv), C.fwait(("ck", 6))], inc=mm)
                        kb.op("vector", lambda E, tt2=tt2, ww=ww: E.tensor_copy(
                            CVA[0:ww, 65 * tt2:65 * tt2 + 64], C.ps[0:ww, 512 * 6:512 * 6 + 64]),
                            waits=[(mm, kb.now(mm)), C.fwait("cva")], inc=vx)
                        C.setfree(("ck", 6), vx, kb.now(vx))
                    for tt2 in range(NTK):
                        ww2 = min(128, NB - 128 * tt2)
                        kb.op("scalar", lambda E, tt2=tt2, ww2=ww2: E.copy(
                            CVA[0:ww2, 65 * tt2 + 64:65 * tt2 + 65],
                            C.sb["onecol_c"][0:ww2, 0:1]),
                            waits=[(vx, kb.now(vx))] if tt2 == 0 else [], inc=ev)
            ckv, cvv = kb.now(ev), kb.now(ev)
            for q in range(NTT):
                fw2 = [(ev, ckv), C.fwait("p1b")]
                mm_acc(C, C.ps[0:128, 0:NB],
                       [((), Q[0:64, 128 * q:128 * (q + 1)], CKA[0:64, 256 * hh:256 * hh + NB])],
                       inc=mm, waits=fw2)
                ms = kb.now(mm)
                rw_ = [(mm, ms)]
                if q == 0 or (q == 1 and NB > 128):
                    msk = CM0 if q == 0 else CM1
                    kb.op("vector", lambda E, msk=msk: E.tensor_tensor(
                        C.ps[0:128, 0:NB], C.ps[0:128, 0:NB], msk[:, 0:NB], ALU.add),
                        waits=[(mm, ms), conv_ready], inc=vx)
                    rw_.append((vx, kb.now(vx)))
                kb.op("vector", lambda E, hh=hh: E.reduce_max(
                    MBc[:, hh:hh + 1], C.ps[0:128, 0:NB], axis=AX), waits=rw_, inc=vx)
                mm_acc(C, C.ps[0:1, 0:128],
                       [([(vx, kb.now(vx))], MBc[:, hh:hh + 1], negI[:, :])], inc=mm)
                kb.op("scalar", lambda E, q=q, Q=Q: E.copy(Q[64:65, 128 * q:128 * (q + 1)],
                                                           C.ps[0:1, 0:128]),
                      waits=[(mm, kb.now(mm))], inc=ev)
                C.setfree("p1b", ev, kb.now(ev))
            qaug2 = kb.now(ev)
            for g in range(T // 512):
                for j in range(NTK):
                    ww = min(128, NB - 128 * j)
                    sb_ = 4 + j
                    mm_acc(C, C.ps[0:ww, 512 * sb_:512 * sb_ + 512],
                           [((), CKA[0:65, 256 * hh + 128 * j:256 * hh + 128 * j + ww],
                             Q[0:65, 512 * g:512 * (g + 1)])],
                           inc=mm, waits=[(ev, qaug2), C.fwait(("ck", 4 + j)), C.fwait("ob2")])
                    mst = kb.now(mm)
                    exp_w = [(mm, mst)]
                    if g == 0:
                        mt = CMT0 if j == 0 else CMT1
                        kb.op("vector", lambda E, sb_=sb_, ww=ww, mt=mt: E.tensor_tensor(
                            C.ps[0:ww, 512 * sb_:512 * sb_ + 256],
                            C.ps[0:ww, 512 * sb_:512 * sb_ + 256], mt[0:ww, 0:256], ALU.add),
                            waits=[(mm, mst), conv_ready], inc=vx)
                        exp_w.append((vx, kb.now(vx)))
                    kb.op("scalar", lambda E, sb_=sb_, ww=ww, j=j: E.activation(
                        PT[0:ww, 512 * j:512 * j + 512], C.ps[0:ww, 512 * sb_:512 * sb_ + 512],
                        AF.Exp), waits=exp_w, inc=pxp)
                    C.setfree(("ck", 4 + j), pxp, kb.now(pxp))
                    mm_acc(C, C.ps[0:65, 512 * 7:512 * 8],
                           [([(pxp, kb.now(pxp)), (ev, cvv)], CVA[0:ww, 65 * j:65 * (j + 1)],
                             PT[0:ww, 512 * j:512 * j + 512])],
                           inc=mm, start=(j == 0), stop=(j == NTK - 1))
                pvd = kb.now(mm)
                kb.op("vector", lambda E: E.reciprocal(RL[0:1, 0:512], C.ps[64:65, 512 * 7:512 * 8]),
                      waits=[(mm, pvd), C.fwait("rl")], inc=vx)
                mm_acc(C, C.ps[0:64, 512 * 6:512 * 6 + 512],
                       [([(vx, kb.now(vx))], ones64[0:1, :], RL[0:1, 0:512])], inc=mm,
                       waits=[C.fwait(("ck", 6))])
                kb.op("scalar", lambda E: E.copy(RB[:, :], C.ps[0:64, 512 * 6:512 * 6 + 512]),
                      waits=[(mm, kb.now(mm)), C.fwait("rb2")], inc=ev)
                C.setfree(("ck", 6), ev, kb.now(ev))
                C.setfree("rl", ev, kb.now(ev))
                kb.op("vector", lambda E: E.tensor_tensor(
                    COMPST[0:64, 0:512], C.ps[0:64, 512 * 7:512 * 8], RB[:, :], ALU.mult),
                    waits=[(ev, kb.now(ev)), C.fwait("cmpst")], inc=vx)
                C.setfree("ob2", vx, kb.now(vx))
                C.setfree("rb2", vx, kb.now(vx))
                kb.dma("gpsimd", comp_hbm[h // 2, 64 * (h % 2):64 * (h % 2) + 64,
                                          512 * g:512 * (g + 1)],
                       COMPST[0:64, 0:512], st, [(vx, kb.now(vx))])
                C.setfree("cmpst", st, kb.now(st))
            C.setfree(("cka", hh), mm, kb.now(mm))
            C.setfree("cva", mm, kb.now(mm))
        return [(mm, kb.now(mm)), (ev, kb.now(ev)), (vx, kb.now(vx)),
                (pxp, kb.now(pxp)), (st, kb.now(st))]
    return hook, st


def alloc_l1_final(C):
    if T >= 4096:
        C.sb["CMB"] = TV(C.sb["XN1"], 0, None, 4096)
        C.sb["COMPL"] = TV(C.sb["XN2"], 0, None, 1024)
        C.sb["CSQ2"] = TV(C.sb["XN2"], 1024, None, 512)
        C.sb["GWS"] = TV(C.sb["XN0"], 0, None, 2048)
        C.sb["RSBG"] = TV(C.sb["XN3"], 0, F32, 1024)
    else:
        C.alloc("CMB", [128, 4096], BF16)
        C.alloc("COMPL", [128, 1024], BF16)
        C.alloc("CSQ2", [128, 512], BF16)
        C.alloc("GWS", [128, 2048], BF16)
        C.alloc("RSBG", [128, 1024], F32)
    C.alloc("GSR", [1, 1024], F32)
    C.alloc("GCOL", [128, 16], F32)
    C.alloc("HCOL", [128, 8], F32)
    C.alloc("HSQg", [128, 8], F32)
    C.alloc("HNg", [128, 8], BF16)
    C.alloc("GSB", [128, 16], BF16)
    C.alloc("LNLW", [128, 16], F32)
    C.alloc("GB", [128, 8], F32)
    C.alloc("GLW", [128, 16], F32)
    C.alloc("G2S", [128, 16], BF16)
    C.alloc("GB2", [2, 1], F32)
    C.alloc("I2B", [2, 2], BF16)
    C.alloc("ONESCB", [128, 1], BF16)
    C.alloc("ONESCF", [128, 1], F32)
    C.alloc("ONESR2", [1, 128], F32)
    C.alloc("M2R2", [1, 512], F32)


def emit_l1_final(C, tag, CPWd, CPBd, GW1d, GB1d, GLWd, GW2d, GB2d, LLWd,
                  comp_hbm, comp_ready, of_ready, sink):
    kb = C.kb
    XN = [C.sb[f"XN{c}"] for c in range(NDT)]
    OF = [C.sb[f"OF{c}"] for c in range(NDT)]
    CMB, COMPL, CSQ2 = C.sb["CMB"], C.sb["COMPL"], C.sb["CSQ2"]
    GWS, RSBG, GSR, GCOL = C.sb["GWS"], C.sb["RSBG"], C.sb["GSR"], C.sb["GCOL"]
    HCOL, HSQ, HN, GSB = C.sb["HCOL"], C.sb["HSQg"], C.sb["HNg"], C.sb["GSB"]
    LNLW, GB, GLW, G2S, GB2, I2B = (C.sb["LNLW"], C.sb["GB"], C.sb["GLW"],
                                    C.sb["G2S"], C.sb["GB2"], C.sb["I2B"])
    ONESCB, ONESR2 = C.sb["ONESCB"], C.sb["ONESR2"]
    WOSTA, WOSTB = C.sb["WOSTA"], C.sb["WOSTB"]
    DELTA, OPB = C.sb["DELTA"], C.sb["OPB"]
    ld, mm, ev, vx = f"{tag}fl", f"{tag}fm2", f"{tag}fe", f"{tag}fv"
    dc2 = f"{tag}dc"
    dcb = C.kb.now(dc2)
    epsap, epsw = C.fconst(1e-5)
    kb.dma("sync", LNLW[:, :], LLWd[:, :], ld, [of_ready])
    kb.dma("sync", GB[:, :], GB1d[:, :], ld)
    kb.dma("sync", GLW[:, :], GLWd[:, :], ld)
    kb.dma("sync", G2S[:, :], GW2d[:, :], ld)
    kb.dma("sync", GB2[0:2, :], GB2d[:, :], ld)
    kb.dma("sync", I2B[0:2, :], C.const("c_I2B", as_bf16(np.eye(2))), ld)
    kb.dma("gpsimd", OPB[:, 0:1024], CPBd[:, :], f"{tag}ob2")
    ldv = kb.now(ld)

    def ln_stats_rows(src_tiles, col_w, bank_a, bank_b, sq_scr, waits):
        mm_acc(C, C.ps[0:1, 512 * bank_a:512 * bank_a + col_w],
               [(list(waits) if cc == 0 else [], ONESCB[:, :], src_tiles[cc]) for cc in range(8)],
               inc=mm, waits=[C.fwait(("bk", bank_a))])
        for cc in range(8):
            kb.op("scalar", lambda E, cc=cc: E.activation(sq_scr, src_tiles[cc], AF.Square),
                  waits=(list(waits) + [C.fwait("sqs")]) if cc == 0 else [], inc=ev)
            mm_acc(C, C.ps[0:1, 512 * bank_b:512 * bank_b + col_w],
                   [([(ev, kb.now(ev))], ONESCB[:, :], sq_scr)],
                   inc=mm, waits=[C.fwait(("bk", bank_b))] if cc == 0 else (),
                   start=(cc == 0), stop=(cc == 7))
        smv = kb.now(mm)
        C.setfree("sqs", mm, smv)
        kb.op("vector", lambda E, col_w=col_w, bank_a=bank_a: E.tensor_scalar(
            GSR[0:1, 0:col_w], C.ps[0:1, 512 * bank_a:512 * bank_a + col_w],
            1.0 / D, None, ALU.mult), waits=[(mm, smv)], inc=vx)
        kb.op("vector", lambda E, col_w=col_w, bank_b=bank_b: E.tensor_scalar(
            GSR[0:1, 512:512 + col_w], C.ps[0:1, 512 * bank_b:512 * bank_b + col_w],
            1.0 / D, None, ALU.mult), inc=vx)
        C.setfree(("bk", bank_a), vx, kb.now(vx))
        C.setfree(("bk", bank_b), vx, kb.now(vx))
        kb.wait("vector", [(vx, kb.now(vx))])
        kb.op("vector", lambda E, col_w=col_w: E.tensor_tensor(
            GSR[0:1, 256:256 + col_w] if col_w <= 256 else GSR[0:1, 0:col_w],
            GSR[0:1, 0:col_w], GSR[0:1, 0:col_w], ALU.mult)
            if False else E.tensor_tensor(C.sb["M2R"][0:1, 0:col_w]
                                          if col_w <= 256 else C.sb["M2R2"][0:1, 0:col_w],
                                          GSR[0:1, 0:col_w], GSR[0:1, 0:col_w], ALU.mult),
            inc=vx)
        kb.wait("vector", [(vx, kb.now(vx))])
        kb.op("vector", lambda E, col_w=col_w: E.tensor_tensor(
            GSR[0:1, 512:512 + col_w], GSR[0:1, 512:512 + col_w],
            C.sb["M2R"][0:1, 0:col_w] if col_w <= 256 else C.sb["M2R2"][0:1, 0:col_w],
            ALU.subtract), inc=vx)
        kb.op("scalar", lambda E, col_w=col_w: E.activation(
            GSR[0:1, 512:512 + col_w], GSR[0:1, 512:512 + col_w], AF.Sqrt, bias=epsap[0:1, :]),
            waits=[(vx, kb.now(vx)), epsw], inc=ev)
        kb.wait("vector", [(ev, kb.now(ev))])
        kb.op("vector", lambda E, col_w=col_w: E.reciprocal(GSR[0:1, 512:512 + col_w],
                                                            GSR[0:1, 512:512 + col_w]), inc=vx)
        kb.wait("vector", [(vx, kb.now(vx))])
        kb.op("vector", lambda E, col_w=col_w: E.scalar_tensor_tensor(
            GSR[0:1, 0:col_w], GSR[0:1, 0:col_w], -1.0, GSR[0:1, 512:512 + col_w],
            ALU.mult, ALU.mult), inc=vx)
        return kb.now(vx)

    sv = ln_stats_rows([OF[c][:, T - 1:T] for c in range(NDT)], 1, 4, 5,
                       CSQ2[:, 0:1], [of_ready, comp_ready, (ld, ldv)])
    mm_acc(C, C.ps[0:128, 512 * 4:512 * 4 + 1],
           [([(vx, sv)], ONESR2[0:1, :], GSR[0:1, 512:513])], inc=mm, waits=[C.fwait(("bk", 4))])
    mm_acc(C, C.ps[0:128, 512 * 5:512 * 5 + 1],
           [((), ONESR2[0:1, :], GSR[0:1, 0:1])], inc=mm, waits=[C.fwait(("bk", 5))])
    bb = kb.now(mm)
    kb.op("scalar", lambda E: E.copy(GCOL[:, 0:1], C.ps[0:128, 512 * 4:512 * 4 + 1]),
          waits=[(mm, bb)], inc=ev)
    kb.op("scalar", lambda E: E.copy(GCOL[:, 1:2], C.ps[0:128, 512 * 5:512 * 5 + 1]), inc=ev)
    gbv = kb.now(ev)
    C.setfree(("bk", 4), ev, gbv)
    C.setfree(("bk", 5), ev, gbv)
    for c in range(NDT):
        kb.op("vector", lambda E, c=c: E.tensor_scalar(
            GSB[:, c:c + 1], OF[c][:, T - 1:T], GCOL[:, 0:1], GCOL[:, 1:2],
            ALU.mult, ALU.add), waits=[(ev, gbv)], inc=vx)
        kb.op("vector", lambda E, c=c: E.tensor_scalar(
            GSB[:, c:c + 1], GSB[:, c:c + 1], LNLW[:, c:c + 1], LNLW[:, 8 + c:8 + c + 1],
            ALU.mult, ALU.add), waits=[(ld, ldv)], inc=vx)
    gh = f"{tag}gh"
    for c in range(NDT):
        kb.op("sync", lambda E, c=c: E.dma_start(
            out=GSB[:, 8 + c:8 + c + 1], in_=comp_hbm[c, :, T - 1:T],
            allow_slow_non_contiguous=True), [comp_ready], inc=gh, inc_n=16)
    fbv, ghv = kb.now(vx), kb.now(gh)
    for c2 in range(NDT):
        gw_w = [C.fwait("gws")]
        if c2 == 0:
            gw_w += [of_ready, comp_ready]
        kb.dma("sync", GWS[:, 0:2048], GW1d[c2, :, :], f"{tag}gw", gw_w)
        gwv = kb.now(f"{tag}gw")
        mm_acc(C, C.ps[0:128, 512 * 4 + 4 * c2:512 * 4 + 4 * c2 + 1],
               [([(vx, fbv), (gh, ghv), (f"{tag}gw", gwv)] if kk == 0 else [],
                 GWS[:, 128 * kk:128 * (kk + 1)], GSB[:, kk:kk + 1]) for kk in range(16)],
               inc=mm, waits=[C.fwait(("bk", 4))] if c2 == 0 else ())
        C.setfree("gws", mm, kb.now(mm))
    hv = kb.now(mm)
    for c2 in range(NDT):
        kb.op("vector", lambda E, c2=c2: E.tensor_tensor(
            HCOL[:, c2:c2 + 1], C.ps[0:128, 512 * 4 + 4 * c2:512 * 4 + 4 * c2 + 1],
            GB[:, c2:c2 + 1], ALU.add), waits=[(mm, hv), (ld, ldv)], inc=vx)
    C.setfree(("bk", 4), vx, kb.now(vx))
    kb.op("vector", lambda E: E.reduce_sum(GCOL[:, 4:5], HCOL[:, :], axis=AX),
          waits=[(vx, kb.now(vx))], inc=vx)
    kb.op("scalar", lambda E: E.activation(HSQ[:, :], HCOL[:, :], AF.Square),
          waits=[(vx, kb.now(vx))], inc=ev)
    kb.op("vector", lambda E: E.reduce_sum(GCOL[:, 5:6], HSQ[:, :], axis=AX),
          waits=[(ev, kb.now(ev))], inc=vx)
    kb.wait("vector", [(vx, kb.now(vx))])
    mm_acc(C, C.ps[0:1, 512 * 4:512 * 4 + 1],
           [([(vx, kb.now(vx))], C.sb["ONESCF"][:, :], GCOL[:, 4:5])], inc=mm,
           waits=[C.fwait(("bk", 4))])
    mm_acc(C, C.ps[0:1, 512 * 4 + 4:512 * 4 + 5],
           [((), C.sb["ONESCF"][:, :], GCOL[:, 5:6])], inc=mm)
    s2v = kb.now(mm)
    kb.op("vector", lambda E: E.tensor_scalar(GSR[0:1, 0:1], C.ps[0:1, 512 * 4:512 * 4 + 1],
                                              1.0 / D, None, ALU.mult), waits=[(mm, s2v)], inc=vx)
    kb.op("vector", lambda E: E.tensor_scalar(GSR[0:1, 1:2], C.ps[0:1, 512 * 4 + 4:512 * 4 + 5],
                                              1.0 / D, None, ALU.mult), inc=vx)
    C.setfree(("bk", 4), vx, kb.now(vx))
    kb.wait("vector", [(vx, kb.now(vx))])
    kb.op("vector", lambda E: E.tensor_tensor(GSR[0:1, 2:3], GSR[0:1, 0:1], GSR[0:1, 0:1],
                                              ALU.mult), inc=vx)
    kb.wait("vector", [(vx, kb.now(vx))])
    kb.op("vector", lambda E: E.tensor_tensor(GSR[0:1, 1:2], GSR[0:1, 1:2], GSR[0:1, 2:3],
                                              ALU.subtract), inc=vx)
    kb.op("scalar", lambda E: E.activation(GSR[0:1, 1:2], GSR[0:1, 1:2], AF.Sqrt,
                                           bias=epsap[0:1, :]),
          waits=[(vx, kb.now(vx)), epsw], inc=ev)
    kb.wait("vector", [(ev, kb.now(ev))])
    kb.op("vector", lambda E: E.reciprocal(GSR[0:1, 1:2], GSR[0:1, 1:2]), inc=vx)
    kb.wait("vector", [(vx, kb.now(vx))])
    kb.op("vector", lambda E: E.scalar_tensor_tensor(GSR[0:1, 0:1], GSR[0:1, 0:1], -1.0,
                                                     GSR[0:1, 1:2], ALU.mult, ALU.mult), inc=vx)
    s3v = kb.now(vx)
    mm_acc(C, C.ps[0:128, 512 * 4:512 * 4 + 1],
           [([(vx, s3v)], ONESR2[0:1, :], GSR[0:1, 1:2])], inc=mm, waits=[C.fwait(("bk", 4))])
    mm_acc(C, C.ps[0:128, 512 * 4 + 4:512 * 4 + 5],
           [((), ONESR2[0:1, :], GSR[0:1, 0:1])], inc=mm)
    b2v = kb.now(mm)
    kb.op("scalar", lambda E: E.copy(GCOL[:, 6:7], C.ps[0:128, 512 * 4:512 * 4 + 1]),
          waits=[(mm, b2v)], inc=ev)
    kb.op("scalar", lambda E: E.copy(GCOL[:, 7:8], C.ps[0:128, 512 * 4 + 4:512 * 4 + 5]), inc=ev)
    C.setfree(("bk", 4), ev, kb.now(ev))
    kb.wait("vector", [(ev, kb.now(ev))])
    kb.op("vector", lambda E: E.tensor_scalar(HCOL[:, :], HCOL[:, :], GCOL[:, 6:7],
                                              GCOL[:, 7:8], ALU.mult, ALU.add), inc=vx)
    kb.wait("vector", [(vx, kb.now(vx))])
    for c2 in range(NDT):
        kb.op("vector", lambda E, c2=c2: E.tensor_scalar(
            HN[:, c2:c2 + 1], HCOL[:, c2:c2 + 1], GLW[:, c2:c2 + 1], GLW[:, 8 + c2:8 + c2 + 1],
            ALU.mult, ALU.add), inc=vx)
    kb.wait("vector", [(vx, kb.now(vx))])
    kb.op("vector", lambda E: E.tensor_relu(HN[:, :], HN[:, :]), inc=vx)
    hnv = kb.now(vx)
    mm_acc(C, C.ps[0:2, 512 * 4:512 * 4 + 1],
           [([(vx, hnv), (ld, ldv)] if c2 == 0 else [], G2S[:, 2 * c2:2 * c2 + 2],
             HN[:, c2:c2 + 1]) for c2 in range(NDT)],
           inc=mm, waits=[C.fwait(("bk", 4))])
    zv = kb.now(mm)
    kb.op("vector", lambda E: E.tensor_tensor(GCOL[0:2, 8:9], C.ps[0:2, 512 * 4:512 * 4 + 1],
                                              GB2[0:2, :], ALU.add), waits=[(mm, zv)], inc=vx)
    C.setfree(("bk", 4), vx, kb.now(vx))
    kb.wait("vector", [(vx, kb.now(vx))])
    kb.op("vector", lambda E: E.tensor_copy(GSB[0:2, 0:1], GCOL[0:2, 8:9]), inc=vx)
    zc = kb.now(vx)
    mm_acc(C, C.ps[0:1, 512 * 4:512 * 4 + 2],
           [([(vx, zc)], GSB[0:2, 0:1], I2B[0:2, 0:2])], inc=mm, waits=[C.fwait(("bk", 4))])
    ztv = kb.now(mm)
    kb.op("vector", lambda E: E.tensor_copy(GSR[0:1, 6:8], C.ps[0:1, 512 * 4:512 * 4 + 2]),
          waits=[(mm, ztv)], inc=vx)
    kb.wait("vector", [(vx, kb.now(vx))])
    kb.op("vector", lambda E: E.tensor_tensor(GSR[0:1, 4:5], GSR[0:1, 6:7],
                                              GSR[0:1, 7:8], ALU.subtract), inc=vx)
    C.setfree(("bk", 4), vx, kb.now(vx))
    kb.wait("scalar", [(vx, kb.now(vx))])
    kb.op("scalar", lambda E: E.activation(GSR[0:1, 5:6], GSR[0:1, 4:5], AF.Sigmoid), inc=ev)
    kb.op("scalar", lambda E: E.activation(GSR[0:1, 6:7], GSR[0:1, 4:5], AF.Sigmoid, scale=-1.0),
          inc=ev)
    sgv = kb.now(ev)
    mm_acc(C, C.ps[0:128, 512 * 4:512 * 4 + 1],
           [([(ev, sgv)], ONESR2[0:1, :], GSR[0:1, 5:6])], inc=mm, waits=[C.fwait(("bk", 4))])
    mm_acc(C, C.ps[0:128, 512 * 4 + 4:512 * 4 + 5],
           [((), ONESR2[0:1, :], GSR[0:1, 6:7])], inc=mm)
    gv2 = kb.now(mm)
    kb.op("scalar", lambda E: E.copy(GCOL[:, 2:3], C.ps[0:128, 512 * 4:512 * 4 + 1]),
          waits=[(mm, gv2)], inc=ev)
    kb.op("scalar", lambda E: E.copy(GCOL[:, 3:4], C.ps[0:128, 512 * 4 + 4:512 * 4 + 5]), inc=ev)
    C.setfree(("bk", 4), ev, kb.now(ev))
    ggv = kb.now(ev)

    for c in range(NDT):
        for n in range(2):
            flat = 1024 * c + 512 * n
            t_ = WOSTA if flat < 4096 else WOSTB
            kb.dma("sync", t_[:, flat % 4096:flat % 4096 + 512],
                   CPWd[128 * c:128 * (c + 1), 512 * n:512 * (n + 1)], f"{tag}cw2",
                   [of_ready] if c == 0 and n == 0 else [])
    cwv2 = kb.now(f"{tag}cw2")
    for tc in range(T // 512):
        tsl = slice(512 * tc, 512 * (tc + 1))
        sv = ln_stats_rows([OF[c][:, tsl] for c in range(NDT)], 512, 4, 5,
                           CSQ2[:, 0:512], [of_ready, (ev, ggv)])
        mm_acc(C, C.ps[0:128, 512 * 4:512 * 4 + 512],
               [([(vx, sv)], ONESR2[0:1, :], GSR[0:1, 512:1024])], inc=mm,
               waits=[C.fwait(("bk", 4))])
        mm_acc(C, C.ps[0:128, 512 * 5:512 * 5 + 512],
               [((), ONESR2[0:1, :], GSR[0:1, 0:512])], inc=mm, waits=[C.fwait(("bk", 5))])
        bcv = kb.now(mm)
        kb.op("scalar", lambda E: E.copy(RSBG[:, 0:512], C.ps[0:128, 512 * 4:512 * 4 + 512]),
              waits=[(mm, bcv), C.fwait("rsbg")], inc=ev)
        kb.op("scalar", lambda E: E.copy(RSBG[:, 512:1024], C.ps[0:128, 512 * 5:512 * 5 + 512]),
              inc=ev)
        rbv = kb.now(ev)
        C.setfree(("bk", 4), ev, rbv)
        C.setfree(("bk", 5), ev, rbv)
        for c in range(NDT):
            cl = kb.dma("sync", COMPL[:, 512 * (c % 2):512 * (c % 2) + 512],
                        comp_hbm[c, :, tsl], f"{tag}cl", [comp_ready, C.fwait(("cpl", c % 2))])
            kb.op("vector", lambda E, c=c, tsl=tsl: E.tensor_tensor(
                CMB[:, 512 * c:512 * (c + 1)], OF[c][:, tsl], RSBG[:, 0:512], ALU.mult),
                waits=[(ev, rbv), C.fwait(("cmb", c))], inc=vx)
            kb.wait("vector", [(vx, kb.now(vx))])
            kb.op("vector", lambda E, c=c: E.tensor_tensor(
                CMB[:, 512 * c:512 * (c + 1)], CMB[:, 512 * c:512 * (c + 1)],
                RSBG[:, 512:1024], ALU.add), inc=vx)
            kb.wait("vector", [(vx, kb.now(vx))])
            kb.op("vector", lambda E, c=c: E.tensor_scalar(
                CMB[:, 512 * c:512 * (c + 1)], CMB[:, 512 * c:512 * (c + 1)],
                LNLW[:, c:c + 1], LNLW[:, 8 + c:8 + c + 1], ALU.mult, ALU.add),
                waits=[(ld, ldv)], inc=vx)
            kb.wait("vector", [(vx, kb.now(vx))])
            kb.op("vector", lambda E, c=c: E.tensor_scalar(
                CMB[:, 512 * c:512 * (c + 1)], CMB[:, 512 * c:512 * (c + 1)],
                GCOL[:, 2:3], None, ALU.mult), waits=[(ev, ggv)], inc=vx)
            kb.wait("vector", [(vx, kb.now(vx))])
            kb.op("vector", lambda E, c=c: E.scalar_tensor_tensor(
                CMB[:, 512 * c:512 * (c + 1)], COMPL[:, 512 * (c % 2):512 * (c % 2) + 512],
                GCOL[:, 3:4], CMB[:, 512 * c:512 * (c + 1)], ALU.mult, ALU.add),
                waits=[(f"{tag}cl", cl)], inc=vx)
            C.setfree(("cpl", c % 2), vx, kb.now(vx))
        cmv = kb.now(vx)
        for tt in range(4):
            i = 4 * tc + tt
            for n in range(2):
                b = n
                flatsl = lambda c, n2: (WOSTA if 1024 * c + 512 * n2 < 4096 else WOSTB,
                                        (1024 * c + 512 * n2) % 4096)
                mm_acc(C, C.bank(b),
                       [((), CMB[:, 512 * c + 128 * tt:512 * c + 128 * tt + 128],
                         flatsl(c, n)[0][:, flatsl(c, n)[1]:flatsl(c, n)[1] + 512])
                        for c in range(NDT)],
                       inc=mm, waits=[(vx, cmv), (f"{tag}cw2", cwv2), C.fwait(("bk", b))])
                mv = kb.now(mm)
                kb.op("vector", lambda E, b=b, n=n: E.tensor_tensor(
                    DELTA[:, 0:512], C.bank(b), OPB[:, 512 * n:512 * n + 512], ALU.add),
                    waits=[(mm, mv), (dc2, dcb + 16 * (2 * i + n)), (f"{tag}ob2", 16)],
                    inc=f"{tag}dv")
                C.setfree(("bk", b), f"{tag}dv", kb.now(f"{tag}dv"))
                sink(i, n, DELTA[:, 0:512], [(f"{tag}dv", kb.now(f"{tag}dv"))])
        for c in range(NDT):
            C.setfree(("cmb", c), mm, kb.now(mm))
    return dc2, kb.now(dc2)


def build_l1_attn_standalone(env=None):
    nc, kb, stack, pref, own = _env_open(env)
    if env is None:
        xn_hbm = nc.dram_tensor("xn", [T, D], BF16, kind="ExternalInput").ap()
        out = nc.dram_tensor("out", [T, D], F32, kind="ExternalOutput").ap()
    else:
        xn_hbm, out = env["xn"], env["out"]
    cattn = nc.dram_tensor("l1_cattn", [D, 3072], BF16, kind="ExternalInput").ap()
    WQd = nc.dram_tensor("l1_wq", [D, D], BF16, kind="ExternalInput").ap()
    WKd = nc.dram_tensor("l1_wk", [D, D], BF16, kind="ExternalInput").ap()
    WVAd = nc.dram_tensor("l1_wva", [D, 1040], BF16, kind="ExternalInput").ap()
    BQd = nc.dram_tensor("l1_bq", [128, 8], F32, kind="ExternalInput").ap()
    BKd = nc.dram_tensor("l1_bk", [128, 8], F32, kind="ExternalInput").ap()
    BVd = nc.dram_tensor("l1_bv", [8, 128, 130], F32, kind="ExternalInput").ap()
    BQK2 = nc.dram_tensor("l1_bqk2", [128, 16], F32, kind="ExternalInput").ap()
    CWd = nc.dram_tensor("l1_cw", [128, 2048], BF16, kind="ExternalInput").ap()
    CBd = nc.dram_tensor("l1_cb", [64, 1], F32, kind="ExternalInput").ap()
    CLNd = nc.dram_tensor("l1_cln", [64, 2], F32, kind="ExternalInput").ap()
    NBv = (T - 32) // 16 + 1
    CM0d = nc.dram_tensor("l1_cm0", [128, NBv], BF16, kind="ExternalInput").ap()
    CM1d = nc.dram_tensor("l1_cm1", [128, NBv], BF16, kind="ExternalInput").ap()
    CMT0d = nc.dram_tensor("l1_cmt0", [128, 256], BF16, kind="ExternalInput").ap()
    CMT1d = nc.dram_tensor("l1_cmt1", [128, 256], BF16, kind="ExternalInput").ap()
    CPWd = nc.dram_tensor("l1_cpw", [D, D], BF16, kind="ExternalInput").ap()
    CPBd = nc.dram_tensor("l1_cpb", [128, 1024], F32, kind="ExternalInput").ap()
    GW1d = nc.dram_tensor("l1_gw1", [8, 128, 2048], BF16, kind="ExternalInput").ap()
    GB1d = nc.dram_tensor("l1_gb1", [128, 8], F32, kind="ExternalInput").ap()
    GLWd = nc.dram_tensor("l1_glw", [128, 16], F32, kind="ExternalInput").ap()
    GW2d = nc.dram_tensor("l1_gw2", [128, 16], BF16, kind="ExternalInput").ap()
    GB2d = nc.dram_tensor("l1_gb2", [2, 1], F32, kind="ExternalInput").ap()
    LLWd = nc.dram_tensor("l1_llw", [128, 16], F32, kind="ExternalInput").ap()
    comp_hbm = nc.dram_tensor(pref + "l1_comp", [8, 128, T], BF16, kind="Internal").ap()

    P = host_consts_common()
    C = Ctx(nc, kb, stack, pref)
    C.l1_cattn = cattn
    for c in range(NDT):
        C.alloc(f"XN{c}", [128, T], BF16)
    alloc_mha(C)
    alloc_mha_bias(C)
    alloc_l1(C)
    alloc_l1_final(C)
    if T >= 4096:
        C.sb["WOSTA"] = C.sb["QA"]
        C.sb["WOSTB"] = C.sb["QB"]
        C.sb["DELTA"] = C.sb["KA"].bitcast(F32)
        C.sb["OPB"] = C.sb["KB_"]
    else:
        C.sb["WOSTA"] = C.alloc("WOSTAd", [128, 4096], BF16)
        C.sb["WOSTB"] = C.alloc("WOSTBd", [128, 4096], BF16)
        C.sb["DELTA"] = C.alloc("DELTAd", [128, 512], F32)
        C.sb["OPB"] = C.alloc("OPBd", [128, 1024], BF16)
    C.alloc("CBCr", [64, 1], F32)
    C.sb["CBC"] = C.sb["CBCr"]
    C.alloc("CLW", [64, 2], F32)
    C.alloc("onerow_c", [1, 256], BF16)
    C.alloc("onecol_c", [128, 1], BF16)
    C.alloc_ps()

    cs = emit_mha_consts(C, P)
    cld = "ccl"
    kb.dma("sync", C.sb["CWV"][:, :], CWd[:, :], cld)
    kb.dma("sync", C.sb["CBCr"][0:64, :], CBd[:, :], cld)
    kb.dma("sync", C.sb["CLW"][0:64, :], CLNd[:, :], cld)
    kb.dma("sync", C.sb["CM0"][:, :], CM0d[:, :], cld)
    kb.dma("sync", C.sb["CM1"][:, :], CM1d[:, :], cld)
    kb.dma("sync", C.sb["CMT0"][:, :], CMT0d[:, :], cld)
    kb.dma("sync", C.sb["CMT1"][:, :], CMT1d[:, :], cld)
    kb.dma("sync", C.sb["IDF3"][:, :], C.const("c_IF32", as_f32(np.eye(128))), cld)
    kb.dma("sync", C.sb["ONE64"][0:64, :], C.const("c_one64", as_f32(np.ones((64, 1)))), cld)
    kb.dma("sync", C.sb["onerow_c"][0:1, :], C.const("c_or256", as_bf16(np.ones((1, 256)))), cld)
    kb.dma("sync", C.sb["onecol_c"][:, :], C.const("c_onescol_bf", as_bf16(np.ones((128, 1)))), cld)
    kb.dma("sync", C.sb["BQK2"][:, :], BQK2[:, :], cld)
    kb.dma("sync", C.sb["ONESCB"][:, :], C.const("c_onescol_bf", as_bf16(np.ones((128, 1)))), cld)
    kb.dma("sync", C.sb["ONESR2"][0:1, :], C.const("c_onesrow", as_f32(np.ones((1, 128)))), cld)
    kb.dma("sync", C.sb["ONESCF"][:, :], C.const("c_onescol_f", as_f32(np.ones((128, 1)))), cld)
    conv_ready = (cld, kb.now(cld))

    xr = emit_xn_from_hbm(C, xn_hbm, "x0")
    hook, cst = emit_l1_comp(C, "a", CWd, CBd, CLNd, comp_hbm, conv_ready)
    ofs = emit_mha(C, "a", WQd, WKd, WVAd, xr, cs, causal=True, q_scale=0.125,
                   biases=dict(bq=BQd, bk=BKd, bv=BVd), rms=None, pair_hook=hook)
    comp_ready = ("acs", kb.now("acs"))

    def sink(i, n, ap, waits):
        kb.dma("gpsimd", out[128 * i:128 * (i + 1), 512 * n:512 * (n + 1)], ap, "l1dc", waits)

    emit_l1_final(C, "l1", CPWd, CPBd, GW1d, GB1d, GLWd, GW2d, GB2d, LLWd,
                  comp_hbm, comp_ready, ofs, sink)
    kb.wait("gpsimd", [("l1dc", kb.now("l1dc"))])
    if own:
        kb.finalize(nc, stack)
    return nc, C, stack


def prep_l1_attn_inputs(I):
    W = I["L1_attn_c_attn_w"].astype(np.float32)
    b = I["L1_attn_c_attn_b"].astype(np.float32)
    bv = b[2048:3072]
    bvs = np.zeros((8, 128, 130), np.float32)
    for p_ in range(8):
        for hh in range(2):
            h = 2 * p_ + hh
            bvs[p_, :, 65 * hh:65 * hh + 64] = bv[64 * h:64 * h + 64][None, :]
            bvs[p_, :, 65 * hh + 64] = 1.0
    NEGM = NEG
    NBv = (T - 32) // 16 + 1
    cm0 = np.zeros((128, NBv), np.float32)
    cm1 = np.zeros((128, NBv), np.float32)
    for qi in range(128):
        cm0[qi, qi + 1:] = NEGM
        if 128 + qi + 1 < NBv:
            cm1[qi, 128 + qi + 1:] = NEGM
    cmt0 = np.zeros((128, 256), np.float32)
    cmt1 = np.zeros((128, 256), np.float32)
    for k_ in range(128):
        cmt0[k_, :k_] = NEGM
        cmt1[k_, :min(128 + k_, 256)] = NEGM
    gw1 = I["L1_attn_gate_w1"].astype(np.float32)
    gw1s = np.zeros((8, 128, 2048), np.float32)
    for c2 in range(8):
        for kk in range(16):
            gw1s[c2, :, 128 * kk:128 * (kk + 1)] = gw1[128 * kk:128 * (kk + 1),
                                                       128 * c2:128 * (c2 + 1)]
    gw2 = I["L1_attn_gate_w2"].astype(np.float32)
    gw2s = np.zeros((128, 16), np.float32)
    for c2 in range(8):
        gw2s[:, 2 * c2:2 * c2 + 2] = gw2[128 * c2:128 * (c2 + 1), :]
    return {
        "l1_cattn": as_bf16(W),
        "l1_wq": as_bf16(W[:, 0:1024]),
        "l1_wk": as_bf16(W[:, 1024:2048]),
        "l1_wva": as_bf16(aug_v_weight(W[:, 2048:3072], 16)),
        "l1_bq": as_f32(b[0:1024].reshape(8, 128).T),
        "l1_bk": as_f32(b[1024:2048].reshape(8, 128).T),
        "l1_bv": as_f32(bvs),
        "l1_bqk2": as_f32(np.concatenate([b[1024:2048].reshape(8, 128).T,
                                          b[2048:3072].reshape(8, 128).T], axis=1)),
        "l1_cw": as_bf16(np.tile(I["L1_attn_conv_w"].astype(np.float32).transpose(1, 2, 0)
                         .reshape(64, 2048), (2, 1))),
        "l1_cb": as_f32(I["L1_attn_conv_b"][:, None]),
        "l1_cln": as_f32(np.stack([I["L1_attn_comp_ln_w"], I["L1_attn_comp_ln_b"]], axis=1)),
        "l1_cm0": as_bf16(cm0), "l1_cm1": as_bf16(cm1), "l1_cmt0": as_bf16(cmt0), "l1_cmt1": as_bf16(cmt1),
        "l1_cpw": as_bf16(I["L1_attn_c_proj_w"]),
        "l1_cpb": as_f32(np.tile(I["L1_attn_c_proj_b"][None, :], (128, 1))),
        "l1_gw1": as_bf16(gw1s),
        "l1_gb1": as_f32(I["L1_attn_gate_b1"].reshape(8, 128).T),
        "l1_glw": as_f32(np.concatenate([I["L1_attn_gate_ln_w"].reshape(8, 128).T,
                                         I["L1_attn_gate_ln_b"].reshape(8, 128).T], axis=1)),
        "l1_gw2": as_bf16(gw2s),
        "l1_gb2": as_f32(I["L1_attn_gate_b2"][:, None]),
        "l1_llw": as_f32(np.concatenate([I["L1_attn_ln_local_w"].reshape(8, 128).T,
                                         I["L1_attn_ln_local_b"].reshape(8, 128).T], axis=1)),
    }


# version 03
# speedup vs baseline: 1.0011x; 1.0011x over previous
import numpy as np
import ml_dtypes
from contextlib import ExitStack
import concourse.bass as bass
import concourse.mybir as mybir
from concourse.bass_utils import run_bass_kernel_spmd

DT = mybir.dt
F32 = DT.float32
BF16 = DT.bfloat16
AF = mybir.ActivationFunctionType
ALU = mybir.AluOpType
AX = mybir.AxisListType.X
BF = ml_dtypes.bfloat16

T = 4096
D = 1024
NTT = T // 128
NDT = D // 128
NEG = -3.0e38


class KB:
    ENGS = ("tensor", "vector", "scalar", "gpsimd", "sync")

    def __init__(self):
        self.q = {e: [] for e in self.ENGS}
        self.cnt = {}
        self.last_wait = {e: {} for e in self.ENGS}
        self.ninstr = 0
        self.pref = ""

    def _s(self, sem):
        if sem.startswith("!"):
            return sem[1:]
        return self.pref + sem

    def now(self, sem):
        return self.cnt.get(self._s(sem), 0)

    def bump(self, sem, n=1):
        sem = self._s(sem)
        self.cnt[sem] = self.cnt.get(sem, 0) + n
        return self.cnt[sem]

    def _filter(self, eng, waits):
        ws = []
        lw = self.last_wait[eng]
        for wv in waits:
            if wv is None:
                continue
            s, v = wv
            s = self._s(s)
            if v <= 0:
                continue
            self.cnt.setdefault(s, 0)
            if v > lw.get(s, 0):
                ws.append((s, v))
                lw[s] = v
        return ws

    def op(self, eng, build, waits=(), inc=None, inc_n=1):
        ws = self._filter(eng, waits)
        incs = []
        if inc is not None:
            inc = self._s(inc)
            self.bump("!" + inc, inc_n)
            incs.append((inc, inc_n))
        import sys as _sys
        _ln = tuple(_sys._getframe(k).f_lineno for k in (1, 2, 3))

        def replay(E, sems, ws=tuple(ws), incs=tuple(incs), build=build, _ln=_ln):
            for s, v in ws:
                E.wait_ge(sems[s], v)
            try:
                ins = build(E)
            except Exception as ex:
                raise RuntimeError(f"op from line {_ln}: {ex}") from ex
            for s, n in incs:
                ins.then_inc(sems[s], n)
        self.q[eng].append(replay)
        self.ninstr += 1 + len(ws)
        return self.cnt.get(inc) if inc else None

    def wait(self, eng, waits):
        ws = self._filter(eng, waits)
        if not ws:
            return

        def replay(E, sems, ws=tuple(ws)):
            for s, v in ws:
                E.wait_ge(sems[s], v)
        self.q[eng].append(replay)
        self.ninstr += len(ws)

    def dma(self, eng, out, in_, sem, waits=()):
        return self.op(eng, lambda E, o=out, i=in_: E.dma_start(out=o, in_=i),
                       waits, inc=sem, inc_n=16)

    def dmat(self, eng, out, in_, sem, waits=()):
        return self.op(eng, lambda E, o=out, i=in_: E.dma_start_transpose(o, i),
                       waits, inc=sem, inc_n=16)

    def finalize(self, nc, stack):
        sems = {name: stack.enter_context(nc.semaphore(name)) for name in self.cnt}
        with nc.Block() as block:
            for e in self.ENGS:
                fns = self.q[e]
                if not fns:
                    continue
                getattr(block, e)(lambda E, fns=fns: [f(E, sems) for f in fns])
        return sems


class Ctx:
    def __init__(self, nc, kb, stack, pref=""):
        self.nc = nc
        self.kb = kb
        self.stack = stack
        self.pref = pref
        self.sb = {}
        self.ps = None
        self.consts = {}
        self.free = {}

    def alloc(self, name, shape, dtype):
        t = self.stack.enter_context(self.nc.sbuf_tensor(self.pref + name, shape, dtype))
        self.sb[name] = t
        return t

    def alloc_ps(self):
        self.ps = self.stack.enter_context(
            self.nc.psum_tensor(self.pref + "PS", [128, 4096], F32))
        return self.ps

    def bank(self, b, rows=128, cols=512):
        return self.ps[0:rows, 512 * b:512 * b + cols]

    def fwait(self, tag):
        return self.free.get(tag)

    def setfree(self, tag, sem, val):
        self.free[tag] = (sem, val)

    def const(self, name, arr):
        name = self.pref + name
        if name in self.consts:
            return self.consts[name][0]
        dt = {np.dtype(np.float32): F32, np.dtype(BF): BF16,
              np.dtype(np.int32): DT.int32}[arr.dtype]
        ap = self.nc.dram_tensor(name, list(arr.shape), dt, kind="ExternalInput").ap()
        self.consts[name] = (ap, np.ascontiguousarray(arr))
        return ap

    def fconst(self, v):
        v = float(v)
        if not hasattr(self, "fc_vals"):
            self.fc_vals = []
            self.fc_tile = self.alloc("FCN", [128, 16], F32)
            ap = self.nc.dram_tensor(self.pref + "c_fconsts", [128, 16], F32,
                                     kind="ExternalInput").ap()
            self.consts[self.pref + "c_fconsts"] = (ap, None)
            self.kb.dma("sync", self.fc_tile[:, :], ap[:, :], "fcl")
        if v not in self.fc_vals:
            self.fc_vals.append(v)
            assert len(self.fc_vals) <= 16
        j = self.fc_vals.index(v)
        return self.fc_tile[:, j:j + 1], ("fcl", 16)

    def const_maps(self):
        m = {n: a for n, (ap, a) in self.consts.items() if a is not None}
        if hasattr(self, "fc_vals"):
            arr = np.zeros((128, 16), np.float32)
            for j, v in enumerate(self.fc_vals):
                arr[:, j] = v
            m[self.pref + "c_fconsts"] = arr
        return m


def mm_acc(C, psum_ap, pairs, inc, waits=(), start=True, stop=True):
    n = len(pairs)
    for i, (w, l, r) in enumerate(pairs):
        allw = (list(waits) + list(w)) if i == 0 else list(w)
        C.kb.op("tensor",
                lambda E, l=l, r=r, s=(start and i == 0), t=(stop and i == n - 1), p=psum_ap:
                    E.matmul(p, l, r, start=s, stop=t),
                waits=allw, inc=inc if i == n - 1 else None)


def as_bf16(a):
    return np.ascontiguousarray(np.asarray(a).astype(BF))


def as_f32(a):
    return np.ascontiguousarray(np.asarray(a).astype(np.float32))


def aug_v_weight(WvT_math, nheads, dhead=64):
    d = WvT_math.shape[0]
    aug = np.zeros((d, nheads * (dhead + 1)), np.float32)
    for h in range(nheads):
        aug[:, (dhead + 1) * h:(dhead + 1) * h + dhead] = \
            WvT_math[:, dhead * h:dhead * h + dhead]
    return aug


def host_consts_common():
    P = {}
    P["c_negI"] = as_bf16(-np.eye(128, dtype=np.float32))
    P["c_ones1x64"] = as_f32(np.ones((1, 64)))
    m = np.zeros((128, 128), np.float32)
    m[np.triu_indices(128, 1)] = NEG
    P["c_mask"] = m
    P["c_maskT"] = np.ascontiguousarray(m.T)
    P["c_onerow"] = as_bf16(np.ones((1, T)))
    return P


def alloc_ln(C):
    C.alloc("LNX", [128, 2048], F32)
    C.alloc("LNSQ", [128, 1024], F32)
    C.alloc("LNST", [128, 8], F32)
    C.alloc("XNTM", [128, 2048], BF16)


def emit_block_ln(C, src_dram, xn_hbm, tag, src_waits=()):
    kb = C.kb
    XT, SQ, ST, XNT = C.sb["LNX"], C.sb["LNSQ"], C.sb["LNST"], C.sb["XNTM"]
    XN = [C.sb[f"XN{c}"] for c in range(NDT)]
    ld, st, vs, sc = (f"{tag}ld", f"{tag}st", f"{tag}vs", f"{tag}sc")
    for i in range(NTT):
        sl = slice((i % 2) * 1024, (i % 2) * 1024 + 1024)
        w = list(src_waits)
        if i >= 2:
            w.append((sc, i - 1))
        lv = kb.dma("sync", XT[:, sl], src_dram[128 * i:128 * (i + 1), :], ld, w)
        kb.op("vector", lambda E, sl=sl: E.reduce_sum(ST[:, 0:1], XT[:, sl], axis=AX),
              waits=[(ld, lv)], inc=vs)
        kb.op("scalar", lambda E, sl=sl: E.activation(SQ[:, :], XT[:, sl], AF.Square,
                                                      accum_out=ST[:, 1:2]),
              waits=[(ld, lv)], inc=vs)
        kb.wait("vector", [(vs, kb.now(vs))])
        kb.op("vector", lambda E: E.tensor_scalar(ST[:, 2:3], ST[:, 0:1], 1.0 / D, None, ALU.mult))
        kb.op("vector", lambda E: E.tensor_scalar(ST[:, 3:4], ST[:, 1:2], 1.0 / D, None, ALU.mult))
        kb.op("vector", lambda E: E.tensor_tensor(ST[:, 4:5], ST[:, 2:3], ST[:, 2:3], ALU.mult), inc=vs)
        kb.op("vector", lambda E: E.tensor_tensor(ST[:, 3:4], ST[:, 3:4], ST[:, 4:5], ALU.subtract),
              waits=[(vs, kb.now(vs))], inc=vs)
        epsap, epsw = C.fconst(1e-5)
        kb.op("scalar", lambda E, epsap=epsap: E.activation(ST[:, 5:6], ST[:, 3:4], AF.Sqrt, bias=epsap),
              waits=[(vs, kb.now(vs)), epsw], inc=vs)
        kb.op("vector", lambda E: E.reciprocal(ST[:, 6:7], ST[:, 5:6]),
              waits=[(vs, kb.now(vs))], inc=vs)
        kb.op("vector", lambda E: E.scalar_tensor_tensor(ST[:, 7:8], ST[:, 2:3], -1.0, ST[:, 6:7],
                                                         ALU.mult, ALU.mult),
              waits=[(vs, kb.now(vs))], inc=vs)
        w2 = [(vs, kb.now(vs))]
        if i >= 2:
            w2.append((st, 16 * (i - 1)))
        kb.op("vector", lambda E, sl=sl: E.tensor_scalar(XNT[:, sl], XT[:, sl], ST[:, 6:7], ST[:, 7:8],
                                                         ALU.mult, ALU.add),
              waits=w2, inc=sc)
        kb.dma("gpsimd", xn_hbm[128 * i:128 * (i + 1), :], XNT[:, sl], st,
               waits=[(sc, kb.now(sc))])
    fin = f"{tag}fmr"
    allst = kb.now(st)
    for c in range(NDT):
        kb.dmat("sync", XN[c][:, :], xn_hbm[:, 128 * c:128 * (c + 1)], fin, waits=[(st, allst)])
    return fin, kb.now(fin)


def emit_xn_from_hbm(C, xn_hbm, tag, waits=()):
    kb = C.kb
    XN = [C.sb[f"XN{c}"] for c in range(NDT)]
    fin = f"{tag}fmr"
    for c in range(NDT):
        kb.dmat("sync", XN[c][:, :], xn_hbm[:, 128 * c:128 * (c + 1)], fin, waits=list(waits))
    return fin, kb.now(fin)


def alloc_mha(C, nof=8, qkvw=True):
    for c in range(nof):
        C.alloc(f"OF{c}", [128, T], BF16)
    C.alloc("QA", [128, T], BF16)
    C.alloc("QB", [128, T], BF16)
    C.alloc("KA", [128, T], BF16)
    C.alloc("KB_", [128, T], BF16)
    C.alloc("VA", [128, 65 * NTT], BF16)
    C.alloc("VB", [128, 65 * NTT], BF16)
    if qkvw:
        C.alloc("QW", [128, 1024], BF16)
        C.alloc("KW", [128, 1024], BF16)
        C.alloc("VW", [128, 1040], BF16)
    C.alloc("PT", [128, 1024], BF16)
    C.alloc("MBc", [128, 2], BF16)
    C.alloc("RL", [1, 512], F32)
    C.alloc("RB", [64, 512], F32)
    C.alloc("negI", [128, 128], BF16)
    C.alloc("ones1x64", [1, 64], F32)
    C.alloc("maskT", [128, 128], F32)
    C.alloc("mask", [128, 128], F32)


def emit_mha_consts(C, P, tag="mhc"):
    kb = C.kb
    s = f"{tag}ld"
    kb.dma("sync", C.sb["negI"][:, :], C.const("c_negI", P["c_negI"]), s)
    kb.dma("sync", C.sb["ones1x64"][0:1, :], C.const("c_ones1x64", P["c_ones1x64"]), s)
    kb.dma("sync", C.sb["maskT"][:, :], C.const("c_maskT", P["c_maskT"]), s)
    kb.dma("sync", C.sb["mask"][:, :], C.const("c_mask", P["c_mask"]), s)
    return s, kb.now(s)


def emit_mha(C, tag, WqT, WkT, WvTa, xn_ready, const_ready, causal=True,
             q_scale=0.125, nheads=16, biases=None, rms=None, pair_hook=None):
    kb = C.kb
    XN = [C.sb[f"XN{c}"] for c in range(NDT)]
    OF = [C.sb[f"OF{c}"] for c in range(nheads // 2)]
    QA, QB, KA, KB_ = C.sb["QA"], C.sb["QB"], C.sb["KA"], C.sb["KB_"]
    V_ = [C.sb["VA"], C.sb["VB"]]
    QW, KW, VW = C.sb["QW"], C.sb["KW"], C.sb["VW"]
    PT, MBc = C.sb["PT"], C.sb["MBc"]
    RL, RB = C.sb["RL"], C.sb["RB"]
    negI, ones64 = C.sb["negI"], C.sb["ones1x64"]
    maskT, mask = C.sb["maskT"], C.sb["mask"]

    wld, mm, ev, vx, pxp, ofs, pd = (f"{tag}wl", f"{tag}mm", f"{tag}ev", f"{tag}vx",
                                     f"{tag}px", f"{tag}of", f"{tag}pd")
    onerow_d = C.const("c_onerow", np.ones((1, T), BF))
    orl = f"{tag}or"

    if biases is not None:
        BQ2, BK2 = C.sb["BQ2"], C.sb["BK2"]
        VBS = C.sb["VBS"]
        kb.dma("sync", BQ2[:, :], biases["bq"][:, :], orl)
        kb.dma("sync", BK2[:, :], biases["bk"][:, :], orl)
        bldv = kb.now(orl)
        if rms is None:
            kb.op("vector", lambda E: E.tensor_scalar(BQ2[:, :], BQ2[:, :], float(q_scale), None,
                                                      ALU.mult), waits=[(orl, bldv)], inc=vx)
        else:
            kb.wait("vector", [(orl, bldv)])
        bqv = kb.now(vx)
    if rms is not None:
        qnw_d, knw_d, rmseps = rms
        QNW, KNW = C.sb["QNW"], C.sb["KNW"]
        SEG, SEGT = C.sb["SEG"], C.sb["SEGT"]
        QS_, QSQ, RSTD = C.sb["QS_"], C.sb["QSQ"], C.sb["RSTD"]
        seg = np.zeros((128, 2), np.float32); seg[0:64, 0] = 1; seg[64:128, 1] = 1
        kb.dma("sync", SEG[:, :], C.const("c_seg", as_bf16(seg)), orl)
        kb.dma("sync", SEGT[0:2, :], C.const("c_segT", as_bf16(seg.T)), orl)
        kb.dma("sync", C.sb["SEGTF"][0:2, :], C.const("c_segTf", as_f32(seg.T)), orl)
        kb.dma("sync", QNW[:, 0:1], qnw_d[:, :], orl)
        kb.dma("sync", KNW[:, 0:1], knw_d[:, :], orl)
        rldv = kb.now(orl)
        kb.op("vector", lambda E: E.tensor_scalar(QNW[:, 1:2], QNW[:, 0:1], float(q_scale), None,
                                                  ALU.mult), waits=[(orl, rldv)], inc=vx)
        rmv = kb.now(vx)
        epsap, epsw = C.fconst(float(rmseps))

    npairs = nheads // 2
    for p in range(npairs):
        wv = [(pd, p), xn_ready, const_ready]
        kb.dma("sync", KA[64:65, :], onerow_d[0:1, :], orl, wv)
        kb.dma("sync", KB_[64:65, :], onerow_d[0:1, :], orl)
        kb.wait("tensor", [(orl, kb.now(orl))])
        for c in range(NDT):
            kb.dma("sync", QW[:, 128 * c:128 * (c + 1)],
                   WqT[128 * c:128 * (c + 1), 128 * p:128 * (p + 1)], wld,
                   wv if c == 0 else ())
            kb.dma("sync", KW[:, 128 * c:128 * (c + 1)],
                   WkT[128 * c:128 * (c + 1), 128 * p:128 * (p + 1)], wld)
            kb.dma("sync", VW[:, 130 * c:130 * (c + 1)],
                   WvTa[128 * c:128 * (c + 1), 130 * p:130 * (p + 1)], wld)
        if biases is not None:
            kb.dma("sync", VBS[:, 130 * (p % 2):130 * (p % 2) + 130], biases["bv"][p, :, :],
                   wld, [C.fwait(("vbs", p % 2))])
        wldv = kb.now(wld)

        for ch in range(T // 512):
            sl = slice(512 * ch, 512 * (ch + 1))
            bq, bk = ch % 2, 2 + ch % 2
            mm_acc(C, C.bank(bq),
                   [((), QW[:, 128 * c:128 * (c + 1)], XN[c][:, sl]) for c in range(NDT)],
                   inc=mm, waits=[(wld, wldv), C.fwait(("bk", bq))])
            mq = kb.now(mm)
            mm_acc(C, C.bank(bk),
                   [((), KW[:, 128 * c:128 * (c + 1)], XN[c][:, sl]) for c in range(NDT)],
                   inc=mm, waits=[C.fwait(("bk", bk))])
            mk = kb.now(mm)
            if rms is None and biases is None:
                kb.op("scalar", lambda E, sl=sl, b=bq: E.activation(
                    QA[0:64, sl], C.ps[0:64, 512 * b:512 * b + 512], AF.Copy, scale=float(q_scale)),
                    waits=[(mm, mq)], inc=ev)
                kb.op("scalar", lambda E, sl=sl, b=bq: E.activation(
                    QB[0:64, sl], C.ps[64:128, 512 * b:512 * b + 512], AF.Copy, scale=float(q_scale)),
                    inc=ev)
                C.setfree(("bk", bq), ev, kb.now(ev))
                kb.op("scalar", lambda E, sl=sl, b=bk: E.copy(KA[0:64, sl],
                                                              C.ps[0:64, 512 * b:512 * b + 512]),
                      waits=[(mm, mk)], inc=ev)
                kb.op("scalar", lambda E, sl=sl, b=bk: E.copy(KB_[0:64, sl],
                                                              C.ps[64:128, 512 * b:512 * b + 512]),
                      inc=ev)
                C.setfree(("bk", bk), ev, kb.now(ev))
            elif rms is None:
                kb.op("vector", lambda E, sl=sl, b=bq, p=p: E.tensor_scalar(
                    QA[0:64, sl], C.ps[0:64, 512 * b:512 * b + 512], float(q_scale),
                    BQ2[0:64, p:p + 1], ALU.mult, ALU.add), waits=[(mm, mq), (vx, bqv)], inc=vx)
                kb.op("vector", lambda E, sl=sl, b=bq, p=p: E.tensor_scalar(
                    QB[0:64, sl], C.ps[64:128, 512 * b:512 * b + 512], float(q_scale),
                    BQ2[64:128, p:p + 1], ALU.mult, ALU.add), inc=vx)
                C.setfree(("bk", bq), vx, kb.now(vx))
                kb.op("vector", lambda E, sl=sl, b=bk, p=p: E.tensor_scalar(
                    KA[0:64, sl], C.ps[0:64, 512 * b:512 * b + 512], 1.0,
                    BK2[0:64, p:p + 1], ALU.mult, ALU.add), waits=[(mm, mk)], inc=vx)
                kb.op("vector", lambda E, sl=sl, b=bk, p=p: E.tensor_scalar(
                    KB_[0:64, sl], C.ps[64:128, 512 * b:512 * b + 512], 1.0,
                    BK2[64:128, p:p + 1], ALU.mult, ALU.add), inc=vx)
                C.setfree(("bk", bk), vx, kb.now(vx))
            else:
                for (src_b, dst, bcol, wcol, mmv) in (
                        (bq, QA, "bq", QNW[:, 1:2], mq), (bk, KA, "bk", KNW[:, 0:1], mk)):
                    dstB = QB if dst is QA else KB_
                    bc = BQ2 if bcol == "bq" else BK2
                    kb.op("vector", lambda E, b=src_b, bc=bc, p=p: E.tensor_scalar(
                        QS_[:, 0:512], C.ps[0:128, 512 * b:512 * b + 512], 1.0,
                        bc[:, p:p + 1], ALU.mult, ALU.add),
                        waits=[(mm, mmv), (vx, bqv), C.fwait("qs")], inc=vx)
                    qrv = kb.now(vx)
                    kb.op("scalar", lambda E: E.activation(QSQ[:, :], QS_[:, 0:512],
                                                           AF.Square),
                          waits=[(vx, qrv), C.fwait("qsq")], inc=ev)
                    mm_acc(C, C.ps[0:2, 512 * 6:512 * 6 + 512],
                           [([(ev, kb.now(ev)), (orl, rldv)], SEG[:, :],
                             QSQ[:, :])], inc=mm, waits=[C.fwait(("bk", 6))])
                    C.setfree("qsq", mm, kb.now(mm))
                    kb.op("vector", lambda E: E.tensor_scalar(RSTD[0:2, 0:512],
                                                              C.ps[0:2, 512 * 6:512 * 6 + 512],
                                                              1.0 / 64, None, ALU.mult),
                          waits=[(mm, kb.now(mm))], inc=vx)
                    C.setfree(("bk", 6), vx, kb.now(vx))
                    kb.op("scalar", lambda E, epsap=epsap: E.activation(
                        RSTD[0:2, 0:512], RSTD[0:2, 0:512], AF.Sqrt, bias=epsap[0:2, :]),
                        waits=[(vx, kb.now(vx)), epsw], inc=ev)
                    kb.wait("vector", [(ev, kb.now(ev))])
                    kb.op("vector", lambda E: E.reciprocal(RSTD[0:2, 0:512],
                                                           RSTD[0:2, 0:512]),
                          inc=vx)
                    mm_acc(C, C.ps[0:128, 512 * 7:512 * 7 + 512],
                           [([(vx, kb.now(vx))], C.sb["SEGTF"][0:2, :], RSTD[0:2, 0:512])],
                           inc=mm, waits=[C.fwait(("bk", 7))])
                    bcv = kb.now(mm)
                    kb.op("vector", lambda E, sl=sl, wcol=wcol: E.scalar_tensor_tensor(
                        QS_[:, 0:512], QS_[:, 0:512], wcol, C.ps[0:128, 512 * 7:512 * 7 + 512],
                        ALU.mult, ALU.mult),
                        waits=[(mm, bcv), (vx, rmv)], inc=vx)
                    kb.op("vector", lambda E, sl=sl, dst=dst: E.tensor_copy(dst[0:64, sl],
                                                                            QS_[0:64, 0:512]),
                          inc=vx)
                    kb.op("vector", lambda E, sl=sl, dstB=dstB: E.tensor_copy(dstB[0:64, sl],
                                                                              QS_[64:128, 0:512]),
                          inc=vx)
                    C.setfree(("bk", 7), vx, kb.now(vx))
                    C.setfree(("bk", src_b), vx, kb.now(vx))
                    C.setfree("qs", vx, kb.now(vx))
        for i in range(NTT):
            b = 4 + i % 2
            mm_acc(C, C.ps[0:128, 512 * b:512 * b + 130],
                   [((), XN[c][:, 128 * i:128 * (i + 1)], VW[:, 130 * c:130 * (c + 1)])
                    for c in range(NDT)], inc=mm, waits=[C.fwait(("bk", b))])
            mv = kb.now(mm)
            if biases is not None:
                kb.op("vector", lambda E, i=i, b=b, p=p: E.tensor_tensor(
                    V_[0][:, 65 * i:65 * i + 65], C.ps[0:128, 512 * b:512 * b + 65],
                    VBS[:, 130 * (p % 2):130 * (p % 2) + 65], ALU.add),
                    waits=[(mm, mv), (wld, wldv)], inc=vx)
                kb.op("vector", lambda E, i=i, b=b, p=p: E.tensor_tensor(
                    V_[1][:, 65 * i:65 * i + 65], C.ps[0:128, 512 * b + 65:512 * b + 130],
                    VBS[:, 130 * (p % 2) + 65:130 * (p % 2) + 130], ALU.add), inc=vx)
                C.setfree(("bk", b), vx, kb.now(vx))
            else:
                kb.op("scalar", lambda E, i=i, b=b: E.copy(V_[0][:, 65 * i:65 * i + 65],
                                                           C.ps[0:128, 512 * b:512 * b + 65]),
                      waits=[(mm, mv)], inc=ev)
                kb.op("scalar", lambda E, i=i, b=b: E.copy(V_[1][:, 65 * i:65 * i + 65],
                                                           C.ps[0:128, 512 * b + 65:512 * b + 130]),
                      inc=ev)
                C.setfree(("bk", b), ev, kb.now(ev))
        if biases is not None:
            C.setfree(("vbs", p % 2), mm, kb.now(mm))
        else:
            for hh in range(2):
                kb.op("scalar", lambda E, hh=hh: E.activation(
                    V_[hh][:, 64::65], V_[hh][:, 64::65], AF.Copy, bias=1.0),
                    waits=[(ev, kb.now(ev)), (vx, kb.now(vx))], inc=ev)
        projv_e, projv_v = kb.now(ev), kb.now(vx)

        for hh in range(2):
            Q = (QA, QB)[hh]
            K = (KA, KB_)[hh]
            V = V_[hh]
            for q in range(NTT):
                W = 128 * (q + 1) if causal else T
                nch = (W + 511) // 512
                fw = [(ev, projv_e), (vx, projv_v), C.fwait("p1banks")]
                for chk in range(nch):
                    cw = min(512, W - 512 * chk)
                    mm_acc(C, C.ps[0:128, 512 * chk:512 * chk + cw],
                           [((), Q[0:64, 128 * q:128 * (q + 1)],
                             K[0:64, 512 * chk:512 * chk + cw])],
                           inc=mm, waits=fw if chk == 0 else ())
                ms = kb.now(mm)
                rw = [(mm, ms)]
                if causal:
                    kb.op("vector", lambda E, W=W: E.tensor_tensor(
                        C.ps[0:128, W - 128:W], C.ps[0:128, W - 128:W], mask[:, :], ALU.add),
                        waits=[(mm, ms)], inc=vx)
                    rw.append((vx, kb.now(vx)))
                kb.op("vector", lambda E, W=W, hh=hh: E.reduce_max(
                    MBc[:, hh:hh + 1], C.ps[0:128, 0:W], axis=AX), waits=rw, inc=vx)
                mm_acc(C, C.ps[0:1, 0:128],
                       [([(vx, kb.now(vx))], MBc[:, hh:hh + 1], negI[:, :])], inc=mm)
                kb.op("scalar", lambda E, q=q, Q=Q: E.copy(Q[64:65, 128 * q:128 * (q + 1)],
                                                           C.ps[0:1, 0:128]),
                      waits=[(mm, kb.now(mm))], inc=ev)
                C.setfree("p1banks", ev, kb.now(ev))
            qaugv = kb.now(ev)
            p1vx = kb.now(vx)

            ptfree = {}
            for g in range(T // 512):
                jmax = (4 * g + 3) if causal else (NTT - 1)
                for j in range(jmax + 1):
                    qlo = max(512 * g, 128 * j) if causal else 512 * g
                    qhi = 512 * (g + 1)
                    N = qhi - qlo
                    sb_ = 4 + j % 2
                    mm_acc(C, C.ps[0:128, 512 * sb_:512 * sb_ + N],
                           [((), K[0:65, 128 * j:128 * (j + 1)], Q[0:65, qlo:qhi])],
                           inc=mm,
                           waits=[(ev, qaugv), (vx, p1vx), C.fwait(("st", sb_)),
                                  C.fwait("ob")])
                    mst = kb.now(mm)
                    exp_w = [(mm, mst), ptfree.get(j % 2)]
                    if causal and qlo == 128 * j:
                        kb.op("vector", lambda E, sb_=sb_: E.tensor_tensor(
                            C.ps[0:128, 512 * sb_:512 * sb_ + 128],
                            C.ps[0:128, 512 * sb_:512 * sb_ + 128], maskT[:, :], ALU.add),
                            waits=[(mm, mst)], inc=vx)
                        exp_w.append((vx, kb.now(vx)))
                    pts = 512 * (j % 2)
                    kb.op("scalar", lambda E, sb_=sb_, pts=pts, N=N: E.activation(
                        PT[:, pts:pts + N], C.ps[0:128, 512 * sb_:512 * sb_ + N], AF.Exp),
                        waits=exp_w, inc=pxp)
                    C.setfree(("st", sb_), pxp, kb.now(pxp))
                    co = qlo - 512 * g
                    mm_acc(C, C.ps[0:65, 512 * 7 + co:512 * 7 + co + N],
                           [([(pxp, kb.now(pxp))], V[:, 65 * j:65 * j + 65], PT[:, pts:pts + N])],
                           inc=mm, start=(j == 0), stop=(j == jmax))
                    ptfree[j % 2] = (mm, kb.now(mm))
                pvd = kb.now(mm)
                kb.op("vector", lambda E: E.reciprocal(RL[0:1, :], C.ps[64:65, 512 * 7:512 * 8]),
                      waits=[(mm, pvd), C.fwait("rl")], inc=vx)
                mm_acc(C, C.ps[0:64, 512 * 6:512 * 6 + 512],
                       [([(vx, kb.now(vx))], ones64[0:1, :], RL[0:1, :])], inc=mm,
                       waits=[C.fwait("b6")])
                kb.op("scalar", lambda E: E.copy(RB[:, :], C.ps[0:64, 512 * 6:512 * 6 + 512]),
                      waits=[(mm, kb.now(mm)), C.fwait("rb")], inc=ev)
                C.setfree("b6", ev, kb.now(ev))
                C.setfree("rl", ev, kb.now(ev))
                po = 64 * hh
                kb.op("vector", lambda E, g=g, po=po, p=p: E.tensor_tensor(
                    OF[p][po:po + 64, 512 * g:512 * (g + 1)],
                    C.ps[0:64, 512 * 7:512 * 8], RB[:, :], ALU.mult),
                    waits=[(ev, kb.now(ev))], inc=ofs)
                C.setfree("ob", ofs, kb.now(ofs))
                C.setfree("rb", ofs, kb.now(ofs))
        hw_ = pair_hook(p) if pair_hook is not None else []
        kb.op("sync", lambda E: E.nop(),
              waits=[(ofs, kb.now(ofs)), (ev, kb.now(ev)), (vx, kb.now(vx))] + list(hw_),
              inc=pd)
    return ofs, kb.now(ofs)


def alloc_mha_bias(C):
    C.alloc("BQ2", [128, 8], F32)
    C.alloc("BK2", [128, 8], F32)
    C.alloc("VBS", [128, 260], F32)


def alloc_mha_rms(C):
    C.alloc("QNW", [128, 2], F32)
    C.alloc("KNW", [128, 1], F32)
    C.alloc("SEG", [128, 2], BF16)
    C.alloc("SEGT", [2, 128], BF16)
    C.alloc("QS_", [128, 512], F32)
    C.alloc("QSQ", [128, 512], BF16)
    C.alloc("RSTD", [2, 512], F32)
    C.alloc("SEGTF", [2, 128], F32)


def emit_out_proj(C, tag, WT_dram, of_ready, sink, nin=1024, nout=1024, ofname="OF",
                  bias=None):
    kb = C.kb
    OF = [C.sb[f"{ofname}{c}"] for c in range(nin // 128)]
    WOSTA, WOSTB = C.sb["WOSTA"], C.sb["WOSTB"]
    DELTA = C.sb["DELTA"]
    wld, mm, ev, dc = f"{tag}wl", f"{tag}mm", f"{tag}ev", f"{tag}dc"
    dcb = C.kb.now(dc)
    nko = nin // 128
    nno = nout // 512

    def wost(c, n):
        flat = 1024 * c + 512 * n
        t = WOSTA if flat < 4096 else WOSTB
        o = flat % 4096
        return t[:, o:o + 512]
    if bias is not None:
        OPB = C.sb["OPB"]
        opl = f"{tag}opb"
        kb.dma("gpsimd", OPB[:, 0:nout], bias[:, 0:nout], opl, [of_ready])
    for c in range(nko):
        for n in range(nno):
            kb.dma("sync", wost(c, n), WT_dram[128 * c:128 * (c + 1), 512 * n:512 * n + 512],
                   wld, [of_ready])
    wldv = kb.now(wld)
    for i in range(NTT):
        for n in range(nno):
            b = (2 * i + n) % 4
            mm_acc(C, C.bank(b),
                   [((), OF[c][:, 128 * i:128 * (i + 1)], wost(c, n))
                    for c in range(nko)],
                   inc=mm, waits=[(wld, wldv), of_ready, C.fwait(("opk", b))])
            mv = kb.now(mm)
            if bias is None:
                kb.op("vector", lambda E, b=b: E.tensor_copy(
                    DELTA[:, 0:512], C.bank(b)),
                    waits=[(mm, mv), (dc, dcb + 16 * (nno * i + n))], inc=ev)
            else:
                kb.op("vector", lambda E, b=b, o=512 * n: E.tensor_tensor(
                    DELTA[:, 0:512], C.bank(b), OPB[:, o:o + 512], ALU.add),
                    waits=[(mm, mv), (dc, dcb + 16 * (nno * i + n)), (wld, wldv),
                           (f"{tag}opb", 16)], inc=ev)
            C.setfree(("opk", b), ev, kb.now(ev))
            sink(i, n, DELTA[:, 0:512], [(ev, kb.now(ev))])
    return dc, kb.now(dc)



def _env_open(env):
    if env is None:
        return (bass.Bass("TRN2", target_bir_lowering=False), KB(), ExitStack(), "", True)
    return env["nc"], env["kb"], env["stack"], env["pref"], False


def build_l2_attn_standalone(env=None):
    nc, kb, stack, pref, own = _env_open(env)
    if env is None:
        xn_hbm = nc.dram_tensor("xn", [T, D], BF16, kind="ExternalInput").ap()
        out = nc.dram_tensor("out", [T, D], F32, kind="ExternalOutput").ap()
    else:
        xn_hbm, out = env["xn"], env["out"]
    WqT = nc.dram_tensor("l2_wqt", [D, D], BF16, kind="ExternalInput").ap()
    WkT = nc.dram_tensor("l2_wkt", [D, D], BF16, kind="ExternalInput").ap()
    WvTa = nc.dram_tensor("l2_wvta", [D, 16 * 65], BF16, kind="ExternalInput").ap()
    WoT = nc.dram_tensor("l2_wot", [D, D], BF16, kind="ExternalInput").ap()

    P = host_consts_common()
    C = Ctx(nc, kb, stack, pref)
    for c in range(NDT):
        C.alloc(f"XN{c}", [128, T], BF16)
    alloc_mha(C)
    if T >= 4096:
        C.sb["WOSTA"] = C.sb["QA"]
        C.sb["WOSTB"] = C.sb["QB"]
        C.sb["DELTA"] = C.sb["KA"].bitcast(F32)
    else:
        C.sb["WOSTA"] = C.alloc("WOSTAd", [128, 4096], BF16)
        C.sb["WOSTB"] = C.alloc("WOSTBd", [128, 4096], BF16)
        C.sb["DELTA"] = C.alloc("DELTAd", [128, 512], F32)
    C.alloc_ps()

    cs = emit_mha_consts(C, P)
    xr = emit_xn_from_hbm(C, xn_hbm, "x0")
    ofs = emit_mha(C, "a", WqT, WkT, WvTa, xr, cs, causal=True, q_scale=0.125)

    def sink(i, n, ap, waits):
        kb.dma("gpsimd", out[128 * i:128 * (i + 1), 512 * n:512 * (n + 1)], ap, "odc", waits)

    emit_out_proj(C, "o", WoT, ofs, sink)
    kb.wait("gpsimd", [("odc", kb.now("odc"))])
    if own:
        kb.finalize(nc, stack)
    return nc, C, stack


def prep_l2_attn_inputs(I):
    return {
        "l2_wqt": as_bf16(I["L2_attn_Wq"].T),
        "l2_wkt": as_bf16(I["L2_attn_Wk"].T),
        "l2_wvta": as_bf16(aug_v_weight(I["L2_attn_Wv"].T.astype(np.float32), 16)),
        "l2_wot": as_bf16(I["L2_attn_Wo"].T),
    }


def build_l0_attn_standalone(env=None):
    nc, kb, stack, pref, own = _env_open(env)
    if env is None:
        xn_hbm = nc.dram_tensor("xn", [T, D], BF16, kind="ExternalInput").ap()
        out = nc.dram_tensor("out", [T, D], F32, kind="ExternalOutput").ap()
    else:
        xn_hbm, out = env["xn"], env["out"]
    WqT = nc.dram_tensor("wqt", [D, D], BF16, kind="ExternalInput").ap()
    WkT = nc.dram_tensor("wkt", [D, D], BF16, kind="ExternalInput").ap()
    WvTa = nc.dram_tensor("wvta", [D, 16 * 65], BF16, kind="ExternalInput").ap()
    WoT = nc.dram_tensor("wot", [D, D], BF16, kind="ExternalInput").ap()
    BQd = nc.dram_tensor("bqd", [128, 8], F32, kind="ExternalInput").ap()
    BKd = nc.dram_tensor("bkd", [128, 8], F32, kind="ExternalInput").ap()
    BVd = nc.dram_tensor("bvd", [8, 128, 130], F32, kind="ExternalInput").ap()
    QNd = nc.dram_tensor("qnd", [128, 1], F32, kind="ExternalInput").ap()
    KNd = nc.dram_tensor("knd", [128, 1], F32, kind="ExternalInput").ap()
    BPd = nc.dram_tensor("bpd", [128, 1024], F32, kind="ExternalInput").ap()

    P = host_consts_common()
    C = Ctx(nc, kb, stack, pref)
    for c in range(NDT):
        C.alloc(f"XN{c}", [128, T], BF16)
    alloc_mha(C)
    alloc_mha_bias(C)
    alloc_mha_rms(C)
    if T >= 4096:
        C.sb["WOSTA"] = C.sb["QA"]
        C.sb["WOSTB"] = C.sb["QB"]
        C.sb["DELTA"] = C.sb["KA"].bitcast(F32)
        C.sb["OPB"] = C.sb["KB_"]
    else:
        C.sb["WOSTA"] = C.alloc("WOSTAd", [128, 4096], BF16)
        C.sb["WOSTB"] = C.alloc("WOSTBd", [128, 4096], BF16)
        C.sb["DELTA"] = C.alloc("DELTAd", [128, 512], F32)
        C.sb["OPB"] = C.alloc("OPBd", [128, 1024], BF16)
    C.alloc_ps()

    cs = emit_mha_consts(C, P)
    xr = emit_xn_from_hbm(C, xn_hbm, "x0")
    ofs = emit_mha(C, "a", WqT, WkT, WvTa, xr, cs, causal=False, q_scale=0.125,
                   biases=dict(bq=BQd, bk=BKd, bv=BVd), rms=(QNd, KNd, 1e-6))

    def sink(i, n, ap, waits):
        kb.dma("gpsimd", out[128 * i:128 * (i + 1), 512 * n:512 * (n + 1)], ap, "odc", waits)

    emit_out_proj(C, "o", WoT, ofs, sink, bias=BPd)
    kb.wait("gpsimd", [("odc", kb.now("odc"))])
    if own:
        kb.finalize(nc, stack)
    return nc, C, stack


def prep_l0_attn_inputs(I):
    WT = I["L0_attn_Wqkv"].T.astype(np.float32)
    b = I["L0_attn_bqkv"].astype(np.float32)
    bv = b[2048:3072]
    bvs = np.zeros((8, 128, 130), np.float32)
    for p_ in range(8):
        for hh in range(2):
            h = 2 * p_ + hh
            bvs[p_, :, 65 * hh:65 * hh + 64] = bv[64 * h:64 * h + 64][None, :]
            bvs[p_, :, 65 * hh + 64] = 1.0
    return {
        "wqt": as_bf16(WT[:, 0:1024]),
        "wkt": as_bf16(WT[:, 1024:2048]),
        "wvta": as_bf16(aug_v_weight(WT[:, 2048:3072], 16)),
        "wot": as_bf16(I["L0_attn_Wproj"].T),
        "bqd": as_f32(b[0:1024].reshape(8, 128).T),
        "bkd": as_f32(b[1024:2048].reshape(8, 128).T),
        "bvd": as_f32(bvs),
        "qnd": as_f32(np.concatenate([I["L0_attn_qn_w"]] * 2)[:, None]),
        "knd": as_f32(np.concatenate([I["L0_attn_kn_w"]] * 2)[:, None]),
        "bpd": as_f32(np.tile(I["L0_attn_bproj"][None, :], (128, 1))),
    }


def build_ln_standalone(nadd, env=None, noxn=False):
    nc, kb, stack, pref, own = _env_open(env)
    if env is None:
        xa = nc.dram_tensor("xa", [T, D], F32, kind="ExternalInput").ap()
        if nadd:
            ab = nc.dram_tensor("ab", [T, D], F32, kind="ExternalInput").ap()
            fb = nc.dram_tensor("fb", [T, D], F32, kind="ExternalInput").ap()
            xout = nc.dram_tensor("xout", [T, D], F32, kind="ExternalOutput").ap()
        xn_hbm = nc.dram_tensor("xn", [T, D], BF16, kind="ExternalOutput").ap()
    else:
        xa, xn_hbm = env["xa"], env.get("xn")
        if nadd:
            ab, fb, xout = env["ab"], env["fb"], env["xout"]
    C = Ctx(nc, kb, stack, pref)
    XT = C.alloc("LNX", [128, 2048], F32)
    if nadd:
        AT = C.alloc("LNA", [128, 2048], F32)
        FT = C.alloc("LNF", [128, 2048], F32)
    C.alloc("LNSQ", [128, 1024], F32)
    ST = C.alloc("LNST", [128, 8], F32)
    XNT = C.alloc("XNTM", [128, 2048], BF16)
    ld, vs, ad, sc, st, so = "ld", "vs", "ad", "sc", "st", "so"
    scb, stb, sob = kb.now(sc), kb.now(st), kb.now(so)
    for i in range(NTT):
        b = i % 2
        sl = slice(1024 * b, 1024 * (b + 1))
        w = []
        if i >= 2:
            if not noxn:
                w.append((sc, scb + i - 1))
            if nadd:
                w.append((so, sob + 16 * (i - 1)))
        kb.dma("sync", XT[:, sl], xa[128 * i:128 * (i + 1), :], ld, w)
        if nadd:
            kb.dma("sync", AT[:, sl], ab[128 * i:128 * (i + 1), :], ld)
            kb.dma("sync", FT[:, sl], fb[128 * i:128 * (i + 1), :], ld)
        lv = kb.now(ld)
        if nadd:
            kb.op("vector", lambda E, sl=sl: E.tensor_tensor(
                XT[:, sl], XT[:, sl], AT[:, sl], ALU.add), waits=[(ld, lv)], inc=ad)
            kb.op("vector", lambda E, sl=sl: E.tensor_tensor(
                XT[:, sl], XT[:, sl], FT[:, sl], ALU.add), inc=ad)
            adv = kb.now(ad)
            kb.dma("gpsimd", xout[128 * i:128 * (i + 1), :], XT[:, sl], so, [(ad, adv)])
            stats_w = [(ad, adv)]
        else:
            stats_w = [(ld, lv)]
        if noxn:
            continue
        ret = ln_stats(C, XT[:, sl], ST, 1e-5, vs, waits=stats_w)
        w2 = [(vs, ret)]
        if i >= 2:
            w2.append((st, stb + 16 * (i - 1)))
        kb.op("vector", lambda E, sl=sl: E.tensor_scalar(
            XNT[:, sl], XT[:, sl], ST[:, 6:7], ST[:, 7:8], ALU.mult, ALU.add),
            waits=w2, inc=sc)
        kb.dma("gpsimd", xn_hbm[128 * i:128 * (i + 1), :], XNT[:, sl], st,
               [(sc, kb.now(sc))])
    fw = [] if noxn else [(st, kb.now(st))]
    if nadd:
        fw.append((so, kb.now(so)))
    kb.wait("gpsimd", fw)
    if own:
        kb.finalize(nc, stack)
    return nc, C, stack


_PROGS = {}


def _get_prog(key, builder):
    if key not in _PROGS:
        _PROGS[key] = builder()
    return _PROGS[key]


def build_fused():
    nc = bass.Bass("TRN2", target_bir_lowering=False)
    kb = KB()
    gstack = ExitStack()
    xin = nc.dram_tensor("xa0", [T, D], F32, kind="ExternalInput").ap()
    xoutd = nc.dram_tensor("xout", [T, D], F32, kind="ExternalOutput").ap()
    xs = [xin] + [nc.dram_tensor(f"x{i}_s", [T, D], F32, kind="Internal").ap()
                  for i in (1, 2, 3)] + [xoutd]
    xns = [nc.dram_tensor(f"xn{i}_s", [T, D], BF16, kind="Internal").ap() for i in range(5)]
    a_s = nc.dram_tensor("a_s", [T, D], F32, kind="Internal").ap()
    f_s = nc.dram_tensor("f_s", [T, D], F32, kind="Internal").ap()
    Cs = []
    state = {"n": 0, "snap": {}}

    def run_phase(pref, fn):
        pstack = ExitStack()
        env = {"nc": nc, "kb": kb, "stack": pstack, "pref": pref}
        _, C, _ = fn(env)
        pstack.close()
        state["n"] += 1
        changed = [("!" + sem, v) for sem, v in kb.cnt.items()
                   if v > state["snap"].get(sem, 0) and sem != "phb"]
        kb.op("sync", lambda E: E.nop(), waits=changed, inc="!phb")
        state["snap"] = dict(kb.cnt)
        for e in KB.ENGS:
            kb.wait(e, [("!phb", state["n"])])
        Cs.append(C)

    attn_b = {0: build_l0_attn_standalone, 1: build_l1_attn_standalone,
              2: build_l2_attn_standalone, 3: build_l3_attn_standalone}
    ffn_b = {2: build_l2_ffn_standalone, 3: build_l3_ffn_standalone}
    for blk in range(4):
        if blk == 0:
            run_phase("q0.", lambda env: build_ln_standalone(
                0, env=dict(env, xa=xs[0], xn=xns[0])))
        else:
            run_phase(f"q{blk}.", lambda env, blk=blk: build_ln_standalone(
                2, env=dict(env, xa=xs[blk - 1], ab=a_s, fb=f_s, xout=xs[blk],
                            xn=xns[blk])))
        run_phase(f"a{blk}.", lambda env, blk=blk: attn_b[blk](
            env=dict(env, xn=xns[blk], out=a_s)))
        if blk in ffn_b:
            run_phase(f"f{blk}.", lambda env, blk=blk: ffn_b[blk](
                env=dict(env, xn=xns[blk], out=f_s)))
        else:
            run_phase(f"f{blk}.", lambda env, blk=blk: build_ffn_standalone(
                env=dict(env, xn=xns[blk], out=f_s), pr=f"f{blk}_"))
    run_phase("q4.", lambda env: build_ln_standalone(
        2, env=dict(env, xa=xs[3], ab=a_s, fb=f_s, xout=xs[4]), noxn=True))
    kb.finalize(nc, gstack)
    return nc, Cs, gstack


def _fused_weights(I):
    W = {}
    W.update(prep_l0_attn_inputs(I))
    W.update(prep_l1_attn_inputs(I))
    W.update(prep_l2_attn_inputs(I))
    W.update(prep_l3_attn_inputs(I))
    W.update(prep_l2_ffn_inputs(I))
    W.update(prep_l3_ffn_inputs(I))
    W.update(prep_ffn_inputs(I, "L0_ffn_", "f0_"))
    W.update(prep_ffn_inputs(I, "L1_ffn_", "f1_"))
    return W


def kernel(**inputs):
    I = inputs
    B = 8
    nc, Cs, _ = _get_prog("fused", build_fused)
    W = _fused_weights(I)
    for C in Cs:
        W.update(C.const_maps())
    in_maps = [dict(W, xa0=np.ascontiguousarray(I["L0_attn_x"][b].astype(np.float32)))
               for b in range(B)]
    res = run_bass_kernel_spmd(nc, in_maps, core_ids=list(range(B)))
    return np.stack([np.asarray(res.results[b]["xout"], np.float32) for b in range(B)])


def kernel_unfused(**inputs):
    I = inputs
    B = 8
    cores = list(range(B))

    def launch(progkey, builder, W, percore, outs):
        nc, C, stack = _get_prog(progkey, builder)
        base = dict(W)
        base.update(C.const_maps())
        in_maps = []
        for b in range(B):
            m = dict(base)
            for k, v in percore.items():
                m[k] = v[b]
            in_maps.append(m)
        res = run_bass_kernel_spmd(nc, in_maps, core_ids=cores)
        return tuple([res.results[b][o] for b in range(B)] for o in outs)

    x = [np.ascontiguousarray(I["L0_attn_x"][b].astype(np.float32)) for b in range(B)]

    ffnW = {0: prep_ffn_inputs(I, "L0_ffn_", "f0_"), 1: prep_ffn_inputs(I, "L1_ffn_", "f0_")}
    attn_launch = {
        0: ("l0a", build_l0_attn_standalone, prep_l0_attn_inputs),
        1: ("l1a", build_l1_attn_standalone, prep_l1_attn_inputs),
        2: ("l2a", build_l2_attn_standalone, prep_l2_attn_inputs),
        3: ("l3a", build_l3_attn_standalone, prep_l3_attn_inputs),
    }
    ffn_launch = {
        2: ("l2f", build_l2_ffn_standalone, prep_l2_ffn_inputs),
        3: ("l3f", build_l3_ffn_standalone, prep_l3_ffn_inputs),
    }

    for blk in range(4):
        if blk == 0:
            (xn,) = launch("ln0", lambda: build_ln_standalone(0), {},
                           {"xa": x}, ["xn"])
        else:
            xn, x = launch("ln2", lambda: build_ln_standalone(2), {},
                           {"xa": x, "ab": a, "fb": f}, ["xn", "xout"])
        xn = [np.ascontiguousarray(t) for t in xn]
        akey, abuild, aprep = attn_launch[blk]
        (a,) = launch(akey, abuild, aprep(I), {"xn": xn}, ["out"])
        if blk in ffn_launch:
            fkey, fbuild, fprep = ffn_launch[blk]
            (f,) = launch(fkey, fbuild, fprep(I), {"xn": xn}, ["out"])
        else:
            (f,) = launch("ffn", build_ffn_standalone, ffnW[blk], {"xn": xn}, ["out"])
    _, x = launch("ln2", lambda: build_ln_standalone(2), {},
                  {"xa": x, "ab": a, "fb": f}, ["xn", "xout"])
    return np.stack([np.asarray(t, dtype=np.float32) for t in x])


def alloc_moe_l2(C):
    C.alloc("RWS", [128, 64], BF16)
    C.alloc("EBt", [128, 8], F32)
    C.alloc("LGt", [128, 16], F32)
    C.alloc("SPt", [128, 256], F32)
    C.alloc("ACTt", [128, 256], F32)
    C.alloc("CW", [128, 288], F32)
    C.alloc("CWT", [9, 4096], BF16)
    C.alloc("TKS", [128, 8], F32)
    C.alloc("W1E", [128, 8192], BF16)
    C.alloc("W2E", [128, 8192], BF16)
    C.alloc("B1E", [128, 8], F32)
    C.alloc("B2S", [9, 1024], BF16)
    C.alloc("H1m", [128, 2048], BF16)
    C.alloc("ACCm", [128, 4096], F32)
    C.alloc("IF32", [128, 128], F32)
    C.alloc("ONES32", [128, 32], F32)


def emit_moe_l2(C, tag, RWT, EB, W1s, B1s, W2s, B2s, xn_ready, sink):
    kb = C.kb
    XN = [C.sb[f"XN{c}"] for c in range(NDT)]
    RWS, EBt, LGt = C.sb["RWS"], C.sb["EBt"], C.sb["LGt"]
    SPt, ACTt, CW, CWT, TKS = C.sb["SPt"], C.sb["ACTt"], C.sb["CW"], C.sb["CWT"], C.sb["TKS"]
    W1E, W2E, B1E, B2S = C.sb["W1E"], C.sb["W2E"], C.sb["B1E"], C.sb["B2S"]
    H1, ACC = C.sb["H1m"], C.sb["ACCm"]
    IF32, ONES32 = C.sb["IF32"], C.sb["ONES32"]
    ld, mm, ev, vx, dc = f"{tag}ld", f"{tag}mm", f"{tag}ev", f"{tag}vx", f"{tag}dc"
    dcb = C.kb.now(dc)
    wl = f"{tag}wl"

    kb.dma("sync", IF32[:, :], C.const("c_IF32", as_f32(np.eye(128))), ld)
    kb.dma("sync", ONES32[:, :], C.const("c_ones32", as_f32(np.ones((128, 32)))), ld)
    for c in range(NDT):
        kb.dma("sync", RWS[:, 8 * c:8 * c + 8], RWT[128 * c:128 * (c + 1), :], ld)
    kb.dma("sync", EBt[:, :], EB[:, :], ld)
    ldv = kb.now(ld)

    for i in range(NTT):
        b = i % 2
        mm_acc(C, C.ps[0:128, 512 * b:512 * b + 8],
               [((), XN[c][:, 128 * i:128 * (i + 1)], RWS[:, 8 * c:8 * c + 8])
                for c in range(NDT)],
               inc=mm, waits=[(ld, ldv), xn_ready, C.fwait(("bk", b))])
        mv = kb.now(mm)
        sl = slice(8 * (i % 2), 8 * (i % 2) + 8)
        kb.op("vector", lambda E, b=b, sl=sl: E.tensor_tensor(
            LGt[:, sl], C.ps[0:128, 512 * b:512 * b + 8], EBt[:, :], ALU.add),
            waits=[(mv and (mm, mv)), C.fwait(("lg", i % 2))], inc=vx)
        kb.op("scalar", lambda E, i=i, sl=sl: E.activation(SPt[:, 8 * i:8 * i + 8], LGt[:, sl], AF.Exp),
              waits=[(vx, kb.now(vx))], inc=ev)
        kb.op("scalar", lambda E, i=i: E.activation(SPt[:, 8 * i:8 * i + 8], SPt[:, 8 * i:8 * i + 8],
                                                    AF.Ln, bias=1.0), inc=ev)
        C.setfree(("bk", b), ev, kb.now(ev))
        C.setfree(("lg", i % 2), ev, kb.now(ev))
    kb.op("scalar", lambda E: E.activation(ACTt[:, 0:8 * NTT], SPt[:, 0:8 * NTT], AF.Sqrt),
          waits=[(ev, kb.now(ev))], inc=ev)
    sq = kb.now(ev)
    for i in range(NTT):
        a = ACTt[:, 8 * i:8 * i + 8]
        kb.op("vector", lambda E, a=a: E.reduce_max(TKS[:, 0:1], a, axis=AX),
              waits=[(ev, sq)], inc=vx)
        kb.wait("vector", [(vx, kb.now(vx))])
        kb.op("vector", lambda E, a=a, i=i: E.tensor_scalar(CW[:, 9 * i:9 * i + 8], a, TKS[:, 0:1],
                                                            None, ALU.is_ge), inc=vx)
        kb.wait("vector", [(vx, kb.now(vx))])
        kb.op("vector", lambda E, a=a, i=i: E.scalar_tensor_tensor(
            SPt[:, 8 * i:8 * i + 8], CW[:, 9 * i:9 * i + 8], -1e30, a, ALU.mult, ALU.add), inc=vx)
        kb.wait("vector", [(vx, kb.now(vx))])
        kb.op("vector", lambda E, i=i: E.reduce_max(TKS[:, 1:2], SPt[:, 8 * i:8 * i + 8], axis=AX),
              inc=vx)
        kb.wait("vector", [(vx, kb.now(vx))])
        kb.op("vector", lambda E, a=a, i=i: E.tensor_scalar(CW[:, 9 * i:9 * i + 8], a, TKS[:, 1:2],
                                                            None, ALU.is_ge), inc=vx)
        kb.wait("vector", [(vx, kb.now(vx))])
        kb.op("vector", lambda E, a=a, i=i: E.tensor_tensor(CW[:, 9 * i:9 * i + 8],
                                                            CW[:, 9 * i:9 * i + 8], a, ALU.mult), inc=vx)
    kb.op("vector", lambda E: E.tensor_copy(CW[:, 8::9], ONES32[:, :]),
          waits=[(ld, ldv)], inc=vx)
    cwv = kb.now(vx)
    for i in range(NTT):
        kb.op("tensor", lambda E, i=i: E.transpose(C.ps[0:9, 3584:3712], CW[:, 9 * i:9 * i + 9],
                                                   IF32[:, :]),
              waits=[(vx, cwv), C.fwait(("bk", 7)), (ld, ldv)], inc=mm)
        kb.op("scalar", lambda E, i=i: E.copy(CWT[0:9, 128 * i:128 * (i + 1)], C.ps[0:9, 3584:3712]),
              waits=[(mm, kb.now(mm))], inc=ev)
        C.setfree(("bk", 7), ev, kb.now(ev))
    cwtv = kb.now(ev)

    for g in range(T // 512):
        gsl = slice(512 * g, 512 * (g + 1))
        for e in range(9):
            par = e % 2
            wo = 8192 // 2 * par
            wv = [C.fwait(("we", par)), (ld, ldv)]
            kb.dma("sync", W1E[:, wo:wo + 4096], W1s[e, :, :], wl, wv)
            kb.dma("sync", W2E[:, wo:wo + 4096], W2s[e, :, :], wl)
            kb.dma("sync", B1E[:, 4 * par:4 * par + 4], B1s[e, :, :], wl)
            if g == 0 and e == 0:
                kb.dma("sync", B2S[0:9, :], B2s[:, :], wl)
            wldv = kb.now(wl)
            for hch in range(4):
                b = hch % 2
                mm_acc(C, C.bank(b),
                       [((), W1E[:, wo + 512 * c + 128 * hch:wo + 512 * c + 128 * hch + 128],
                         XN[c][:, gsl]) for c in range(NDT)],
                       inc=mm, waits=[(wl, wldv), xn_ready, C.fwait(("bk", b))])
                mv = kb.now(mm)
                kb.op("scalar", lambda E, b=b, hch=hch, par=par: E.activation(
                    H1[:, 512 * hch:512 * (hch + 1)], C.bank(b), AF.Silu,
                    bias=B1E[:, 4 * par + hch:4 * par + hch + 1]),
                    waits=[(mm, mv), C.fwait(("h1s", hch))], inc=ev)
                C.setfree(("bk", b), ev, kb.now(ev))
            h1v = kb.now(ev)
            for ti in range(4):
                for n in range(2):
                    b = 4 + (2 * ti + n) % 2
                    mm_acc(C, C.bank(b),
                           [((), H1[:, 512 * hch + 128 * ti:512 * hch + 128 * ti + 128],
                             W2E[:, wo + 1024 * hch + 512 * n:wo + 1024 * hch + 512 * n + 512])
                            for hch in range(4)],
                           inc=mm, waits=[(ev, h1v), C.fwait(("bk", b))])
                    mv = kb.now(mm)
                    o = 1024 * ti + 512 * n
                    cwc = CW[:, 9 * (4 * g + ti) + e:9 * (4 * g + ti) + e + 1]
                    if e == 0:
                        kb.op("vector", lambda E, b=b, o=o, cwc=cwc: E.tensor_scalar(
                            ACC[:, o:o + 512], C.bank(b), cwc, None, ALU.mult),
                            waits=[(mm, mv), (vx, cwv), C.fwait(("acc", o))], inc=vx)
                    else:
                        kb.op("vector", lambda E, b=b, o=o, cwc=cwc: E.scalar_tensor_tensor(
                            ACC[:, o:o + 512], C.bank(b), cwc, ACC[:, o:o + 512], ALU.mult, ALU.add),
                            waits=[(mm, mv)], inc=vx)
                    C.setfree(("bk", b), vx, kb.now(vx))
            C.setfree(("we", par), mm, kb.now(mm))
            C.setfree(("h1s", 0), mm, kb.now(mm))
            C.setfree(("h1s", 1), mm, kb.now(mm))
            C.setfree(("h1s", 2), mm, kb.now(mm))
            C.setfree(("h1s", 3), mm, kb.now(mm))
        for ti in range(4):
            for n in range(2):
                b = 6
                mm_acc(C, C.bank(b),
                       [([(ev, cwtv)], CWT[0:9, 512 * g + 128 * ti:512 * g + 128 * (ti + 1)],
                         B2S[0:9, 512 * n:512 * (n + 1)])],
                       inc=mm, waits=[C.fwait(("bk", 6))])
                mv = kb.now(mm)
                o = 1024 * ti + 512 * n
                kb.op("vector", lambda E, o=o: E.tensor_tensor(
                    ACC[:, o:o + 512], ACC[:, o:o + 512], C.bank(6), ALU.add),
                    waits=[(mm, mv), (vx, kb.now(vx))], inc=vx)
                C.setfree(("bk", 6), vx, kb.now(vx))
        accv = kb.now(vx)
        for ti in range(4):
            i = 4 * g + ti
            sink(i, ACC[:, 1024 * ti:1024 * (ti + 1)], [(vx, accv)])
            C.setfree(("acc", 1024 * ti), dc, dcb + 16 * (i + 1))
            C.setfree(("acc", 1024 * ti + 512), dc, 16 * (i + 1))
    return dc, kb.now(dc)


def build_l2_ffn_standalone(env=None):
    nc, kb, stack, pref, own = _env_open(env)
    if env is None:
        xn_hbm = nc.dram_tensor("xn", [T, D], BF16, kind="ExternalInput").ap()
        out = nc.dram_tensor("out", [T, D], F32, kind="ExternalOutput").ap()
    else:
        xn_hbm, out = env["xn"], env["out"]
    RWT = nc.dram_tensor("l2_rwt", [D, 8], BF16, kind="ExternalInput").ap()
    EB = nc.dram_tensor("l2_eb", [128, 8], F32, kind="ExternalInput").ap()
    W1s = nc.dram_tensor("l2_w1s", [9, 128, 4096], BF16, kind="ExternalInput").ap()
    B1s = nc.dram_tensor("l2_b1s", [9, 128, 4], F32, kind="ExternalInput").ap()
    W2s = nc.dram_tensor("l2_w2s", [9, 128, 4096], BF16, kind="ExternalInput").ap()
    B2s = nc.dram_tensor("l2_b2s", [9, 1024], BF16, kind="ExternalInput").ap()
    C = Ctx(nc, kb, stack, pref)
    for c in range(NDT):
        C.alloc(f"XN{c}", [128, T], BF16)
    alloc_moe_l2(C)
    C.alloc_ps()
    xr = emit_xn_from_hbm(C, xn_hbm, "x0")

    def sink(i, ap, waits):
        kb.dma("gpsimd", out[128 * i:128 * (i + 1), :], ap, "mdc", waits)

    emit_moe_l2(C, "m", RWT, EB, W1s, B1s, W2s, B2s, xr, sink)
    kb.wait("gpsimd", [("mdc", kb.now("mdc"))])
    if own:
        kb.finalize(nc, stack)
    return nc, C, stack


def prep_l2_ffn_inputs(I):
    w1 = np.concatenate([I["L2_ffn_rW1"], I["L2_ffn_sW1"][None]], 0)
    w2 = np.concatenate([I["L2_ffn_rW2"], I["L2_ffn_sW2"][None]], 0)
    b1 = np.concatenate([I["L2_ffn_rb1"], I["L2_ffn_sb1"][None]], 0)
    b2 = np.concatenate([I["L2_ffn_rb2"], I["L2_ffn_sb2"][None]], 0)
    return {
        "l2_rwt": as_bf16(I["L2_ffn_routing_W"].T),
        "l2_eb": as_f32(np.tile(I["L2_ffn_expert_bias"][None, :], (128, 1))),
        "l2_w1s": as_bf16(w1.reshape(9, 8, 128, 512).transpose(0, 2, 1, 3).reshape(9, 128, 4096)),
        "l2_b1s": as_f32(b1.reshape(9, 4, 128).transpose(0, 2, 1)),
        "l2_w2s": as_bf16(w2.reshape(9, 4, 128, 1024).transpose(0, 2, 1, 3).reshape(9, 128, 4096)),
        "l2_b2s": as_bf16(b2),
    }


def ln_stats(C, x_ap, ST, eps, vs, waits=()):
    kb = C.kb
    Dn = 1024
    kb.op("vector", lambda E: E.reduce_sum(ST[:, 0:1], x_ap, axis=AX),
          waits=list(waits), inc=vs)
    kb.op("scalar", lambda E: E.activation(C.sb["LNSQ"][:, :], x_ap, AF.Square,
                                           accum_out=ST[:, 1:2]),
          waits=list(waits), inc=vs)
    kb.wait("vector", [(vs, kb.now(vs))])
    kb.op("vector", lambda E: E.tensor_scalar(ST[:, 2:3], ST[:, 0:1], 1.0 / Dn, None, ALU.mult))
    kb.op("vector", lambda E: E.tensor_scalar(ST[:, 3:4], ST[:, 1:2], 1.0 / Dn, None, ALU.mult))
    kb.op("vector", lambda E: E.tensor_tensor(ST[:, 4:5], ST[:, 2:3], ST[:, 2:3], ALU.mult), inc=vs)
    kb.op("vector", lambda E: E.tensor_tensor(ST[:, 3:4], ST[:, 3:4], ST[:, 4:5], ALU.subtract),
          waits=[(vs, kb.now(vs))], inc=vs)
    epsap, epsw = C.fconst(eps)
    kb.op("scalar", lambda E, epsap=epsap: E.activation(ST[:, 5:6], ST[:, 3:4], AF.Sqrt, bias=epsap),
          waits=[(vs, kb.now(vs)), epsw], inc=vs)
    kb.op("vector", lambda E: E.reciprocal(ST[:, 6:7], ST[:, 5:6]),
          waits=[(vs, kb.now(vs))], inc=vs)
    kb.op("vector", lambda E: E.scalar_tensor_tensor(ST[:, 7:8], ST[:, 2:3], -1.0, ST[:, 6:7],
                                                     ALU.mult, ALU.mult),
          waits=[(vs, kb.now(vs))], inc=vs)
    return kb.now(vs)


def alloc_moe_l3(C):
    C.alloc("RWS", [128, 64], BF16)
    C.alloc("RBt", [128, 8], F32)
    C.alloc("LGt", [128, 256], F32)
    C.alloc("SGt", [128, 256], F32)
    C.alloc("CW", [128, 256], F32)
    C.alloc("CWT", [8, 4096], BF16)
    C.alloc("TKS", [128, 8], F32)
    C.alloc("W1C", [128, 2048], BF16)
    C.alloc("B1L", [128, 64], F32)
    C.alloc("W2R", [128, 32768], BF16)
    C.alloc("H1C", [128, 1024], BF16)
    C.alloc("H1G", [128, 16384], BF16)
    C.alloc("YTl", [128, 1024], F32)
    C.alloc("B2S", [8, 1024], BF16)
    C.alloc("ACF", [128, 1024], F32)
    C.alloc("XNL", [128, 1024], BF16)
    C.alloc("OT", [128, 1024], F32)
    C.alloc("LNW", [128, 1024], F32)
    C.alloc("LNB", [128, 1024], F32)
    C.alloc("IF32", [128, 128], F32)
    C.alloc("ONES32", [128, 32], F32)
    C.alloc("LNSQ", [128, 1024], F32)
    C.alloc("LNST", [128, 8], F32)


def emit_moe_l3(C, tag, RWT, RBb, W1s, B1s, W2s, B2s, LNWd, LNBd,
                xn_hbm, h1_hbm, acc_hbm, xn_ready, sink):
    kb = C.kb
    XN = [C.sb[f"XN{c}"] for c in range(NDT)]
    RWS, RBt, LGt, SGt = C.sb["RWS"], C.sb["RBt"], C.sb["LGt"], C.sb["SGt"]
    CW, CWT, TKS = C.sb["CW"], C.sb["CWT"], C.sb["TKS"]
    W1C, B1L, W2R = C.sb["W1C"], C.sb["B1L"], C.sb["W2R"]
    H1C, H1G, YT = C.sb["H1C"], C.sb["H1G"], C.sb["YTl"]
    B2S, ACF, XNL, OT = C.sb["B2S"], C.sb["ACF"], C.sb["XNL"], C.sb["OT"]
    LNW, LNB, IF32 = C.sb["LNW"], C.sb["LNB"], C.sb["IF32"]
    ST = C.sb["LNST"]
    ld, mm, ev, vx, wl = f"{tag}ld", f"{tag}mm", f"{tag}ev", f"{tag}vx", f"{tag}wl"
    sp, ac, dc = f"{tag}sp", f"{tag}ac", f"{tag}dc"
    dcb = C.kb.now(dc)

    kb.dma("sync", IF32[:, :], C.const("c_IF32", as_f32(np.eye(128))), ld)
    for c in range(NDT):
        kb.dma("sync", RWS[:, 8 * c:8 * c + 8], RWT[128 * c:128 * (c + 1), :], ld)
    kb.dma("sync", RBt[:, :], RBb[:, :], ld)
    kb.dma("sync", B2S[0:8, :], B2s[:, :], ld)
    kb.dma("sync", LNW[:, :], LNWd[:, :], ld)
    kb.dma("sync", LNB[:, :], LNBd[:, :], ld)
    ldv = kb.now(ld)

    for i in range(NTT):
        b = i % 2
        mm_acc(C, C.ps[0:128, 512 * b:512 * b + 8],
               [((), XN[c][:, 128 * i:128 * (i + 1)], RWS[:, 8 * c:8 * c + 8])
                for c in range(NDT)],
               inc=mm, waits=[(ld, ldv), xn_ready, C.fwait(("bk", b))])
        mv = kb.now(mm)
        kb.op("vector", lambda E, b=b, i=i: E.tensor_tensor(
            LGt[:, 8 * i:8 * i + 8], C.ps[0:128, 512 * b:512 * b + 8], RBt[:, :], ALU.add),
            waits=[(mm, mv)], inc=vx)
        C.setfree(("bk", b), vx, kb.now(vx))
    for i in range(NTT):
        lg = LGt[:, 8 * i:8 * i + 8]
        kb.op("vector", lambda E, lg=lg: E.reduce_max(TKS[:, 0:1], lg, axis=AX), inc=vx)
        kb.wait("vector", [(vx, kb.now(vx))])
        kb.op("vector", lambda E, lg=lg, i=i: E.tensor_scalar(CW[:, 8 * i:8 * i + 8], lg, TKS[:, 0:1],
                                                              None, ALU.is_ge), inc=vx)
        kb.wait("vector", [(vx, kb.now(vx))])
        kb.op("vector", lambda E, lg=lg, i=i: E.scalar_tensor_tensor(
            SGt[:, 8 * i:8 * i + 8], CW[:, 8 * i:8 * i + 8], -1e30, lg, ALU.mult, ALU.add), inc=vx)
        kb.wait("vector", [(vx, kb.now(vx))])
        kb.op("vector", lambda E, i=i: E.reduce_max(TKS[:, 1:2], SGt[:, 8 * i:8 * i + 8], axis=AX),
              inc=vx)
        kb.wait("vector", [(vx, kb.now(vx))])
        kb.op("vector", lambda E: E.tensor_tensor(TKS[:, 2:3], TKS[:, 0:1], TKS[:, 1:2], ALU.add),
              inc=vx)
        kb.wait("vector", [(vx, kb.now(vx))])
        kb.op("vector", lambda E, lg=lg, i=i: E.tensor_scalar(SGt[:, 8 * i:8 * i + 8], lg, 2.0,
                                                              TKS[:, 2:3], ALU.mult, ALU.subtract),
              inc=vx)
        kb.op("scalar", lambda E, i=i: E.activation(SGt[:, 8 * i:8 * i + 8], SGt[:, 8 * i:8 * i + 8],
                                                    AF.Sigmoid),
              waits=[(vx, kb.now(vx))], inc=ev)
        kb.op("vector", lambda E, lg=lg, i=i: E.tensor_scalar(CW[:, 8 * i:8 * i + 8], lg, TKS[:, 1:2],
                                                              None, ALU.is_ge),
              waits=[(ev, kb.now(ev))], inc=vx)
        kb.wait("vector", [(vx, kb.now(vx))])
        kb.op("vector", lambda E, i=i: E.tensor_tensor(CW[:, 8 * i:8 * i + 8], CW[:, 8 * i:8 * i + 8],
                                                       SGt[:, 8 * i:8 * i + 8], ALU.mult), inc=vx)
    cwv = kb.now(vx)
    for i in range(NTT):
        kb.op("tensor", lambda E, i=i: E.transpose(C.ps[0:8, 3584:3712], CW[:, 8 * i:8 * i + 8],
                                                   IF32[:, :]),
              waits=[(vx, cwv), C.fwait(("bk", 7)), (ld, ldv)], inc=mm)
        kb.op("scalar", lambda E, i=i: E.copy(CWT[0:8, 128 * i:128 * (i + 1)], C.ps[0:8, 3584:3712]),
              waits=[(mm, kb.now(mm))], inc=ev)
        C.setfree(("bk", 7), ev, kb.now(ev))
    cwtv = kb.now(ev)

    for e in range(8):
        w2w = [C.fwait("w2r")] if e > 0 else []
        for hch in range(32):
            kb.dma("gpsimd", W2R[:, 1024 * hch:1024 * (hch + 1)], W2s[e, hch, :, :], sp,
                   w2w if hch == 0 else ())
        w2v = kb.now(sp)
        kb.dma("sync", B1L[:, 32 * (e % 2):32 * (e % 2) + 32], B1s[e, :, :], wl,
               [C.fwait(("b1", e % 2))])
        b1v = kb.now(wl)
        for hch in range(32):
            par = hch % 2
            kb.dma("sync", W1C[:, 1024 * par:1024 * (par + 1)], W1s[e, hch, :, :], wl,
                   [C.fwait(("w1c", par)), C.fwait("h1r")])
            wv = kb.now(wl)
            for g in range(T // 512):
                b = g % 2
                mm_acc(C, C.bank(b),
                       [((), W1C[:, 1024 * par + 128 * c:1024 * par + 128 * c + 128],
                         XN[c][:, 512 * g:512 * (g + 1)]) for c in range(NDT)],
                       inc=mm, waits=[(wl, wv), xn_ready, C.fwait(("bk", b))])
                mv = kb.now(mm)
                sl = (g % 2) * 512
                kb.op("scalar", lambda E, b=b, sl=sl, e=e, hch=hch: E.activation(
                    H1C[:, sl:sl + 512], C.bank(b), AF.Gelu,
                    bias=B1L[:, 32 * (e % 2) + hch:32 * (e % 2) + hch + 1]),
                    waits=[(mm, mv), (wl, b1v), C.fwait(("h1c", sl))], inc=ev)
                C.setfree(("bk", b), ev, kb.now(ev))
                kb.dma("gpsimd", h1_hbm[hch, :, 512 * g:512 * (g + 1)], H1C[:, sl:sl + 512], sp,
                       [(ev, kb.now(ev))])
                C.setfree(("h1c", sl), sp, kb.now(sp))
            C.setfree(("w1c", par), mm, kb.now(mm))
        C.setfree(("b1", e % 2), ev, kb.now(ev))
        h1done = kb.now(sp)
        for g in range(T // 512):
            for hch in range(32):
                kb.dma("sync", H1G[:, 512 * hch:512 * (hch + 1)],
                       h1_hbm[hch, :, 512 * g:512 * (g + 1)], wl,
                       [(sp, h1done), C.fwait("h1g")] if hch == 0 else ())
            hgv = kb.now(wl)
            for ti in range(4):
                for n in range(2):
                    b = 4 + (2 * ti + n) % 2
                    mm_acc(C, C.bank(b),
                           [((), H1G[:, 512 * hch + 128 * ti:512 * hch + 128 * ti + 128],
                             W2R[:, 1024 * hch + 512 * n:1024 * hch + 512 * n + 512])
                            for hch in range(32)],
                           inc=mm, waits=[(wl, hgv), (sp, w2v), C.fwait(("bk", b))])
                    mv = kb.now(mm)
                    i = 4 * g + ti
                    cwc = CW[:, 8 * i + e:8 * i + e + 1]
                    sl = (2 * ti + n) % 2 * 512
                    kb.op("vector", lambda E, b=b, sl=sl, cwc=cwc: E.tensor_scalar(
                        YT[:, sl:sl + 512], C.bank(b), cwc, None, ALU.mult),
                        waits=[(mm, mv), (vx, cwv), C.fwait(("yt", sl))], inc=vx)
                    C.setfree(("bk", b), vx, kb.now(vx))
                    if e == 0:
                        kb.dma("gpsimd", acc_hbm[128 * i:128 * (i + 1), 512 * n:512 * (n + 1)],
                               YT[:, sl:sl + 512], ac, [(vx, kb.now(vx))])
                    else:
                        kb.op("gpsimd", lambda E, i=i, n=n, sl=sl: E.dma_start(
                            out=acc_hbm[128 * i:128 * (i + 1), 512 * n:512 * (n + 1)],
                            in_=YT[:, sl:sl + 512], accum_op=ALU.add),
                            waits=[(vx, kb.now(vx))], inc=ac, inc_n=16)
                    C.setfree(("yt", sl), ac, kb.now(ac))
            C.setfree("h1g", mm, kb.now(mm))
        C.setfree("w2r", mm, kb.now(mm))
        C.setfree("h1r", wl, kb.now(wl))
    accv = kb.now(ac)

    for i in range(NTT):
        lv = kb.dma("sync", ACF[:, :], acc_hbm[128 * i:128 * (i + 1), :], ld,
                    [(ac, accv), C.fwait("acf")])
        xv = kb.dma("sync", XNL[:, :], xn_hbm[128 * i:128 * (i + 1), :], ld)
        for n in range(2):
            mm_acc(C, C.ps[0:128, 512 * 6 + 512 * 0:512 * 6 + 512],
                   [([(ev, cwtv)], CWT[0:8, 128 * i:128 * (i + 1)], B2S[0:8, 512 * n:512 * (n + 1)])],
                   inc=mm, waits=[C.fwait(("bk", 6))])
            mv = kb.now(mm)
            kb.op("vector", lambda E, n=n: E.tensor_tensor(
                ACF[:, 512 * n:512 * (n + 1)], ACF[:, 512 * n:512 * (n + 1)],
                C.ps[0:128, 512 * 6:512 * 6 + 512], ALU.add),
                waits=[(mm, mv), (ld, lv)], inc=vx)
            C.setfree(("bk", 6), vx, kb.now(vx))
        kb.op("vector", lambda E: E.tensor_tensor(ACF[:, :], ACF[:, :], XNL[:, :], ALU.add),
              waits=[(ld, xv)], inc=vx)
        vs = ln_stats(C, ACF[:, :], ST, 1e-12, vx, waits=[(vx, kb.now(vx))])
        kb.op("vector", lambda E: E.tensor_scalar(OT[:, :], ACF[:, :], ST[:, 6:7], ST[:, 7:8],
                                                  ALU.mult, ALU.add),
              waits=[(vx, vs), C.fwait("ot")], inc=ev)
        kb.op("vector", lambda E: E.tensor_tensor(OT[:, :], OT[:, :], LNW[:, :], ALU.mult),
              waits=[(ev, kb.now(ev)), (ld, ldv)], inc=vx)
        kb.op("vector", lambda E: E.tensor_tensor(OT[:, :], OT[:, :], LNB[:, :], ALU.add), inc=vx)
        sink(i, OT[:, :], [(vx, kb.now(vx))])
        C.setfree("ot", dc, dcb + 16 * (i + 1))
        C.setfree("acf", dc, dcb + 16 * (i + 1))
    return dc, kb.now(dc)


def build_l3_ffn_standalone(env=None):
    nc, kb, stack, pref, own = _env_open(env)
    if env is None:
        xn_hbm = nc.dram_tensor("xn", [T, D], BF16, kind="ExternalInput").ap()
        out = nc.dram_tensor("out", [T, D], F32, kind="ExternalOutput").ap()
    else:
        xn_hbm, out = env["xn"], env["out"]
    RWT = nc.dram_tensor("rwt", [D, 8], BF16, kind="ExternalInput").ap()
    RBb = nc.dram_tensor("rbb", [128, 8], F32, kind="ExternalInput").ap()
    W1s = nc.dram_tensor("w1s", [8, 32, 128, 1024], BF16, kind="ExternalInput").ap()
    B1s = nc.dram_tensor("b1s", [8, 128, 32], F32, kind="ExternalInput").ap()
    W2s = nc.dram_tensor("w2s", [8, 32, 128, 1024], BF16, kind="ExternalInput").ap()
    B2s = nc.dram_tensor("b2s", [8, 1024], BF16, kind="ExternalInput").ap()
    LNWd = nc.dram_tensor("lnw", [128, 1024], F32, kind="ExternalInput").ap()
    LNBd = nc.dram_tensor("lnb", [128, 1024], F32, kind="ExternalInput").ap()
    h1_hbm = nc.dram_tensor(pref + "h1s", [32, 128, 4096], BF16, kind="Internal").ap()
    acc_hbm = nc.dram_tensor(pref + "accs", [T, D], F32, kind="Internal").ap()
    C = Ctx(nc, kb, stack, pref)
    for c in range(NDT):
        C.alloc(f"XN{c}", [128, T], BF16)
    alloc_moe_l3(C)
    C.alloc_ps()
    xr = emit_xn_from_hbm(C, xn_hbm, "x0")

    def sink(i, ap, waits):
        kb.dma("gpsimd", out[128 * i:128 * (i + 1), :], ap, "fdc", waits)

    emit_moe_l3(C, "f", RWT, RBb, W1s, B1s, W2s, B2s, LNWd, LNBd,
                xn_hbm, h1_hbm, acc_hbm, xr, sink)
    kb.wait("gpsimd", [("fdc", kb.now("fdc"))])
    if own:
        kb.finalize(nc, stack)
    return nc, C, stack


def prep_l3_ffn_inputs(I):
    w1 = I["L3_ffn_w1"].astype(np.float32)
    w1T = w1.transpose(0, 2, 1)
    w1s = w1T.reshape(8, 8, 128, 32, 128).transpose(0, 3, 2, 1, 4).reshape(8, 32, 128, 1024)
    w2 = I["L3_ffn_w2"].astype(np.float32)
    w2s = w2.transpose(0, 2, 1).reshape(8, 32, 128, 1024)
    return {
        "rwt": as_bf16(I["L3_ffn_router_w"].T),
        "rbb": as_f32(np.tile(I["L3_ffn_router_b"][None, :], (128, 1))),
        "w1s": as_bf16(w1s),
        "b1s": as_f32(I["L3_ffn_b1"].reshape(8, 32, 128).transpose(0, 2, 1)),
        "w2s": as_bf16(w2s),
        "b2s": as_bf16(I["L3_ffn_b2"]),
        "lnw": as_f32(np.tile(I["L3_ffn_ln_w"][None, :], (128, 1))),
        "lnb": as_f32(np.tile(I["L3_ffn_ln_b"][None, :], (128, 1))),
    }


def alloc_ssm(C):
    C.alloc("WGB", [128, 1024], BF16)
    C.alloc("BGB", [128, 64], F32)
    C.alloc("GTs", [128, 128], F32)
    C.alloc("UTs", [128, 128], F32)
    C.alloc("UF", [128, T], BF16)
    C.alloc("ABC", [64, T], F32)
    C.alloc("STF", [64, T], BF16)
    C.alloc("ALC", [64, 2], F32)
    C.alloc("WCS", [64, 1024], BF16)
    C.alloc("HG", [128, 4096], F32)
    C.alloc("HSQ2", [128, 4096], F32)
    C.alloc("HN", [128, 4096], BF16)
    C.alloc("WOS", [128, 8192], BF16)
    C.alloc("RSB", [128, 512], F32)
    C.alloc("MRB", [128, 512], F32)
    C.alloc("STR", [1, 2048], F32)
    C.alloc("GAM", [128, 8], F32)
    C.alloc("BET", [128, 8], F32)
    C.alloc("DC1", [128, 8], F32)
    C.alloc("BOUT", [128, 1024], F32)
    C.alloc("DELTA3", [128, 1024], F32)
    C.alloc("ONESC", [128, 1], F32)
    C.alloc("ONESR", [1, 128], F32)


def emit_ssm(C, tag, WGBd, BGBd, ALd, WCTd, Dd, WOTd, BOUTd, GAMd, BETd,
             u_hbm, xn_ready, sink):
    kb = C.kb
    XN = [C.sb[f"XN{c}"] for c in range(NDT)]
    WGB, BGB, GT, UT = C.sb["WGB"], C.sb["BGB"], C.sb["GTs"], C.sb["UTs"]
    UF, ABC, STF, ALC = C.sb["UF"], C.sb["ABC"], C.sb["STF"], C.sb["ALC"]
    WCS, HG, HSQ, HN = C.sb["WCS"], C.sb["HG"], C.sb["HSQ2"], C.sb["HN"]
    WOS, RSB, MRB, STR = C.sb["WOS"], C.sb["RSB"], C.sb["MRB"], C.sb["STR"]
    GAM, BET, DC1, BOUT = C.sb["GAM"], C.sb["BET"], C.sb["DC1"], C.sb["BOUT"]
    ONESC, ONESR = C.sb["ONESC"], C.sb["ONESR"]
    dcb = C.kb.now(f"{tag}dc")
    ld, mm, ev, vx, st, dc = (f"{tag}ld", f"{tag}mm", f"{tag}ev", f"{tag}vx",
                              f"{tag}st", f"{tag}dc")
    for c in range(NDT):
        kb.dma("sync", WGB[:, 128 * c:128 * (c + 1)], WGBd[128 * c:128 * (c + 1), :], ld)
    kb.dma("sync", BGB[:, :], BGBd[:, :], ld)
    kb.dma("sync", ALC[0:64, 0:1], ALd[:, :], ld)
    kb.dma("sync", WCS[0:64, :], WCTd[:, :], ld)
    kb.dma("sync", C.sb["DC1"][:, :], Dd[:, :], ld)
    for c in range(NDT):
        kb.dma("sync", WOS[:, 1024 * c:1024 * (c + 1)],
               WOTd[128 * c:128 * (c + 1), :], ld)
    kb.dma("sync", BOUT[:, :], BOUTd[:, :], ld)
    kb.dma("sync", GAM[:, :], GAMd[:, :], ld)
    kb.dma("sync", BET[:, :], BETd[:, :], ld)
    kb.dma("sync", ONESC[:, :], C.const("c_onescol", as_f32(np.ones((128, 1)))), ld)
    kb.dma("sync", ONESR[0:1, :], C.const("c_onesrow", as_f32(np.ones((1, 128)))), ld)
    ldv = kb.now(ld)
    kb.op("vector", lambda E: E.tensor_scalar(DC1[:, :], DC1[:, :], 1.0, None, ALU.add),
          waits=[(ld, ldv)], inc=vx)
    d1v = kb.now(vx)

    for i in range(NTT):
        b = i % 2
        sl = (i % 2) * 64
        mm_acc(C, C.ps[0:128, 512 * b:512 * b + 128],
               [((), XN[c][:, 128 * i:128 * (i + 1)], WGB[:, 128 * c:128 * (c + 1)])
                for c in range(NDT)],
               inc=mm, waits=[(ld, ldv), xn_ready, C.fwait(("bk", b))])
        mv = kb.now(mm)
        kb.op("vector", lambda E, b=b, sl=sl: E.tensor_tensor(
            GT[:, sl:sl + 64], C.ps[0:128, 512 * b:512 * b + 64], BGB[:, :], ALU.add),
            waits=[(mm, mv), C.fwait(("gt", sl))], inc=vx)
        kb.op("scalar", lambda E, sl=sl: E.activation(GT[:, sl:sl + 64], GT[:, sl:sl + 64], AF.Sigmoid),
              waits=[(vx, kb.now(vx))], inc=ev)
        kb.op("vector", lambda E, b=b, sl=sl: E.tensor_tensor(
            UT[:, sl:sl + 64], GT[:, sl:sl + 64], C.ps[0:128, 512 * b + 64:512 * b + 128], ALU.mult),
            waits=[(ev, kb.now(ev)), C.fwait(("ut", sl))], inc=vx)
        C.setfree(("bk", b), vx, kb.now(vx))
        C.setfree(("gt", sl), vx, kb.now(vx))
        kb.dma("gpsimd", u_hbm[128 * i:128 * (i + 1), 0:64], UT[:, sl:sl + 64], st,
               [(vx, kb.now(vx))])
        C.setfree(("ut", sl), st, kb.now(st))
    ustv = kb.now(st)
    kb.dmat("sync", UF[0:128, :], u_hbm[:, :], ld, [(st, ustv)])
    ufv = kb.now(ld)
    kb.op("scalar", lambda E: E.activation(ALC[0:64, 1:2], ALC[0:64, 0:1], AF.Exp),
          waits=[(ld, ldv)], inc=ev)
    kb.op("vector", lambda E: E.tensor_scalar(ABC[0:64, :], UF[0:64, :], 0.0,
                                              ALC[0:64, 1:2], ALU.mult, ALU.add),
          waits=[(ev, kb.now(ev)), (ld, ufv)], inc=vx)
    kb.wait("vector", [(vx, kb.now(vx))])
    kb.op("vector", lambda E: E.tensor_tensor_scan(STF[0:64, :], ABC[0:64, :], UF[0:64, :],
                                                   0.0, ALU.mult, ALU.add), inc=vx)
    scv = kb.now(vx)

    for tc in range(T // 512):
        tsl = slice(512 * tc, 512 * (tc + 1))
        for hc in range(NDT):
            b = hc % 2
            mm_acc(C, C.bank(b),
                   [((), WCS[0:64, 128 * hc:128 * (hc + 1)], STF[0:64, tsl])],
                   inc=mm, waits=[(vx, scv), (ld, ldv), C.fwait(("bk", b))])
            mv = kb.now(mm)
            kb.op("vector", lambda E, b=b, hc=hc, tsl=tsl: E.scalar_tensor_tensor(
                HG[:, 512 * hc:512 * (hc + 1)], XN[hc][:, tsl], DC1[:, hc:hc + 1],
                C.bank(b), ALU.mult, ALU.add),
                waits=[(mm, mv), (vx, d1v), C.fwait(("hg", hc))], inc=vx)
            C.setfree(("bk", b), vx, kb.now(vx))
        hgv = kb.now(vx)
        kb.op("scalar", lambda E: E.activation(HSQ[:, :], HG[:, :], AF.Square),
              waits=[(vx, hgv), C.fwait("hsq")], inc=ev)
        sqv = kb.now(ev)
        mm_acc(C, C.ps[0:1, 512 * 4:512 * 4 + 512],
               [((), ONESC[:, :], HG[:, 512 * hc:512 * (hc + 1)]) for hc in range(NDT)],
               inc=mm, waits=[(vx, hgv), C.fwait(("bk", 4))])
        mm_acc(C, C.ps[0:1, 512 * 5:512 * 5 + 512],
               [((), ONESC[:, :], HSQ[:, 512 * hc:512 * (hc + 1)]) for hc in range(NDT)],
               inc=mm, waits=[(ev, sqv), C.fwait(("bk", 5))])
        smv = kb.now(mm)
        kb.op("vector", lambda E: E.tensor_scalar(STR[0:1, 0:512], C.ps[0:1, 512 * 4:512 * 4 + 512],
                                                  1.0 / D, None, ALU.mult),
              waits=[(mm, smv)], inc=vx)
        kb.op("vector", lambda E: E.tensor_scalar(STR[0:1, 512:1024], C.ps[0:1, 512 * 5:512 * 5 + 512],
                                                  1.0 / D, None, ALU.mult), inc=vx)
        C.setfree(("bk", 4), vx, kb.now(vx))
        C.setfree(("bk", 5), vx, kb.now(vx))
        kb.op("vector", lambda E: E.tensor_tensor(STR[0:1, 1024:1536], STR[0:1, 0:512],
                                                  STR[0:1, 0:512], ALU.mult), inc=vx)
        kb.wait("vector", [(vx, kb.now(vx))])
        kb.op("vector", lambda E: E.tensor_tensor(STR[0:1, 512:1024], STR[0:1, 512:1024],
                                                  STR[0:1, 1024:1536], ALU.subtract), inc=vx)
        epsap, epsw = C.fconst(1e-5)
        kb.op("scalar", lambda E, epsap=epsap: E.activation(STR[0:1, 1024:1536], STR[0:1, 512:1024],
                                                            AF.Sqrt, bias=epsap[0:1, :]),
              waits=[(vx, kb.now(vx)), epsw], inc=ev)
        kb.op("vector", lambda E: E.reciprocal(STR[0:1, 512:1024], STR[0:1, 1024:1536]),
              waits=[(ev, kb.now(ev))], inc=vx)
        kb.wait("vector", [(vx, kb.now(vx))])
        kb.op("vector", lambda E: E.scalar_tensor_tensor(STR[0:1, 0:512], STR[0:1, 0:512], -1.0,
                                                         STR[0:1, 512:1024], ALU.mult, ALU.mult),
              inc=vx)
        stv = kb.now(vx)
        mm_acc(C, C.ps[0:128, 512 * 6:512 * 6 + 512],
               [([(vx, stv)], ONESR[0:1, :], STR[0:1, 512:1024])], inc=mm,
               waits=[C.fwait(("bk", 6))])
        mm_acc(C, C.ps[0:128, 512 * 7:512 * 7 + 512],
               [((), ONESR[0:1, :], STR[0:1, 0:512])], inc=mm,
               waits=[C.fwait(("bk", 7))])
        bcv = kb.now(mm)
        kb.op("scalar", lambda E: E.copy(RSB[:, :], C.ps[0:128, 512 * 6:512 * 6 + 512]),
              waits=[(mm, bcv), C.fwait("rsb")], inc=ev)
        kb.op("scalar", lambda E: E.copy(MRB[:, :], C.ps[0:128, 512 * 7:512 * 7 + 512]), inc=ev)
        bcev = kb.now(ev)
        C.setfree(("bk", 6), ev, bcev)
        C.setfree(("bk", 7), ev, bcev)
        for hc in range(NDT):
            hsl = slice(512 * hc, 512 * (hc + 1))
            kb.op("vector", lambda E, hsl=hsl: E.tensor_tensor(HSQ[:, hsl], HG[:, hsl], RSB[:, :],
                                                               ALU.mult),
                  waits=[(ev, bcev), (ev, sqv)], inc=vx)
            kb.op("vector", lambda E, hsl=hsl: E.tensor_tensor(HSQ[:, hsl], HSQ[:, hsl], MRB[:, :],
                                                               ALU.add), inc=vx)
            kb.op("vector", lambda E, hsl=hsl, hc=hc: E.tensor_scalar(
                HN[:, hsl], HSQ[:, hsl], GAM[:, hc:hc + 1], BET[:, hc:hc + 1], ALU.mult, ALU.add),
                waits=[(ld, ldv), (vx, kb.now(vx))], inc=vx)
            C.setfree(("hg", hc), vx, kb.now(vx))
        hnv = kb.now(vx)
        C.setfree("hsq", vx, hnv)
        C.setfree("rsb", vx, hnv)
        for tt in range(4):
            i = 4 * tc + tt
            for n in range(2):
                b = n
                mm_acc(C, C.bank(b),
                       [((), HN[:, 512 * hc + 128 * tt:512 * hc + 128 * tt + 128],
                         WOS[:, 1024 * hc + 512 * n:1024 * hc + 512 * n + 512])
                        for hc in range(NDT)],
                       inc=mm, waits=[(vx, hnv), (ld, ldv), C.fwait(("bk", b))])
            mv = kb.now(mm)
            kb.op("vector", lambda E: E.tensor_tensor(C.sb["DELTA3"][:, 0:512],
                                                      C.ps[0:128, 0:512], BOUT[:, 0:512], ALU.add),
                  waits=[(mm, mv), (dc, dcb + 16 * i)], inc=vx)
            kb.op("vector", lambda E: E.tensor_tensor(C.sb["DELTA3"][:, 512:1024],
                                                      C.ps[0:128, 512:1024], BOUT[:, 512:1024],
                                                      ALU.add), inc=vx)
            C.setfree(("bk", 0), vx, kb.now(vx))
            C.setfree(("bk", 1), vx, kb.now(vx))
            sink(i, C.sb["DELTA3"][:, 0:1024], [(vx, kb.now(vx))])
    return dc, kb.now(dc)


def build_l3_attn_standalone(env=None):
    nc, kb, stack, pref, own = _env_open(env)
    if env is None:
        xn_hbm = nc.dram_tensor("xn", [T, D], BF16, kind="ExternalInput").ap()
        out = nc.dram_tensor("out", [T, D], F32, kind="ExternalOutput").ap()
    else:
        xn_hbm, out = env["xn"], env["out"]
    WGBd = nc.dram_tensor("wgb", [D, 128], BF16, kind="ExternalInput").ap()
    BGBd = nc.dram_tensor("bgb", [128, 64], F32, kind="ExternalInput").ap()
    ALd = nc.dram_tensor("alog", [64, 1], F32, kind="ExternalInput").ap()
    WCTd = nc.dram_tensor("wct", [64, D], BF16, kind="ExternalInput").ap()
    Dd = nc.dram_tensor("dvec", [128, 8], F32, kind="ExternalInput").ap()
    WOTd = nc.dram_tensor("wot3", [D, D], BF16, kind="ExternalInput").ap()
    BOUTd = nc.dram_tensor("bout", [128, 1024], F32, kind="ExternalInput").ap()
    GAMd = nc.dram_tensor("gam", [128, 8], F32, kind="ExternalInput").ap()
    BETd = nc.dram_tensor("bet", [128, 8], F32, kind="ExternalInput").ap()
    u_hbm = nc.dram_tensor(pref + "u_s", [T, 128], BF16, kind="Internal").ap()
    C = Ctx(nc, kb, stack, pref)
    for c in range(NDT):
        C.alloc(f"XN{c}", [128, T], BF16)
    alloc_ssm(C)
    C.alloc_ps()
    xr = emit_xn_from_hbm(C, xn_hbm, "x0")

    def sink(i, ap, waits):
        kb.dma("gpsimd", out[128 * i:128 * (i + 1), :], ap, "sdc", waits)

    emit_ssm(C, "s", WGBd, BGBd, ALd, WCTd, Dd, WOTd, BOUTd, GAMd, BETd,
             u_hbm, xr, sink)
    kb.wait("gpsimd", [("sdc", kb.now("sdc"))])
    if own:
        kb.finalize(nc, stack)
    return nc, C, stack


def prep_l3_attn_inputs(I):
    return {
        "wgb": as_bf16(np.concatenate([I["L3_attn_Wg"].T, I["L3_attn_WB"].T], axis=1)),
        "bgb": as_f32(np.tile(I["L3_attn_bg"][None, :], (128, 1))),
        "alog": as_f32(I["L3_attn_A_log"][:, None]),
        "wct": as_bf16(I["L3_attn_WC"].T),
        "dvec": as_f32(I["L3_attn_D"].reshape(8, 128).T),
        "wot3": as_bf16(I["L3_attn_Wout"].T),
        "bout": as_f32(np.tile(I["L3_attn_bout"][None, :], (128, 1))),
        "gam": as_f32(I["L3_attn_gamma"].reshape(8, 128).T),
        "bet": as_f32(I["L3_attn_beta"].reshape(8, 128).T),
    }

W1M = ([(128 * j, 128 if j < 21 else 42) for j in range(22)] +
       [(2730 + 128 * j, 128 if j < 21 else 42) for j in range(22)])
W2K = [(128 * j, 128 if j < 21 else 42) for j in range(22)]


class TV:
    def __init__(self, h, coff=0, dtype=None, ncols=None):
        self.h = h.bitcast(dtype) if dtype is not None else h
        self.coff = coff
        self.ncols = ncols if ncols is not None else (self.h.shape[1] - coff)

    def __getitem__(self, idx):
        pp, cc = idx
        st = (cc.start or 0) + self.coff
        sp = (self.ncols if cc.stop is None else cc.stop) + self.coff
        return self.h[pp, st:sp]


def vap(tile, off, dims):
    if isinstance(tile, TV):
        off = off + tile.coff
        tile = tile.h
    t = tile[:, 0:1]
    fw = t.ap[0][0]
    return bass.AP(t.tensor, off, [[fw, 128]] + dims)


def alloc_ffn(C):
    C.alloc("TBL", [128, T], BF16)
    bfa = C.alloc("BFA", [128, 4096], BF16)
    C.sb["XIB"] = TV(bfa, 0)
    C.sb["QR"] = TV(bfa, 1024, F32, 512)
    C.sb["Q1"] = TV(bfa, 1536, F32, 512)
    C.sb["LNH"] = TV(bfa, 0)
    XNs = [C.sb[f"XN{c}"] for c in range(NDT)]
    if T >= 4096:
        C.alloc("WS", [128, 4096], BF16)
        C.sb["SQB"] = TV(C.sb["OF7"], 0)
        C.sb["W2SA"] = TV(XNs[0], 2080, None, 1920)
        C.sb["W2SB"] = TV(XNs[1], 2080, None, 896)
        C.sb["ST0"] = TV(XNs[2], 1040, F32, 512)
        C.sb["ST1"] = TV(XNs[3], 1040, F32, 512)
        C.sb["ST2"] = TV(XNs[4], 1040, F32, 512)
        C.sb["RSB0"] = TV(XNs[5], 1040, F32, 512)
        C.sb["RSB1"] = TV(XNs[6], 1040, F32, 512)
        C.sb["RW0"] = TV(XNs[5], 1552, F32, 256)
        C.sb["RW1v"] = TV(XNs[6], 1552, F32, 256)
        C.sb["AGT"] = TV(XNs[7], 2080, None, 1024)
        C.sb["QKS"] = TV(XNs[7], 3104, None, 512)
    else:
        C.alloc("WS", [128, 4096], BF16)
        C.sb["SQB"] = TV(C.alloc("SQBd", [128, 4096], BF16), 0)
        C.alloc("H4d", [128, 11264], BF16)
        w2d = C.alloc("W2SD", [128, 2816], BF16)
        C.sb["W2SA"] = TV(w2d, 0, None, 1920)
        C.sb["W2SB"] = TV(w2d, 1920, None, 896)
        st2 = C.alloc("ST2D", [1, 1536], F32)
        C.sb["ST0"] = TV(st2, 0, None, 512)
        C.sb["ST1"] = TV(st2, 512, None, 512)
        C.sb["ST2"] = TV(st2, 1024, None, 512)
        rsb = C.alloc("RSB2D", [128, 1024], F32)
        C.sb["RSB0"] = TV(rsb, 0, None, 512)
        C.sb["RSB1"] = TV(rsb, 512, None, 512)
        rw = C.alloc("RWD", [128, 512], F32)
        C.sb["RW0"] = TV(rw, 0, None, 256)
        C.sb["RW1v"] = TV(rw, 256, None, 256)
        C.alloc("AGTD", [128, 1024], BF16)
        C.sb["AGT"] = TV(C.sb["AGTD"], 0)
        C.alloc("QKSD", [128, 512], BF16)
        C.sb["QKS"] = TV(C.sb["QKSD"], 0)
    C.alloc("RSc", [128, 32], F32)
    C.alloc("XIF", [128, 512], F32)
    C.alloc("OUTT", [128, 1024], F32)
    C.alloc("B1C", [128, 44], F32)
    C.alloc("B2C", [128, 8], F32)
    C.alloc("ONESCB", [128, 1], BF16)
    C.alloc("ONESCF", [128, 1], F32)
    C.alloc("ONESR2", [1, 128], F32)
    C.alloc("LNGB", [128, 32], F32)
    C.alloc("ZCOL", [128, 1], BF16)
    C.alloc("IDF2", [128, 128], F32)


def emit_ffn(C, tag, Wqd, Wkd, Wvd, Wod, W1d, B1d, W2d, B2d, LNGd, LNBd,
             QSd, QSRd, KSd, KSRd, CQd, SQd, CKd, SKd,
             xi_hbm, qk_hbm, xn_ready, const_ready, sink, nlayers=2, phases="ab"):
    kb = C.kb
    XN = [C.sb[f"XN{c}"] for c in range(NDT)]
    OF = [C.sb[f"OF{c}"] for c in range(4)]
    QP, KP = C.sb["OF4"], C.sb["OF5"]
    TB1, TB2 = C.sb["OF6"], C.sb["OF7"]
    TQC, TQS = C.sb["VA"], C.sb["VB"]
    TBL = C.sb["TBL"]
    VH = XN
    QA, QB, KA, KB_ = C.sb["QA"], C.sb["QB"], C.sb["KA"], C.sb["KB_"]
    WS, XIB, QKS = C.sb["WS"], C.sb["XIB"], C.sb["QKS"]
    QR, Q1, RS = C.sb["QR"], C.sb["Q1"], C.sb["RSc"]
    RW0, RW1v = C.sb["RW0"], C.sb["RW1v"]
    W2SA, W2SB = C.sb["W2SA"], C.sb["W2SB"]

    def w2sl(kk, rows):
        if kk < 15:
            return W2SA[rows, 128 * kk:128 * (kk + 1)]
        return W2SB[rows, 128 * (kk - 15):128 * (kk - 14)]
    LNH, SQB, AGT = C.sb["LNH"], C.sb["SQB"], C.sb["AGT"]
    if "H4d" in C.sb:
        H4T = [TV(C.sb["H4d"], 4096 * z_) for z_ in range(3)]
    else:
        H4T = [TV(QP, 0), TV(KP, 0), TV(TB1, 0)]

    def h4sl(j, rows, c0=0, c1=512):
        g = 512 * j
        return H4T[g // 4096][rows, g % 4096 + c0:g % 4096 + c1]
    RSB0, RSB1 = C.sb["RSB0"], C.sb["RSB1"]
    XIF, OUTT = C.sb["XIF"], C.sb["OUTT"]
    ST0, ST1, ST2 = C.sb["ST0"], C.sb["ST1"], C.sb["ST2"]
    B1C, B2C = C.sb["B1C"], C.sb["B2C"]
    ONESCB, ONESR2, LNGB, ZCOL = C.sb["ONESCB"], C.sb["ONESR2"], C.sb["LNGB"], C.sb["ZCOL"]
    IDF = C.sb["IDF2"]
    PT, MBc = C.sb["PT"], C.sb["MBc"]
    RL, RB = C.sb["RL"], C.sb["RB"]
    negI, ones64 = C.sb["negI"], C.sb["ones1x64"]
    maskT, mask = C.sb["maskT"], C.sb["mask"]

    ld, mm, ev, vx, pxp = f"{tag}ld", f"{tag}mm", f"{tag}ev", f"{tag}vx", f"{tag}px"
    wl, st, dc_, ofs = f"{tag}wl", f"{tag}st", f"{tag}dc", f"{tag}of"

    TW = T // 2
    kb.dma("sync", ONESCB[:, :], C.const("c_onescol_bf", as_bf16(np.ones((128, 1)))), ld)
    kb.dma("sync", ONESR2[0:1, :], C.const("c_onesrow", as_f32(np.ones((1, 128)))), ld)
    kb.dma("sync", ZCOL[:, :], C.const("c_zcol", as_bf16(np.zeros((128, 1)))), ld)
    kb.dma("sync", IDF[:, :], C.const("c_IF32", as_f32(np.eye(128))), ld)
    kb.dma("sync", LNGB[:, 0:8], LNGd[0, :, :], ld)
    kb.dma("sync", LNGB[:, 8:16], LNBd[0, :, :], ld)
    kb.dma("sync", LNGB[:, 16:24], LNGd[1, :, :], ld)
    kb.dma("sync", LNGB[:, 24:32], LNBd[1, :, :], ld)
    onerow_d = C.const("c_onerow", np.ones((1, T), BF))
    kb.dma("sync", KA[64:65, :], onerow_d[0:1, :], ld)
    kb.dma("sync", KB_[64:65, :], onerow_d[0:1, :], ld)
    ldv = kb.now(ld)
    kb.wait("tensor", [(ld, ldv)])

    for c in range(NDT):
        kb.dma("gpsimd", xi_hbm[c, :, :], XN[c][:, :], st, [xn_ready])
    xiv = kb.now(st)
    xi_ready = xiv

    for l in range(nlayers):
        qsld = f"{tag}q{l}"
        kb.dma("sync", TB1[:, 0:TW], CQd[:, :], qsld,
               [xn_ready, const_ready, C.fwait("h4of"), (ld, kb.now(ld))])
        kb.dma("sync", TB1[:, TW:2 * TW], SQd[:, :], qsld)
        kb.dma("sync", TB2[:, 0:TW], CKd[:, :], qsld)
        kb.dma("sync", TB2[:, TW:2 * TW], SKd[:, :], qsld)
        kb.dma("sync", QR[:, 0:64], QSd[l, :, :], qsld, [C.fwait("qr")])
        kb.dma("sync", QR[:, 64:128], QSRd[l, :, :], qsld)
        kb.dma("sync", QR[:, 128:192], KSd[l, :, :], qsld)
        kb.dma("sync", QR[:, 192:256], KSRd[l, :, :], qsld)
        qslv = kb.now(qsld)
        for (dst, doff, src, soff, sco) in (
                (TQC, 0, TB1, 0, 0), (TQS, 0, TB1, TW, 64),
                (TBL, 0, TB2, 0, 128), (TBL, TW, TB2, TW, 192)):
            kb.op("vector", lambda E, dst=dst, doff=doff, src=src, soff=soff, sco=sco:
                  E.tensor_tensor(
                      vap(dst, doff, [[64, NTT], [1, 64]]),
                      vap(src, soff, [[64, NTT], [1, 64]]),
                      vap(QR, sco, [[0, NTT], [1, 64]]),
                      ALU.mult),
                  waits=[(qsld, qslv), (ld, ldv), C.fwait(("tbl", l))], inc=vx)
        tblv = kb.now(vx)
        C.setfree("qr", vx, tblv)

        for fi, (fw_, dfw, doff, fdram, hbase) in enumerate((
                (512, 512, 0, Wqd, 0), (512, 512, 0, Wkd, 0),
                (260, 520, 0, Wvd, 0), (260, 520, 260, Wvd, 4))):
            for c in range(NDT):
                kb.dma("sync", WS[:, fw_ * c:fw_ * c + fw_],
                       fdram[l, :, dfw * c + doff:dfw * c + doff + fw_], wl,
                       [C.fwait("ws")] if c == 0 else [])
            wlv = kb.now(wl)
            for i in range(NTT):
                par = i % 2
                o = 1024 * par
                gl = f"{tag}gl"
                xw = [C.fwait(("xib", par)), (st, xi_ready)]
                for c in range(NDT):
                    if c < 4:
                        kb.dma("gpsimd", XIB[:, o + 128 * c:o + 128 * c + 128],
                               xi_hbm[c, :, 128 * i:128 * (i + 1)], gl, xw if c == 0 else [])
                    elif i > 0:
                        kb.dma("gpsimd", XIB[:, o + 128 * c:o + 128 * c + 128],
                               xi_hbm[c, :, 128 * i - 1:128 * (i + 1) - 1], gl)
                    else:
                        kb.op("scalar", lambda E, o=o, c=c: E.copy(
                            XIB[:, o + 128 * c:o + 128 * c + 1], ZCOL[:, :]),
                            waits=[(ld, ldv), C.fwait(("xib", par))], inc=ev)
                        kb.dma("gpsimd", XIB[:, o + 128 * c + 1:o + 128 * c + 128],
                               xi_hbm[c, :, 0:127], gl)
                xlv = kb.now(gl)
                if i == 0:
                    kb.wait("tensor", [(ev, kb.now(ev))])
                b = par
                mm_acc(C, C.ps[0:128, 512 * b:512 * b + fw_],
                       [((), XIB[:, o + 128 * c:o + 128 * c + 128],
                         WS[:, fw_ * c:fw_ * c + fw_]) for c in range(NDT)],
                       inc=mm, waits=[(gl, xlv), (wl, wlv), C.fwait(("bk", b))])
                mv = kb.now(mm)
                C.setfree(("xib", par), mm, mv)
                if fi < 2:
                    kb.op("vector", lambda E, b=b: E.tensor_copy(
                        QR[:, :], C.ps[0:128, 512 * b:512 * b + 512]),
                        waits=[(mm, mv), C.fwait("qr")], inc=vx)
                    C.setfree(("bk", b), vx, kb.now(vx))
                    kb.op("scalar", lambda E: E.activation(Q1[:, :], QR[:, :], AF.Square),
                          waits=[(vx, kb.now(vx)), C.fwait("q1")], inc=ev)
                    kb.op("vector", lambda E: E.reduce_sum(
                        RS[:, 0:8], Q1[:, :].rearrange("p (s d) -> p s d", d=64), axis=AX),
                        waits=[(ev, kb.now(ev))], inc=vx)
                    kb.wait("vector", [(vx, kb.now(vx))])
                    kb.op("vector", lambda E: E.tensor_scalar(RS[:, 8:16], RS[:, 0:8], 1e-24,
                                                              None, ALU.max), inc=vx)
                    kb.wait("scalar", [(vx, kb.now(vx))])
                    kb.op("scalar", lambda E: E.activation(RS[:, 16:24], RS[:, 8:16], AF.Sqrt),
                          inc=ev)
                    kb.wait("vector", [(ev, kb.now(ev))])
                    kb.op("vector", lambda E: E.reciprocal(RS[:, 24:32], RS[:, 16:24]), inc=vx)
                    kb.wait("vector", [(vx, kb.now(vx))])
                    kb.op("vector", lambda E: E.tensor_tensor(
                        Q1[:, :].rearrange("p (s d) -> p s d", d=64),
                        QR[:, :].rearrange("p (s d) -> p s d", d=64),
                        vap(RS, 24, [[1, 8], [0, 64]]), ALU.mult), inc=vx)
                    kb.wait("vector", [(vx, kb.now(vx))])
                    ct, cto = (TQC, 0) if fi == 0 else (TBL, 0)
                    stt, sto = (TQS, 0) if fi == 0 else (TBL, TW)
                    kb.op("vector", lambda E, ct=ct, cto=cto, i=i: E.tensor_tensor(
                        QR[:, :].rearrange("p (s d) -> p s d", d=64),
                        Q1[:, :].rearrange("p (s d) -> p s d", d=64),
                        vap(ct, cto + 64 * i, [[0, 8], [1, 64]]), ALU.mult),
                        waits=[(vx, tblv)], inc=vx)
                    kb.op("vector", lambda E, stt=stt, sto=sto, i=i: E.tensor_tensor(
                        vap(RW0, 0, [[32, 8], [1, 32]]),
                        vap(Q1, 32, [[64, 8], [1, 32]]),
                        vap(stt, sto + 64 * i, [[0, 8], [1, 32]]), ALU.mult),
                        waits=[C.fwait("rw")], inc=vx)
                    kb.op("vector", lambda E, stt=stt, sto=sto, i=i: E.tensor_tensor(
                        vap(RW1v, 0, [[32, 8], [1, 32]]),
                        vap(Q1, 0, [[64, 8], [1, 32]]),
                        vap(stt, sto + 64 * i + 32, [[0, 8], [1, 32]]), ALU.mult), inc=vx)
                    kb.wait("vector", [(vx, kb.now(vx))])
                    kb.op("vector", lambda E: E.tensor_tensor(
                        vap(QKS, 0, [[64, 8], [1, 32]]),
                        vap(QR, 0, [[64, 8], [1, 32]]),
                        vap(RW0, 0, [[32, 8], [1, 32]]), ALU.subtract),
                        waits=[C.fwait("qks")], inc=vx)
                    kb.op("vector", lambda E: E.tensor_tensor(
                        vap(QKS, 32, [[64, 8], [1, 32]]),
                        vap(QR, 32, [[64, 8], [1, 32]]),
                        vap(RW1v, 0, [[32, 8], [1, 32]]), ALU.add), inc=vx)
                    kb.wait("vector", [(vx, kb.now(vx))])
                    C.setfree("qr", vx, kb.now(vx))
                    C.setfree("q1", vx, kb.now(vx))
                    C.setfree("rw", vx, kb.now(vx))
                    kb.dma("gpsimd", qk_hbm[128 * i:128 * (i + 1), 512 * fi:512 * (fi + 1)],
                           QKS[:, :], st, [(vx, kb.now(vx))])
                    C.setfree("qks", st, kb.now(st))
                else:
                    for hh2 in range(4):
                        kb.op("scalar", lambda E, hh2=hh2, hbase=hbase, i=i, b=b: E.copy(
                            VH[hbase + hh2][:, 65 * i:65 * i + 65],
                            C.ps[0:128, 512 * b + 65 * hh2:512 * b + 65 * hh2 + 65]),
                            waits=[(mm, mv), (st, xi_ready)] if hh2 == 0 else [], inc=ev)
                    C.setfree(("bk", b), ev, kb.now(ev))
            C.setfree("ws", mm, kb.now(mm))
        qkst = kb.now(st)
        for h in range(8):
            kb.op("scalar", lambda E, h=h: E.activation(
                VH[h][:, 64::65], VH[h][:, 64::65], AF.Copy, bias=1.0),
                waits=[(ev, kb.now(ev))], inc=ev)
        vdone = kb.now(ev)

        for p in range(4):
            pr = f"{tag}pr{l}"
            kb.dmat("sync", QP[:, :], qk_hbm[:, 128 * p:128 * (p + 1)], pr,
                    [(st, qkst), C.fwait("qp"), C.fwait("h4of")])
            kb.dmat("sync", KP[:, :], qk_hbm[:, 512 + 128 * p:512 + 128 * (p + 1)], pr)
            prv = kb.now(pr)
            kb.op("vector", lambda E: E.tensor_copy(QA[0:64, :], QP[0:64, :]),
                  waits=[(pr, prv), C.fwait("qa")], inc=vx)
            kb.op("vector", lambda E: E.tensor_copy(QB[0:64, :], QP[64:128, :]), inc=vx)
            kb.op("vector", lambda E: E.tensor_copy(KA[0:64, :], KP[0:64, :]), inc=vx)
            kb.op("vector", lambda E: E.tensor_copy(KB_[0:64, :], KP[64:128, :]), inc=vx)
            C.setfree("qp", vx, kb.now(vx))
            qkv_v = kb.now(vx)
            for hh in range(2):
                h = 2 * p + hh
                Q = (QA, QB)[hh]
                K = (KA, KB_)[hh]
                V = VH[h]
                for q in range(NTT):
                    klo = max(0, q - 4)
                    Wd = 128 * (q + 1) - 128 * klo
                    fwt = [(vx, qkv_v), (ev, vdone), C.fwait("p1banks")]
                    for chk in range((Wd + 511) // 512):
                        cw = min(512, Wd - 512 * chk)
                        mm_acc(C, C.ps[0:128, 512 * chk:512 * chk + cw],
                               [((), Q[0:64, 128 * q:128 * (q + 1)],
                                 K[0:64, 128 * klo + 512 * chk:128 * klo + 512 * chk + cw])],
                               inc=mm, waits=fwt if chk == 0 else ())
                    ms = kb.now(mm)
                    rw_ = [(mm, ms)]
                    if q >= 4:
                        kb.op("vector", lambda E: E.tensor_tensor(
                            C.ps[0:128, 0:128], C.ps[0:128, 0:128], maskT[:, :], ALU.add),
                            waits=[(mm, ms)], inc=vx)
                    kb.op("vector", lambda E, Wd=Wd: E.tensor_tensor(
                        C.ps[0:128, Wd - 128:Wd], C.ps[0:128, Wd - 128:Wd], mask[:, :], ALU.add),
                        waits=[(mm, ms)], inc=vx)
                    rw_.append((vx, kb.now(vx)))
                    kb.op("vector", lambda E, Wd=Wd, hh=hh: E.reduce_max(
                        MBc[:, hh:hh + 1], C.ps[0:128, 0:Wd], axis=AX), waits=rw_, inc=vx)
                    mm_acc(C, C.ps[0:1, 0:128],
                           [([(vx, kb.now(vx))], MBc[:, hh:hh + 1], negI[:, :])], inc=mm)
                    kb.op("scalar", lambda E, q=q, Q=Q: E.copy(
                        Q[64:65, 128 * q:128 * (q + 1)], C.ps[0:1, 0:128]),
                        waits=[(mm, kb.now(mm))], inc=ev)
                    C.setfree("p1banks", ev, kb.now(ev))
                qaugv = kb.now(ev)
                p1vx = kb.now(vx)
                ptfree = {}
                for g in range(T // 512):
                    jlo = max(0, 4 * g - 4)
                    for j in range(jlo, 4 * g + 4):
                        qlo = max(512 * g, 128 * j)
                        qhi = min(512 * (g + 1), 128 * j + 640)
                        N = qhi - qlo
                        sb_ = 4 + j % 2
                        mm_acc(C, C.ps[0:128, 512 * sb_:512 * sb_ + N],
                               [((), K[0:65, 128 * j:128 * (j + 1)], Q[0:65, qlo:qhi])],
                               inc=mm, waits=[(ev, qaugv), (vx, p1vx),
                                              C.fwait(("st", sb_)), C.fwait("ob")])
                        mst = kb.now(mm)
                        exp_w = [(mm, mst), ptfree.get(j % 2)]
                        nvx = 0
                        if qlo == 128 * j:
                            kb.op("vector", lambda E, sb_=sb_: E.tensor_tensor(
                                C.ps[0:128, 512 * sb_:512 * sb_ + 128],
                                C.ps[0:128, 512 * sb_:512 * sb_ + 128], maskT[:, :], ALU.add),
                                waits=[(mm, mst)], inc=vx)
                            nvx = 1
                        if qhi == 128 * j + 640:
                            kb.op("vector", lambda E, sb_=sb_, N=N: E.tensor_tensor(
                                C.ps[0:128, 512 * sb_ + N - 128:512 * sb_ + N],
                                C.ps[0:128, 512 * sb_ + N - 128:512 * sb_ + N], mask[:, :],
                                ALU.add), waits=[(mm, mst)], inc=vx)
                            nvx = 1
                        if nvx:
                            exp_w.append((vx, kb.now(vx)))
                        pts = 512 * (j % 2)
                        kb.op("scalar", lambda E, sb_=sb_, pts=pts, N=N: E.activation(
                            PT[:, pts:pts + N], C.ps[0:128, 512 * sb_:512 * sb_ + N], AF.Exp),
                            waits=exp_w, inc=pxp)
                        C.setfree(("st", sb_), pxp, kb.now(pxp))
                        co = qlo - 512 * g
                        mm_acc(C, C.ps[0:65, 512 * 7 + co:512 * 7 + co + N],
                               [([(pxp, kb.now(pxp))], V[:, 65 * j:65 * j + 65],
                                 PT[:, pts:pts + N])],
                               inc=mm, start=(j == jlo), stop=(j == 4 * g + 3))
                        ptfree[j % 2] = (mm, kb.now(mm))
                    pvd = kb.now(mm)
                    kb.op("vector", lambda E: E.reciprocal(RL[0:1, :], C.ps[64:65, 512 * 7:512 * 8]),
                          waits=[(mm, pvd), C.fwait("rl")], inc=vx)
                    mm_acc(C, C.ps[0:64, 512 * 6:512 * 6 + 512],
                           [([(vx, kb.now(vx))], ones64[0:1, :], RL[0:1, :])], inc=mm,
                           waits=[C.fwait("b6")])
                    kb.op("scalar", lambda E: E.copy(RB[:, :], C.ps[0:64, 512 * 6:512 * 6 + 512]),
                          waits=[(mm, kb.now(mm)), C.fwait("rb")], inc=ev)
                    C.setfree("b6", ev, kb.now(ev))
                    C.setfree("rl", ev, kb.now(ev))
                    kb.op("vector", lambda E, g=g, hh=hh, p=p: E.tensor_tensor(
                        OF[p][64 * hh:64 * hh + 64, 512 * g:512 * (g + 1)],
                        C.ps[0:64, 512 * 7:512 * 8], RB[:, :], ALU.mult),
                        waits=[(ev, kb.now(ev))], inc=ofs)
                    C.setfree("ob", ofs, kb.now(ofs))
                    C.setfree("rb", ofs, kb.now(ofs))
            C.setfree("qa", ofs, kb.now(ofs))
        atdone = kb.now(ofs)

        for c in range(4):
            kb.dma("sync", WS[:, 1024 * c:1024 * (c + 1)], Wod[l, :, 1024 * c:1024 * (c + 1)],
                   wl, [C.fwait("ws")] if c == 0 else [])
        wov = kb.now(wl)
        for dcc in range(NDT):
            for tc in range(T // 512):
                b = (8 * dcc + tc) % 2
                xv = kb.dma("sync", XIF[:, :], xi_hbm[dcc, :, 512 * tc:512 * (tc + 1)], ld,
                            [C.fwait("xif")])
                mm_acc(C, C.bank(b),
                       [((), WS[:, 1024 * kc + 128 * dcc:1024 * kc + 128 * dcc + 128],
                         OF[kc][:, 512 * tc:512 * (tc + 1)]) for kc in range(4)],
                       inc=mm, waits=[(wl, wov), (ofs, atdone), C.fwait(("bk", b))])
                mv = kb.now(mm)
                kb.op("vector", lambda E, b=b: E.tensor_tensor(XIF[:, :], XIF[:, :],
                                                               C.bank(b), ALU.add),
                      waits=[(mm, mv), (ld, xv)], inc=vx)
                C.setfree(("bk", b), vx, kb.now(vx))
                kb.dma("gpsimd", xi_hbm[dcc, :, 512 * tc:512 * (tc + 1)], XIF[:, :], st,
                       [(vx, kb.now(vx))])
                C.setfree("xif", st, kb.now(st))
        wodone = kb.now(st)
        C.setfree("ws", mm, kb.now(mm))

        final = (l == nlayers - 1)
        kb.dma("sync", B1C[:, :], B1d[l, :, :], wl, [(st, wodone)])
        kb.dma("sync", B2C[:, :], B2d[l, :, :], wl)
        bcv = kb.now(wl)
        for tc in range(T // 512):
            gl = f"{tag}gl"
            xw = [(st, wodone), C.fwait("xib2")]
            for c in range(NDT):
                if c < 4:
                    kb.dma("gpsimd", LNH[:, 512 * c:512 * c + 512],
                           xi_hbm[c, :, 512 * tc:512 * (tc + 1)], gl, xw if c == 0 else [])
                elif tc > 0:
                    kb.dma("gpsimd", LNH[:, 512 * c:512 * c + 512],
                           xi_hbm[c, :, 512 * tc - 1:512 * (tc + 1) - 1], gl)
                else:
                    kb.op("scalar", lambda E, c=c: E.copy(
                        LNH[:, 512 * c:512 * c + 1], ZCOL[:, :]),
                        waits=[(ld, ldv), C.fwait("sqb")], inc=ev)
                    kb.dma("gpsimd", LNH[:, 512 * c + 1:512 * c + 512],
                           xi_hbm[c, :, 0:511], gl)
            xlv = kb.now(gl)
            if tc == 0:
                kb.wait("scalar", [(ev, kb.now(ev))])
                kb.wait("tensor", [(ev, kb.now(ev))])
            kb.op("scalar", lambda E: E.activation(SQB[:, :], LNH[:, :], AF.Square),
                  waits=[(f"{tag}gl", xlv), C.fwait("sqb")], inc=ev)
            sqv = kb.now(ev)
            mm_acc(C, C.ps[0:1, 512 * 2:512 * 2 + 512],
                   [([(f"{tag}gl", xlv)], ONESCB[:, :], LNH[:, 512 * c:512 * (c + 1)])
                    for c in range(NDT)],
                   inc=mm, waits=[C.fwait(("bk", 2))])
            mm_acc(C, C.ps[0:1, 512 * 3:512 * 3 + 512],
                   [([(ev, sqv)], ONESCB[:, :], SQB[:, 512 * c:512 * (c + 1)])
                    for c in range(NDT)],
                   inc=mm, waits=[C.fwait(("bk", 3))])
            smv = kb.now(mm)
            C.setfree("sqb", mm, smv)
            kb.op("vector", lambda E: E.tensor_scalar(ST0[0:1, 0:512],
                                                      C.ps[0:1, 512 * 2:512 * 2 + 512],
                                                      1.0 / D, None, ALU.mult),
                  waits=[(mm, smv)], inc=vx)
            kb.op("vector", lambda E: E.tensor_scalar(ST1[0:1, 0:512],
                                                      C.ps[0:1, 512 * 3:512 * 3 + 512],
                                                      1.0 / D, None, ALU.mult), inc=vx)
            C.setfree(("bk", 2), vx, kb.now(vx))
            C.setfree(("bk", 3), vx, kb.now(vx))
            kb.op("vector", lambda E: E.tensor_tensor(ST2[0:1, 0:512], ST0[0:1, 0:512],
                                                      ST0[0:1, 0:512], ALU.mult), inc=vx)
            kb.wait("vector", [(vx, kb.now(vx))])
            kb.op("vector", lambda E: E.tensor_tensor(ST1[0:1, 0:512], ST1[0:1, 0:512],
                                                      ST2[0:1, 0:512], ALU.subtract), inc=vx)
            epsap, epsw = C.fconst(1e-5)
            kb.op("scalar", lambda E, epsap=epsap: E.activation(
                ST2[0:1, 0:512], ST1[0:1, 0:512], AF.Sqrt, bias=epsap[0:1, :]),
                waits=[(vx, kb.now(vx)), epsw], inc=ev)
            kb.op("vector", lambda E: E.reciprocal(ST1[0:1, 0:512], ST2[0:1, 0:512]),
                  waits=[(ev, kb.now(ev))], inc=vx)
            kb.wait("vector", [(vx, kb.now(vx))])
            kb.op("vector", lambda E: E.scalar_tensor_tensor(
                ST0[0:1, 0:512], ST0[0:1, 0:512], -1.0, ST1[0:1, 0:512],
                ALU.mult, ALU.mult), inc=vx)
            stv = kb.now(vx)
            mm_acc(C, C.ps[0:128, 512 * 2:512 * 2 + 512],
                   [([(vx, stv)], ONESR2[0:1, :], ST1[0:1, 0:512])], inc=mm,
                   waits=[C.fwait(("bk", 2))])
            mm_acc(C, C.ps[0:128, 512 * 3:512 * 3 + 512],
                   [((), ONESR2[0:1, :], ST0[0:1, 0:512])], inc=mm,
                   waits=[C.fwait(("bk", 3))])
            bmv = kb.now(mm)
            kb.op("scalar", lambda E: E.copy(RSB0[:, 0:512], C.ps[0:128, 512 * 2:512 * 2 + 512]),
                  waits=[(mm, bmv), C.fwait("rsb2")], inc=ev)
            kb.op("scalar", lambda E: E.copy(RSB1[:, 0:512], C.ps[0:128, 512 * 3:512 * 3 + 512]),
                  inc=ev)
            rbv = kb.now(ev)
            C.setfree(("bk", 2), ev, rbv)
            C.setfree(("bk", 3), ev, rbv)
            for c in range(NDT):
                kb.op("vector", lambda E, c=c: E.tensor_tensor(
                    SQB[:, 512 * c:512 * (c + 1)], LNH[:, 512 * c:512 * (c + 1)],
                    RSB0[:, 0:512], ALU.mult), waits=[(ev, rbv)], inc=vx)
                kb.op("vector", lambda E, c=c: E.tensor_tensor(
                    SQB[:, 512 * c:512 * (c + 1)], SQB[:, 512 * c:512 * (c + 1)],
                    RSB1[:, 0:512], ALU.add), inc=vx)
                kb.op("vector", lambda E, c=c, l=l: E.tensor_scalar(
                    LNH[:, 512 * c:512 * (c + 1)], SQB[:, 512 * c:512 * (c + 1)],
                    LNGB[:, 16 * l + c:16 * l + c + 1], LNGB[:, 16 * l + 8 + c:16 * l + 8 + c + 1],
                    ALU.mult, ALU.add), waits=[(wl, bcv), (ld, ldv)], inc=vx)
            lnv = kb.now(vx)
            C.setfree("sqb", vx, lnv)

            for j in range(22 if "b2" in phases or phases == "ab" else 0):
                parj = j % 2
                (a0, asz) = W1M[j]
                (g0, gsz) = W1M[22 + j]
                wj = f"{tag}w1"
                kb.dma("sync", WS[:, 2048 * parj:2048 * parj + 1024],
                       W1d[l, j, :, :], wj, [C.fwait(("w1s", parj)), C.fwait("ws")])
                kb.dma("sync", WS[:, 2048 * parj + 1024:2048 * parj + 2048],
                       W1d[l, 22 + j, :, :], wj)
                wjv = kb.now(wj)
                for (which, m0, msz, bcol) in (("a", a0, asz, j), ("g", g0, gsz, 22 + j)):
                    woff = 2048 * parj + (0 if which == "a" else 1024)
                    bnk = 4 if which == "a" else 5
                    mm_acc(C, C.ps[0:msz, 512 * bnk:512 * bnk + 512],
                           [((), WS[:, woff + 128 * c:woff + 128 * c + msz],
                             LNH[:, 512 * c:512 * (c + 1)]) for c in range(NDT)],
                           inc=mm, waits=[(wj, wjv), (vx, lnv), C.fwait(("bk", bnk))])
                    mv = kb.now(mm)
                    if which == "a":
                        kb.op("vector", lambda E, msz=msz, bcol=bcol, bnk=bnk: E.tensor_scalar(
                            AGT[0:msz, 0:512], C.ps[0:msz, 512 * bnk:512 * bnk + 512], 1.0,
                            B1C[0:msz, bcol:bcol + 1], ALU.mult, ALU.add),
                            waits=[(mm, mv), (wl, bcv), C.fwait("agta")], inc=vx)
                        C.setfree(("bk", bnk), vx, kb.now(vx))
                    else:
                        kb.op("scalar", lambda E, msz=msz, bcol=bcol, bnk=bnk: E.activation(
                            AGT[0:msz, 512:1024], C.ps[0:msz, 512 * bnk:512 * bnk + 512],
                            AF.Gelu, bias=B1C[0:msz, bcol:bcol + 1]),
                            waits=[(mm, mv), (wl, bcv), C.fwait("agtg")], inc=ev)
                        C.setfree(("bk", bnk), ev, kb.now(ev))
                C.setfree(("w1s", parj), mm, kb.now(mm))
                kb.op("vector", lambda E, j=j, asz=asz: E.tensor_tensor(
                    h4sl(j, slice(0, asz)), AGT[0:asz, 0:512], AGT[0:asz, 512:1024],
                    ALU.mult), waits=[(ev, kb.now(ev)), (vx, kb.now(vx)), C.fwait(("h4", j))],
                    inc=vx)
                C.setfree("agta", vx, kb.now(vx))
                C.setfree("agtg", vx, kb.now(vx))
            h4v = kb.now(vx)

            for dcc in range(NDT if "b2" in phases or phases == "ab" else 0):
                w2j = f"{tag}w2"
                kb.dma("sync", W2SA[0:128, 0:1920], W2d[l, dcc, :, 0:1920], w2j,
                       [C.fwait("w2s"), (st, xiv)])
                kb.dma("sync", W2SB[0:128, 0:896], W2d[l, dcc, :, 1920:2816], w2j)
                w2v = kb.now(w2j)
                if not final:
                    xv = kb.dma("sync", XIF[:, :], xi_hbm[dcc, :, 512 * tc:512 * (tc + 1)], ld,
                                [C.fwait("xif")])
                mm_acc(C, C.ps[0:128, 0:512],
                       [((), w2sl(kk, slice(0, W2K[kk][1])),
                         h4sl(kk, slice(0, W2K[kk][1])))
                        for kk in range(22)],
                       inc=mm, waits=[(w2j, w2v), (vx, h4v), C.fwait(("bk", 0))])
                mv = kb.now(mm)
                C.setfree("w2s", mm, mv)
                if not final:
                    kb.op("vector", lambda E, dcc=dcc: E.tensor_scalar(
                        QR[:, :], C.ps[0:128, 0:512], 1.0, B2C[:, dcc:dcc + 1],
                        ALU.mult, ALU.add),
                        waits=[(mm, mv), (wl, bcv), C.fwait("qr")], inc=vx)
                    C.setfree(("bk", 0), vx, kb.now(vx))
                    kb.op("vector", lambda E: E.tensor_tensor(XIF[:, :], XIF[:, :], QR[:, :],
                                                              ALU.add),
                          waits=[(ld, xv)], inc=vx)
                    C.setfree("qr", vx, kb.now(vx))
                    kb.dma("gpsimd", xi_hbm[dcc, :, 512 * tc:512 * (tc + 1)], XIF[:, :], st,
                           [(vx, kb.now(vx))])
                    C.setfree("xif", st, kb.now(st))
                else:
                    kb.op("vector", lambda E, dcc=dcc: E.tensor_scalar(
                        XIF[:, :], C.ps[0:128, 0:512], 1.0, B2C[:, dcc:dcc + 1],
                        ALU.mult, ALU.add), waits=[(mm, mv), (wl, bcv), C.fwait("xif")], inc=vx)
                    xv2 = kb.dma("sync", QR[:, :], xi_hbm[dcc, :, 512 * tc:512 * (tc + 1)],
                                 ld, [C.fwait("qr"), (vx, h4v)])
                    kb.op("vector", lambda E: E.tensor_tensor(XIF[:, :], XIF[:, :], QR[:, :],
                                                              ALU.add),
                          waits=[(ld, xv2), (vx, kb.now(vx))], inc=vx)
                    xfv = kb.now(vx)
                    C.setfree(("bk", 0), vx, xfv)
                    C.setfree("qr", vx, xfv)
                    for tt in range(4):
                        kb.op("tensor", lambda E, tt=tt: E.transpose(
                            C.ps[0:128, 512 * 5:512 * 5 + 128], XIF[:, 128 * tt:128 * (tt + 1)],
                            IDF[:, :]),
                            waits=[(vx, xfv), C.fwait(("bk", 5)), (ld, ldv)], inc=mm)
                        kb.op("vector", lambda E, tt=tt, dcc=dcc: E.tensor_copy(
                            OUTT[:, 128 * dcc:128 * (dcc + 1)],
                            C.ps[0:128, 512 * 5:512 * 5 + 128]),
                            waits=[(mm, kb.now(mm)), C.fwait(("outt", dcc))], inc=vx)
                        C.setfree(("bk", 5), vx, kb.now(vx))
                        sink(4 * tc + tt, dcc, OUTT[:, 128 * dcc:128 * (dcc + 1)],
                             [(vx, kb.now(vx))])
                        C.setfree(("outt", dcc), dc_, kb.now(dc_))
                    C.setfree("xif", mm, kb.now(mm))
            for jj in range(22):
                C.setfree(("h4", jj), mm, kb.now(mm))
            C.setfree("xib2", vx, kb.now(vx))
        C.setfree(("tbl", 1 - l), vx, kb.now(vx))
        C.setfree("ws", mm, kb.now(mm))
        C.setfree("h4of", mm, kb.now(mm))
        for jj in range(22):
            C.setfree(("h4", jj), mm, kb.now(mm))
        xi_ready = kb.now(st)
    return dc_, kb.now(dc_)


def rotary_tables(Tn=T):
    DIM_HEAD, WINDOW = 64, 512
    half_idx = np.arange(0, DIM_HEAD, 2, dtype=np.float32)
    inv_freq = 1.0 / 10000.0 ** (half_idx / DIM_HEAD)
    t = np.arange(Tn, dtype=np.float32)
    freqs = t[:, None] * inv_freq[None, :]
    freqs = np.concatenate([freqs, freqs], axis=-1).astype(np.float32)
    base = ((half_idx + 0.4 * DIM_HEAD) / (1.4 * DIM_HEAD)).astype(np.float32)
    power = ((t - Tn // 2) / float(WINDOW // 2)).astype(np.float32)
    scale = base[None, :] ** power[:, None]
    scale = np.concatenate([scale, scale], axis=-1).astype(np.float32)
    cos, sin = np.cos(freqs), np.sin(freqs)
    CQ = cos * scale * 8.0
    SQ = sin * scale * 8.0
    CK = cos / scale
    SK = sin / scale
    def tl(tab):
        return as_bf16(tab.reshape(Tn // 128, 128, 64).transpose(1, 0, 2).reshape(128, -1))
    return tl(CQ), tl(SQ), tl(CK), tl(SK)


def prep_ffn_inputs(I, pref, pr):
    wqkv = I[pref + "wqkv"].astype(np.float32)
    wq, wk, wv = wqkv[:, :, 0:512], wqkv[:, :, 512:1024], wqkv[:, :, 1024:1536]
    wva = np.zeros((2, 1024, 520), np.float32)
    for h in range(8):
        wva[:, :, 65 * h:65 * h + 64] = wv[:, :, 64 * h:64 * h + 64]

    def cm(w):
        fw = w.shape[2]
        return as_bf16(w.reshape(2, 8, 128, fw).transpose(0, 2, 1, 3)
                       .reshape(2, 128, 8 * fw))
    wo = I[pref + "wo"].astype(np.float32)
    ffwo = np.ascontiguousarray(wo.reshape(2, 4, 128, 1024).transpose(0, 2, 1, 3)
                                .reshape(2, 128, 4096))
    w1 = I[pref + "w1"].astype(np.float32)
    ffw1 = np.zeros((2, 44, 128, 1024), np.float32)
    for m, (c0, sz) in enumerate(W1M):
        blk = w1[:, :, c0:c0 + sz]
        tmp = blk.reshape(2, 8, 128, sz).transpose(0, 2, 1, 3)
        for c in range(8):
            ffw1[:, m, :, 128 * c:128 * c + sz] = tmp[:, :, c, :]
    b1 = I[pref + "b1"].astype(np.float32)
    ffb1 = np.zeros((2, 128, 44), np.float32)
    for m, (c0, sz) in enumerate(W1M):
        ffb1[:, 0:sz, m] = b1[:, c0:c0 + sz]
    w2 = I[pref + "w2"].astype(np.float32)
    ffw2 = np.zeros((2, 8, 128, 2816), np.float32)
    for kk, (k0, szk) in enumerate(W2K):
        blk = w2[:, k0:k0 + szk, :]
        for dcc in range(8):
            ffw2[:, dcc, 0:szk, 128 * kk:128 * (kk + 1)] = blk[:, :, 128 * dcc:128 * (dcc + 1)]
    b2 = I[pref + "b2"].astype(np.float32)
    ffb2 = np.ascontiguousarray(b2.reshape(2, 8, 128).transpose(0, 2, 1))
    lng = np.ascontiguousarray(I[pref + "ln_g"].astype(np.float32).reshape(2, 8, 128)
                               .transpose(0, 2, 1))
    lnb = np.ascontiguousarray(I[pref + "ln_b"].astype(np.float32).reshape(2, 8, 128)
                               .transpose(0, 2, 1))
    qs = I[pref + "q_scale"].astype(np.float32)
    ks = I[pref + "k_scale"].astype(np.float32)
    CQ, SQ, CK, SK = rotary_tables(T)
    return {
        pr + "wq": cm(wq), pr + "wk": cm(wk), pr + "wv": cm(wva),
        pr + "wo": as_bf16(ffwo),
        pr + "w1": as_bf16(ffw1), pr + "b1": as_f32(ffb1),
        pr + "w2": as_bf16(ffw2), pr + "b2": as_f32(ffb2),
        pr + "lng": as_f32(lng), pr + "lnb": as_f32(lnb),
        pr + "qs": as_f32(np.tile(qs[:, None, :], (1, 128, 1))),
        pr + "qsr": as_f32(np.tile(np.roll(qs, -32, axis=-1)[:, None, :], (1, 128, 1))),
        pr + "ks": as_f32(np.tile(ks[:, None, :], (1, 128, 1))),
        pr + "ksr": as_f32(np.tile(np.roll(ks, -32, axis=-1)[:, None, :], (1, 128, 1))),
        pr + "cq": CQ, pr + "sq": SQ, pr + "ck": CK, pr + "sk": SK,
    }


def ffn_drams(nc, pr):
    d = {}
    d["Wqd"] = nc.dram_tensor(pr + "wq", [2, 128, 4096], BF16, kind="ExternalInput").ap()
    d["Wkd"] = nc.dram_tensor(pr + "wk", [2, 128, 4096], BF16, kind="ExternalInput").ap()
    d["Wvd"] = nc.dram_tensor(pr + "wv", [2, 128, 4160], BF16, kind="ExternalInput").ap()
    d["Wod"] = nc.dram_tensor(pr + "wo", [2, 128, 4096], BF16, kind="ExternalInput").ap()
    d["W1d"] = nc.dram_tensor(pr + "w1", [2, 44, 128, 1024], BF16, kind="ExternalInput").ap()
    d["B1d"] = nc.dram_tensor(pr + "b1", [2, 128, 44], F32, kind="ExternalInput").ap()
    d["W2d"] = nc.dram_tensor(pr + "w2", [2, 8, 128, 2816], BF16, kind="ExternalInput").ap()
    d["B2d"] = nc.dram_tensor(pr + "b2", [2, 128, 8], F32, kind="ExternalInput").ap()
    d["LNGd"] = nc.dram_tensor(pr + "lng", [2, 128, 8], F32, kind="ExternalInput").ap()
    d["LNBd"] = nc.dram_tensor(pr + "lnb", [2, 128, 8], F32, kind="ExternalInput").ap()
    d["QSd"] = nc.dram_tensor(pr + "qs", [2, 128, 64], F32, kind="ExternalInput").ap()
    d["QSRd"] = nc.dram_tensor(pr + "qsr", [2, 128, 64], F32, kind="ExternalInput").ap()
    d["KSd"] = nc.dram_tensor(pr + "ks", [2, 128, 64], F32, kind="ExternalInput").ap()
    d["KSRd"] = nc.dram_tensor(pr + "ksr", [2, 128, 64], F32, kind="ExternalInput").ap()
    d["CQd"] = nc.dram_tensor(pr + "cq", [128, T // 2], BF16, kind="ExternalInput").ap()
    d["SQd"] = nc.dram_tensor(pr + "sq", [128, T // 2], BF16, kind="ExternalInput").ap()
    d["CKd"] = nc.dram_tensor(pr + "ck", [128, T // 2], BF16, kind="ExternalInput").ap()
    d["SKd"] = nc.dram_tensor(pr + "sk", [128, T // 2], BF16, kind="ExternalInput").ap()
    return d


def build_ffn_standalone(env=None, pr="f0_"):
    nc, kb, stack, pref, own = _env_open(env)
    if env is None:
        xn_hbm = nc.dram_tensor("xn", [T, D], BF16, kind="ExternalInput").ap()
        out = nc.dram_tensor("out", [T, D], F32, kind="ExternalOutput").ap()
    else:
        xn_hbm, out = env["xn"], env["out"]
    xi_hbm = nc.dram_tensor(pref + "xi_s", [8, 128, T], F32, kind="Internal").ap()
    qk_hbm = nc.dram_tensor(pref + "qk_s", [T, 1024], BF16, kind="Internal").ap()
    dr = ffn_drams(nc, pr)
    P = host_consts_common()
    C = Ctx(nc, kb, stack, pref)
    for c in range(NDT):
        C.alloc(f"XN{c}", [128, T], BF16)
    alloc_mha(C, qkvw=False)
    alloc_ffn(C)
    C.alloc_ps()
    cs = emit_mha_consts(C, P)
    xr = emit_xn_from_hbm(C, xn_hbm, "x0")

    def sink(ti, dcc, ap, waits):
        kb.dma("gpsimd", out[128 * ti:128 * (ti + 1), 128 * dcc:128 * (dcc + 1)],
               ap, "fdc", waits)

    emit_ffn(C, "f", dr["Wqd"], dr["Wkd"], dr["Wvd"], dr["Wod"], dr["W1d"], dr["B1d"],
             dr["W2d"], dr["B2d"], dr["LNGd"], dr["LNBd"], dr["QSd"], dr["QSRd"],
             dr["KSd"], dr["KSRd"], dr["CQd"], dr["SQd"], dr["CKd"], dr["SKd"],
             xi_hbm, qk_hbm, xr, cs, sink)
    kb.wait("gpsimd", [("fdc", kb.now("fdc"))])
    if own:
        kb.finalize(nc, stack)
    return nc, C, stack


def alloc_l1(C):
    C.alloc("CWV", [128, 2048], BF16)
    C.alloc("CKA", [65, 512], BF16)
    C.alloc("CVA", [128, 130], BF16)
    C.alloc("CST", [128, 288], F32)
    C.alloc("COMPST", [64, 512], BF16)
    NBv = (T - 32) // 16 + 1
    C.alloc("CM0", [128, NBv], BF16)
    C.alloc("CM1", [128, NBv], BF16)
    C.alloc("CMT0", [128, 256], BF16)
    C.alloc("CMT1", [128, 256], BF16)
    C.alloc("IDF3", [128, 128], F32)
    C.alloc("ONE64", [64, 1], F32)
    C.alloc("M2R", [1, 256], F32)
    C.alloc("CSQ", [64, 256], F32)
    C.alloc("BQK2", [128, 16], F32)
    C.alloc("CGL", [64, 256], F32)


def emit_l1_comp(C, tag, CWd, CBd, CLNd, comp_hbm, conv_ready):
    kb = C.kb
    XN = [C.sb[f"XN{c}"] for c in range(NDT)]
    QA, QB, KA, KB_ = C.sb["QA"], C.sb["QB"], C.sb["KA"], C.sb["KB_"]
    QW, KW = C.sb["QW"], C.sb["KW"]
    CWV, CKA, CVA, CST = C.sb["CWV"], C.sb["CKA"], C.sb["CVA"], C.sb["CST"]
    COMPST = C.sb["COMPST"]
    PT, MBc = C.sb["PT"], C.sb["MBc"]
    RL, RB = C.sb["RL"], C.sb["RB"]
    negI, ones64 = C.sb["negI"], C.sb["ones1x64"]
    CM0, CM1, CMT0, CMT1 = C.sb["CM0"], C.sb["CM1"], C.sb["CMT0"], C.sb["CMT1"]
    IDF3, ONE64 = C.sb["IDF3"], C.sb["ONE64"]
    mm, ev, vx, pxp = f"{tag}cm", f"{tag}ce", f"{tag}cv", f"{tag}cp"
    wld, st = f"{tag}cw", f"{tag}cs"
    NB = (T - 32) // 16 + 1
    NTK = (NB + 127) // 128

    def hook(p):
        wv = [conv_ready]
        for (dstb, wsrc, woff) in ((KA, KW, 1024), (KB_, KW, 2048)):
            for c in range(NDT):
                wvc = []
                if c == 0 and dstb is KA:
                    wvc = [(f"{tag}of", kb.now(f"{tag}of"))]
                elif c == 0:
                    wvc = [(mm, kb.now(mm))]
                kb.dma("sync", wsrc[:, 128 * c:128 * (c + 1)],
                       C.l1_cattn[128 * c:128 * (c + 1), woff + 128 * p:woff + 128 * (p + 1)],
                       wld, wvc)
            wlv = kb.now(wld)
            for ch in range(T // 512):
                sl = slice(512 * ch, 512 * (ch + 1))
                b = ch % 2
                mm_acc(C, C.bank(b),
                       [((), wsrc[:, 128 * c:128 * (c + 1)], XN[c][:, sl]) for c in range(NDT)],
                       inc=mm, waits=[(wld, wlv), C.fwait(("ck", b))])
                mv = kb.now(mm)
                kb.op("vector", lambda E, sl=sl, b=b, woff=woff, p=p, dstb=dstb: E.tensor_scalar(
                    dstb[0:128, sl], C.ps[0:128, 512 * b:512 * b + 512], 1.0,
                    C.sb["BQK2"][:, (woff // 1024 - 1) * 8 + p:(woff // 1024 - 1) * 8 + p + 1],
                    ALU.mult, ALU.add), waits=[(mm, mv)], inc=vx)
                C.setfree(("ck", b), vx, kb.now(vx))
        kvv = kb.now(vx)
        for hh in range(2):
            h = 2 * p + hh
            Q = (QA, QB)[hh]
            KH = KA
            VH_ = KB_
            pb = 64 * hh
            for (isv, srcb) in ((0, KH), (1, VH_)):
                t0 = srcb[pb:pb + 64, 0:1]
                for tau in range(32):
                    src = bass.AP(t0.tensor, t0.offset + tau, [[t0.ap[0][0], 64], [16, NB]])
                    kb.op("tensor", lambda E, tau=tau, src=src, pb=pb: E.matmul(
                        C.ps[0:64, 512 * 4:512 * 4 + NB], CWV[pb:pb + 64, 64 * tau:64 * (tau + 1)],
                        src, start=(tau == 0), stop=(tau == 31)),
                        waits=([(vx, kvv), C.fwait(("ck", 4))] if tau == 0 else ()),
                        inc=mm if tau == 31 else None)
                cmv = kb.now(mm)
                kb.op("vector", lambda E: E.tensor_scalar(
                    CST[0:64, 0:NB], C.ps[0:64, 512 * 4:512 * 4 + NB], 1.0,
                    C.sb["CBC"][0:64, 0:1], ALU.mult, ALU.add),
                    waits=[(mm, cmv), conv_ready, (ev, kb.now(ev))], inc=vx)
                C.setfree(("ck", 4), vx, kb.now(vx))
                kb.op("scalar", lambda E: E.activation(C.sb["CSQ"][0:64, 0:NB],
                                                       CST[0:64, 0:NB], AF.Square),
                      waits=[(vx, kb.now(vx))], inc=ev)
                mm_acc(C, C.ps[0:1, 512 * 5:512 * 5 + NB],
                       [([(vx, kb.now(vx))], ONE64[0:64, 0:1], CST[0:64, 0:NB])],
                       inc=mm, waits=[C.fwait(("ck", 5))])
                mm_acc(C, C.ps[0:1, 512 * 5 + 256:512 * 5 + 256 + NB],
                       [([(ev, kb.now(ev))], ONE64[0:64, 0:1], C.sb["CSQ"][0:64, 0:NB])],
                       inc=mm)
                smv = kb.now(mm)
                kb.op("vector", lambda E: E.tensor_scalar(RL[0:1, 0:NB],
                                                          C.ps[0:1, 512 * 5:512 * 5 + NB],
                                                          1.0 / 64, None, ALU.mult),
                      waits=[(mm, smv), C.fwait("rl")], inc=vx)
                kb.op("vector", lambda E: E.tensor_scalar(CST[0:1, 287:287 + NB]
                                                          if False else RL[0:1, 256:256 + NB],
                                                          C.ps[0:1, 512 * 5 + 256:512 * 5 + 256 + NB],
                                                          1.0 / 64, None, ALU.mult), inc=vx)
                C.setfree(("ck", 5), vx, kb.now(vx))
                kb.wait("vector", [(vx, kb.now(vx))])
                kb.op("vector", lambda E: E.tensor_tensor(C.sb["M2R"][0:1, 0:NB], RL[0:1, 0:NB],
                                                          RL[0:1, 0:NB], ALU.mult), inc=vx)
                kb.wait("vector", [(vx, kb.now(vx))])
                kb.op("vector", lambda E: E.tensor_tensor(RL[0:1, 256:256 + NB],
                                                          RL[0:1, 256:256 + NB],
                                                          C.sb["M2R"][0:1, 0:NB], ALU.subtract), inc=vx)
                epsap, epsw = C.fconst(1e-5)
                kb.op("scalar", lambda E, epsap=epsap: E.activation(
                    RL[0:1, 256:256 + NB], RL[0:1, 256:256 + NB], AF.Sqrt, bias=epsap[0:1, :]),
                    waits=[(vx, kb.now(vx)), epsw], inc=ev)
                kb.wait("vector", [(ev, kb.now(ev))])
                kb.op("vector", lambda E: E.reciprocal(RL[0:1, 256:256 + NB],
                                                       RL[0:1, 256:256 + NB]), inc=vx)
                kb.wait("vector", [(vx, kb.now(vx))])
                kb.op("vector", lambda E: E.scalar_tensor_tensor(
                    RL[0:1, 0:NB], RL[0:1, 0:NB], -1.0, RL[0:1, 256:256 + NB],
                    ALU.mult, ALU.mult), inc=vx)
                stv = kb.now(vx)
                mm_acc(C, C.ps[0:64, 512 * 5:512 * 5 + NB],
                       [([(vx, stv)], ones64[0:1, :], RL[0:1, 256:256 + NB])], inc=mm,
                       waits=[C.fwait(("ck", 5))])
                mm_acc(C, C.ps[0:64, 512 * 5 + 256:512 * 5 + 256 + NB],
                       [((), ones64[0:1, :], RL[0:1, 0:NB])], inc=mm)
                bcv = kb.now(mm)
                kb.op("vector", lambda E: E.tensor_tensor(CST[0:64, 0:NB], CST[0:64, 0:NB],
                                                          C.ps[0:64, 512 * 5:512 * 5 + NB],
                                                          ALU.mult),
                      waits=[(mm, bcv)], inc=vx)
                kb.wait("vector", [(vx, kb.now(vx))])
                kb.op("vector", lambda E: E.tensor_tensor(CST[0:64, 0:NB], CST[0:64, 0:NB],
                                                          C.ps[0:64, 512 * 5 + 256:512 * 5 + 256 + NB],
                                                          ALU.add), inc=vx)
                C.setfree(("ck", 5), vx, kb.now(vx))
                kb.wait("vector", [(vx, kb.now(vx))])
                kb.op("vector", lambda E: E.tensor_scalar(
                    CST[0:64, 0:NB], CST[0:64, 0:NB], C.sb["CLW"][0:64, 0:1],
                    C.sb["CLW"][0:64, 1:2], ALU.mult, ALU.add),
                    waits=[conv_ready], inc=vx)
                kb.wait("scalar", [(vx, kb.now(vx))])
                if isv == 0:
                    kb.op("scalar", lambda E, hh=hh: E.activation(
                        CKA[0:64, 256 * hh:256 * hh + NB], CST[0:64, 0:NB], AF.Gelu),
                        waits=[C.fwait(("cka", hh))], inc=ev)
                    kb.op("scalar", lambda E, hh=hh: E.copy(
                        CKA[64:65, 256 * hh:256 * hh + 256],
                        C.sb["onerow_c"][0:1, 0:256]), inc=ev)
                else:
                    kb.op("scalar", lambda E: E.activation(
                        C.sb["CGL"][0:64, 0:NB], CST[0:64, 0:NB], AF.Gelu), inc=ev)
                    gv = kb.now(ev)
                    for tt2 in range(NTK):
                        ww = min(128, NB - 128 * tt2)
                        kb.op("tensor", lambda E, tt2=tt2, ww=ww: E.transpose(
                            C.ps[0:ww, 512 * 6:512 * 6 + 64],
                            C.sb["CGL"][0:64, 128 * tt2:128 * tt2 + ww],
                            IDF3[0:64, 0:64]),
                            waits=[(ev, gv), C.fwait(("ck", 6))], inc=mm)
                        kb.op("vector", lambda E, tt2=tt2, ww=ww: E.tensor_copy(
                            CVA[0:ww, 65 * tt2:65 * tt2 + 64], C.ps[0:ww, 512 * 6:512 * 6 + 64]),
                            waits=[(mm, kb.now(mm)), C.fwait("cva")], inc=vx)
                        C.setfree(("ck", 6), vx, kb.now(vx))
                    for tt2 in range(NTK):
                        ww2 = min(128, NB - 128 * tt2)
                        kb.op("scalar", lambda E, tt2=tt2, ww2=ww2: E.copy(
                            CVA[0:ww2, 65 * tt2 + 64:65 * tt2 + 65],
                            C.sb["onecol_c"][0:ww2, 0:1]),
                            waits=[(vx, kb.now(vx))] if tt2 == 0 else [], inc=ev)
            ckv, cvv = kb.now(ev), kb.now(ev)
            for q in range(NTT):
                fw2 = [(ev, ckv), C.fwait("p1b")]
                mm_acc(C, C.ps[0:128, 0:NB],
                       [((), Q[0:64, 128 * q:128 * (q + 1)], CKA[0:64, 256 * hh:256 * hh + NB])],
                       inc=mm, waits=fw2)
                ms = kb.now(mm)
                rw_ = [(mm, ms)]
                if q == 0 or (q == 1 and NB > 128):
                    msk = CM0 if q == 0 else CM1
                    kb.op("vector", lambda E, msk=msk: E.tensor_tensor(
                        C.ps[0:128, 0:NB], C.ps[0:128, 0:NB], msk[:, 0:NB], ALU.add),
                        waits=[(mm, ms), conv_ready], inc=vx)
                    rw_.append((vx, kb.now(vx)))
                kb.op("vector", lambda E, hh=hh: E.reduce_max(
                    MBc[:, hh:hh + 1], C.ps[0:128, 0:NB], axis=AX), waits=rw_, inc=vx)
                mm_acc(C, C.ps[0:1, 0:128],
                       [([(vx, kb.now(vx))], MBc[:, hh:hh + 1], negI[:, :])], inc=mm)
                kb.op("scalar", lambda E, q=q, Q=Q: E.copy(Q[64:65, 128 * q:128 * (q + 1)],
                                                           C.ps[0:1, 0:128]),
                      waits=[(mm, kb.now(mm))], inc=ev)
                C.setfree("p1b", ev, kb.now(ev))
            qaug2 = kb.now(ev)
            for g in range(T // 512):
                for j in range(NTK):
                    ww = min(128, NB - 128 * j)
                    sb_ = 4 + j
                    mm_acc(C, C.ps[0:ww, 512 * sb_:512 * sb_ + 512],
                           [((), CKA[0:65, 256 * hh + 128 * j:256 * hh + 128 * j + ww],
                             Q[0:65, 512 * g:512 * (g + 1)])],
                           inc=mm, waits=[(ev, qaug2), C.fwait(("ck", 4 + j)), C.fwait("ob2")])
                    mst = kb.now(mm)
                    exp_w = [(mm, mst)]
                    if g == 0:
                        mt = CMT0 if j == 0 else CMT1
                        kb.op("vector", lambda E, sb_=sb_, ww=ww, mt=mt: E.tensor_tensor(
                            C.ps[0:ww, 512 * sb_:512 * sb_ + 256],
                            C.ps[0:ww, 512 * sb_:512 * sb_ + 256], mt[0:ww, 0:256], ALU.add),
                            waits=[(mm, mst), conv_ready], inc=vx)
                        exp_w.append((vx, kb.now(vx)))
                    kb.op("scalar", lambda E, sb_=sb_, ww=ww, j=j: E.activation(
                        PT[0:ww, 512 * j:512 * j + 512], C.ps[0:ww, 512 * sb_:512 * sb_ + 512],
                        AF.Exp), waits=exp_w, inc=pxp)
                    C.setfree(("ck", 4 + j), pxp, kb.now(pxp))
                    mm_acc(C, C.ps[0:65, 512 * 7:512 * 8],
                           [([(pxp, kb.now(pxp)), (ev, cvv)], CVA[0:ww, 65 * j:65 * (j + 1)],
                             PT[0:ww, 512 * j:512 * j + 512])],
                           inc=mm, start=(j == 0), stop=(j == NTK - 1))
                pvd = kb.now(mm)
                kb.op("vector", lambda E: E.reciprocal(RL[0:1, 0:512], C.ps[64:65, 512 * 7:512 * 8]),
                      waits=[(mm, pvd), C.fwait("rl")], inc=vx)
                mm_acc(C, C.ps[0:64, 512 * 6:512 * 6 + 512],
                       [([(vx, kb.now(vx))], ones64[0:1, :], RL[0:1, 0:512])], inc=mm,
                       waits=[C.fwait(("ck", 6))])
                kb.op("scalar", lambda E: E.copy(RB[:, :], C.ps[0:64, 512 * 6:512 * 6 + 512]),
                      waits=[(mm, kb.now(mm)), C.fwait("rb2")], inc=ev)
                C.setfree(("ck", 6), ev, kb.now(ev))
                C.setfree("rl", ev, kb.now(ev))
                kb.op("vector", lambda E: E.tensor_tensor(
                    COMPST[0:64, 0:512], C.ps[0:64, 512 * 7:512 * 8], RB[:, :], ALU.mult),
                    waits=[(ev, kb.now(ev)), C.fwait("cmpst")], inc=vx)
                C.setfree("ob2", vx, kb.now(vx))
                C.setfree("rb2", vx, kb.now(vx))
                kb.dma("gpsimd", comp_hbm[h // 2, 64 * (h % 2):64 * (h % 2) + 64,
                                          512 * g:512 * (g + 1)],
                       COMPST[0:64, 0:512], st, [(vx, kb.now(vx))])
                C.setfree("cmpst", st, kb.now(st))
            C.setfree(("cka", hh), mm, kb.now(mm))
            C.setfree("cva", mm, kb.now(mm))
        return [(mm, kb.now(mm)), (ev, kb.now(ev)), (vx, kb.now(vx)),
                (pxp, kb.now(pxp)), (st, kb.now(st))]
    return hook, st


def alloc_l1_final(C):
    if T >= 4096:
        C.sb["CMB"] = TV(C.sb["XN1"], 0, None, 4096)
        C.sb["COMPL"] = TV(C.sb["XN2"], 0, None, 1024)
        C.sb["CSQ2"] = TV(C.sb["XN2"], 1024, None, 512)
        C.sb["GWS"] = TV(C.sb["XN0"], 0, None, 2048)
        C.sb["RSBG"] = TV(C.sb["XN3"], 0, F32, 1024)
    else:
        C.alloc("CMB", [128, 4096], BF16)
        C.alloc("COMPL", [128, 1024], BF16)
        C.alloc("CSQ2", [128, 512], BF16)
        C.alloc("GWS", [128, 2048], BF16)
        C.alloc("RSBG", [128, 1024], F32)
    C.alloc("GSR", [1, 1024], F32)
    C.alloc("GCOL", [128, 16], F32)
    C.alloc("HCOL", [128, 8], F32)
    C.alloc("HSQg", [128, 8], F32)
    C.alloc("HNg", [128, 8], BF16)
    C.alloc("GSB", [128, 16], BF16)
    C.alloc("LNLW", [128, 16], F32)
    C.alloc("GB", [128, 8], F32)
    C.alloc("GLW", [128, 16], F32)
    C.alloc("G2S", [128, 16], BF16)
    C.alloc("GB2", [2, 1], F32)
    C.alloc("I2B", [2, 2], BF16)
    C.alloc("ONESCB", [128, 1], BF16)
    C.alloc("ONESCF", [128, 1], F32)
    C.alloc("ONESR2", [1, 128], F32)
    C.alloc("M2R2", [1, 512], F32)


def emit_l1_final(C, tag, CPWd, CPBd, GW1d, GB1d, GLWd, GW2d, GB2d, LLWd,
                  comp_hbm, comp_ready, of_ready, sink):
    kb = C.kb
    XN = [C.sb[f"XN{c}"] for c in range(NDT)]
    OF = [C.sb[f"OF{c}"] for c in range(NDT)]
    CMB, COMPL, CSQ2 = C.sb["CMB"], C.sb["COMPL"], C.sb["CSQ2"]
    GWS, RSBG, GSR, GCOL = C.sb["GWS"], C.sb["RSBG"], C.sb["GSR"], C.sb["GCOL"]
    HCOL, HSQ, HN, GSB = C.sb["HCOL"], C.sb["HSQg"], C.sb["HNg"], C.sb["GSB"]
    LNLW, GB, GLW, G2S, GB2, I2B = (C.sb["LNLW"], C.sb["GB"], C.sb["GLW"],
                                    C.sb["G2S"], C.sb["GB2"], C.sb["I2B"])
    ONESCB, ONESR2 = C.sb["ONESCB"], C.sb["ONESR2"]
    WOSTA, WOSTB = C.sb["WOSTA"], C.sb["WOSTB"]
    DELTA, OPB = C.sb["DELTA"], C.sb["OPB"]
    ld, mm, ev, vx = f"{tag}fl", f"{tag}fm2", f"{tag}fe", f"{tag}fv"
    dc2 = f"{tag}dc"
    dcb = C.kb.now(dc2)
    epsap, epsw = C.fconst(1e-5)
    kb.dma("sync", LNLW[:, :], LLWd[:, :], ld, [of_ready])
    kb.dma("sync", GB[:, :], GB1d[:, :], ld)
    kb.dma("sync", GLW[:, :], GLWd[:, :], ld)
    kb.dma("sync", G2S[:, :], GW2d[:, :], ld)
    kb.dma("sync", GB2[0:2, :], GB2d[:, :], ld)
    kb.dma("sync", I2B[0:2, :], C.const("c_I2B", as_bf16(np.eye(2))), ld)
    kb.dma("gpsimd", OPB[:, 0:1024], CPBd[:, :], f"{tag}ob2")
    ldv = kb.now(ld)

    def ln_stats_rows(src_tiles, col_w, bank_a, bank_b, sq_scr, waits):
        mm_acc(C, C.ps[0:1, 512 * bank_a:512 * bank_a + col_w],
               [(list(waits) if cc == 0 else [], ONESCB[:, :], src_tiles[cc]) for cc in range(8)],
               inc=mm, waits=[C.fwait(("bk", bank_a))])
        for cc in range(8):
            kb.op("scalar", lambda E, cc=cc: E.activation(sq_scr, src_tiles[cc], AF.Square),
                  waits=(list(waits) + [C.fwait("sqs")]) if cc == 0 else [], inc=ev)
            mm_acc(C, C.ps[0:1, 512 * bank_b:512 * bank_b + col_w],
                   [([(ev, kb.now(ev))], ONESCB[:, :], sq_scr)],
                   inc=mm, waits=[C.fwait(("bk", bank_b))] if cc == 0 else (),
                   start=(cc == 0), stop=(cc == 7))
        smv = kb.now(mm)
        C.setfree("sqs", mm, smv)
        kb.op("vector", lambda E, col_w=col_w, bank_a=bank_a: E.tensor_scalar(
            GSR[0:1, 0:col_w], C.ps[0:1, 512 * bank_a:512 * bank_a + col_w],
            1.0 / D, None, ALU.mult), waits=[(mm, smv)], inc=vx)
        kb.op("vector", lambda E, col_w=col_w, bank_b=bank_b: E.tensor_scalar(
            GSR[0:1, 512:512 + col_w], C.ps[0:1, 512 * bank_b:512 * bank_b + col_w],
            1.0 / D, None, ALU.mult), inc=vx)
        C.setfree(("bk", bank_a), vx, kb.now(vx))
        C.setfree(("bk", bank_b), vx, kb.now(vx))
        kb.wait("vector", [(vx, kb.now(vx))])
        kb.op("vector", lambda E, col_w=col_w: E.tensor_tensor(
            GSR[0:1, 256:256 + col_w] if col_w <= 256 else GSR[0:1, 0:col_w],
            GSR[0:1, 0:col_w], GSR[0:1, 0:col_w], ALU.mult)
            if False else E.tensor_tensor(C.sb["M2R"][0:1, 0:col_w]
                                          if col_w <= 256 else C.sb["M2R2"][0:1, 0:col_w],
                                          GSR[0:1, 0:col_w], GSR[0:1, 0:col_w], ALU.mult),
            inc=vx)
        kb.wait("vector", [(vx, kb.now(vx))])
        kb.op("vector", lambda E, col_w=col_w: E.tensor_tensor(
            GSR[0:1, 512:512 + col_w], GSR[0:1, 512:512 + col_w],
            C.sb["M2R"][0:1, 0:col_w] if col_w <= 256 else C.sb["M2R2"][0:1, 0:col_w],
            ALU.subtract), inc=vx)
        kb.op("scalar", lambda E, col_w=col_w: E.activation(
            GSR[0:1, 512:512 + col_w], GSR[0:1, 512:512 + col_w], AF.Sqrt, bias=epsap[0:1, :]),
            waits=[(vx, kb.now(vx)), epsw], inc=ev)
        kb.wait("vector", [(ev, kb.now(ev))])
        kb.op("vector", lambda E, col_w=col_w: E.reciprocal(GSR[0:1, 512:512 + col_w],
                                                            GSR[0:1, 512:512 + col_w]), inc=vx)
        kb.wait("vector", [(vx, kb.now(vx))])
        kb.op("vector", lambda E, col_w=col_w: E.scalar_tensor_tensor(
            GSR[0:1, 0:col_w], GSR[0:1, 0:col_w], -1.0, GSR[0:1, 512:512 + col_w],
            ALU.mult, ALU.mult), inc=vx)
        return kb.now(vx)

    sv = ln_stats_rows([OF[c][:, T - 1:T] for c in range(NDT)], 1, 4, 5,
                       CSQ2[:, 0:1], [of_ready, comp_ready, (ld, ldv)])
    mm_acc(C, C.ps[0:128, 512 * 4:512 * 4 + 1],
           [([(vx, sv)], ONESR2[0:1, :], GSR[0:1, 512:513])], inc=mm, waits=[C.fwait(("bk", 4))])
    mm_acc(C, C.ps[0:128, 512 * 5:512 * 5 + 1],
           [((), ONESR2[0:1, :], GSR[0:1, 0:1])], inc=mm, waits=[C.fwait(("bk", 5))])
    bb = kb.now(mm)
    kb.op("scalar", lambda E: E.copy(GCOL[:, 0:1], C.ps[0:128, 512 * 4:512 * 4 + 1]),
          waits=[(mm, bb)], inc=ev)
    kb.op("scalar", lambda E: E.copy(GCOL[:, 1:2], C.ps[0:128, 512 * 5:512 * 5 + 1]), inc=ev)
    gbv = kb.now(ev)
    C.setfree(("bk", 4), ev, gbv)
    C.setfree(("bk", 5), ev, gbv)
    for c in range(NDT):
        kb.op("vector", lambda E, c=c: E.tensor_scalar(
            GSB[:, c:c + 1], OF[c][:, T - 1:T], GCOL[:, 0:1], GCOL[:, 1:2],
            ALU.mult, ALU.add), waits=[(ev, gbv)], inc=vx)
        kb.op("vector", lambda E, c=c: E.tensor_scalar(
            GSB[:, c:c + 1], GSB[:, c:c + 1], LNLW[:, c:c + 1], LNLW[:, 8 + c:8 + c + 1],
            ALU.mult, ALU.add), waits=[(ld, ldv)], inc=vx)
    gh = f"{tag}gh"
    for c in range(NDT):
        kb.op("sync", lambda E, c=c: E.dma_start(
            out=GSB[:, 8 + c:8 + c + 1], in_=comp_hbm[c, :, T - 1:T],
            allow_slow_non_contiguous=True), [comp_ready], inc=gh, inc_n=16)
    fbv, ghv = kb.now(vx), kb.now(gh)
    for c2 in range(NDT):
        gw_w = [C.fwait("gws")]
        if c2 == 0:
            gw_w += [of_ready, comp_ready]
        kb.dma("sync", GWS[:, 0:2048], GW1d[c2, :, :], f"{tag}gw", gw_w)
        gwv = kb.now(f"{tag}gw")
        mm_acc(C, C.ps[0:128, 512 * 4 + 4 * c2:512 * 4 + 4 * c2 + 1],
               [([(vx, fbv), (gh, ghv), (f"{tag}gw", gwv)] if kk == 0 else [],
                 GWS[:, 128 * kk:128 * (kk + 1)], GSB[:, kk:kk + 1]) for kk in range(16)],
               inc=mm, waits=[C.fwait(("bk", 4))] if c2 == 0 else ())
        C.setfree("gws", mm, kb.now(mm))
    hv = kb.now(mm)
    for c2 in range(NDT):
        kb.op("vector", lambda E, c2=c2: E.tensor_tensor(
            HCOL[:, c2:c2 + 1], C.ps[0:128, 512 * 4 + 4 * c2:512 * 4 + 4 * c2 + 1],
            GB[:, c2:c2 + 1], ALU.add), waits=[(mm, hv), (ld, ldv)], inc=vx)
    C.setfree(("bk", 4), vx, kb.now(vx))
    kb.op("vector", lambda E: E.reduce_sum(GCOL[:, 4:5], HCOL[:, :], axis=AX),
          waits=[(vx, kb.now(vx))], inc=vx)
    kb.op("scalar", lambda E: E.activation(HSQ[:, :], HCOL[:, :], AF.Square),
          waits=[(vx, kb.now(vx))], inc=ev)
    kb.op("vector", lambda E: E.reduce_sum(GCOL[:, 5:6], HSQ[:, :], axis=AX),
          waits=[(ev, kb.now(ev))], inc=vx)
    kb.wait("vector", [(vx, kb.now(vx))])
    mm_acc(C, C.ps[0:1, 512 * 4:512 * 4 + 1],
           [([(vx, kb.now(vx))], C.sb["ONESCF"][:, :], GCOL[:, 4:5])], inc=mm,
           waits=[C.fwait(("bk", 4))])
    mm_acc(C, C.ps[0:1, 512 * 4 + 4:512 * 4 + 5],
           [((), C.sb["ONESCF"][:, :], GCOL[:, 5:6])], inc=mm)
    s2v = kb.now(mm)
    kb.op("vector", lambda E: E.tensor_scalar(GSR[0:1, 0:1], C.ps[0:1, 512 * 4:512 * 4 + 1],
                                              1.0 / D, None, ALU.mult), waits=[(mm, s2v)], inc=vx)
    kb.op("vector", lambda E: E.tensor_scalar(GSR[0:1, 1:2], C.ps[0:1, 512 * 4 + 4:512 * 4 + 5],
                                              1.0 / D, None, ALU.mult), inc=vx)
    C.setfree(("bk", 4), vx, kb.now(vx))
    kb.wait("vector", [(vx, kb.now(vx))])
    kb.op("vector", lambda E: E.tensor_tensor(GSR[0:1, 2:3], GSR[0:1, 0:1], GSR[0:1, 0:1],
                                              ALU.mult), inc=vx)
    kb.wait("vector", [(vx, kb.now(vx))])
    kb.op("vector", lambda E: E.tensor_tensor(GSR[0:1, 1:2], GSR[0:1, 1:2], GSR[0:1, 2:3],
                                              ALU.subtract), inc=vx)
    kb.op("scalar", lambda E: E.activation(GSR[0:1, 1:2], GSR[0:1, 1:2], AF.Sqrt,
                                           bias=epsap[0:1, :]),
          waits=[(vx, kb.now(vx)), epsw], inc=ev)
    kb.wait("vector", [(ev, kb.now(ev))])
    kb.op("vector", lambda E: E.reciprocal(GSR[0:1, 1:2], GSR[0:1, 1:2]), inc=vx)
    kb.wait("vector", [(vx, kb.now(vx))])
    kb.op("vector", lambda E: E.scalar_tensor_tensor(GSR[0:1, 0:1], GSR[0:1, 0:1], -1.0,
                                                     GSR[0:1, 1:2], ALU.mult, ALU.mult), inc=vx)
    s3v = kb.now(vx)
    mm_acc(C, C.ps[0:128, 512 * 4:512 * 4 + 1],
           [([(vx, s3v)], ONESR2[0:1, :], GSR[0:1, 1:2])], inc=mm, waits=[C.fwait(("bk", 4))])
    mm_acc(C, C.ps[0:128, 512 * 4 + 4:512 * 4 + 5],
           [((), ONESR2[0:1, :], GSR[0:1, 0:1])], inc=mm)
    b2v = kb.now(mm)
    kb.op("scalar", lambda E: E.copy(GCOL[:, 6:7], C.ps[0:128, 512 * 4:512 * 4 + 1]),
          waits=[(mm, b2v)], inc=ev)
    kb.op("scalar", lambda E: E.copy(GCOL[:, 7:8], C.ps[0:128, 512 * 4 + 4:512 * 4 + 5]), inc=ev)
    C.setfree(("bk", 4), ev, kb.now(ev))
    kb.wait("vector", [(ev, kb.now(ev))])
    kb.op("vector", lambda E: E.tensor_scalar(HCOL[:, :], HCOL[:, :], GCOL[:, 6:7],
                                              GCOL[:, 7:8], ALU.mult, ALU.add), inc=vx)
    kb.wait("vector", [(vx, kb.now(vx))])
    for c2 in range(NDT):
        kb.op("vector", lambda E, c2=c2: E.tensor_scalar(
            HN[:, c2:c2 + 1], HCOL[:, c2:c2 + 1], GLW[:, c2:c2 + 1], GLW[:, 8 + c2:8 + c2 + 1],
            ALU.mult, ALU.add), inc=vx)
    kb.wait("vector", [(vx, kb.now(vx))])
    kb.op("vector", lambda E: E.tensor_relu(HN[:, :], HN[:, :]), inc=vx)
    hnv = kb.now(vx)
    mm_acc(C, C.ps[0:2, 512 * 4:512 * 4 + 1],
           [([(vx, hnv), (ld, ldv)] if c2 == 0 else [], G2S[:, 2 * c2:2 * c2 + 2],
             HN[:, c2:c2 + 1]) for c2 in range(NDT)],
           inc=mm, waits=[C.fwait(("bk", 4))])
    zv = kb.now(mm)
    kb.op("vector", lambda E: E.tensor_tensor(GCOL[0:2, 8:9], C.ps[0:2, 512 * 4:512 * 4 + 1],
                                              GB2[0:2, :], ALU.add), waits=[(mm, zv)], inc=vx)
    C.setfree(("bk", 4), vx, kb.now(vx))
    kb.wait("vector", [(vx, kb.now(vx))])
    kb.op("vector", lambda E: E.tensor_copy(GSB[0:2, 0:1], GCOL[0:2, 8:9]), inc=vx)
    zc = kb.now(vx)
    mm_acc(C, C.ps[0:1, 512 * 4:512 * 4 + 2],
           [([(vx, zc)], GSB[0:2, 0:1], I2B[0:2, 0:2])], inc=mm, waits=[C.fwait(("bk", 4))])
    ztv = kb.now(mm)
    kb.op("vector", lambda E: E.tensor_copy(GSR[0:1, 6:8], C.ps[0:1, 512 * 4:512 * 4 + 2]),
          waits=[(mm, ztv)], inc=vx)
    kb.wait("vector", [(vx, kb.now(vx))])
    kb.op("vector", lambda E: E.tensor_tensor(GSR[0:1, 4:5], GSR[0:1, 6:7],
                                              GSR[0:1, 7:8], ALU.subtract), inc=vx)
    C.setfree(("bk", 4), vx, kb.now(vx))
    kb.wait("scalar", [(vx, kb.now(vx))])
    kb.op("scalar", lambda E: E.activation(GSR[0:1, 5:6], GSR[0:1, 4:5], AF.Sigmoid), inc=ev)
    kb.op("scalar", lambda E: E.activation(GSR[0:1, 6:7], GSR[0:1, 4:5], AF.Sigmoid, scale=-1.0),
          inc=ev)
    sgv = kb.now(ev)
    mm_acc(C, C.ps[0:128, 512 * 4:512 * 4 + 1],
           [([(ev, sgv)], ONESR2[0:1, :], GSR[0:1, 5:6])], inc=mm, waits=[C.fwait(("bk", 4))])
    mm_acc(C, C.ps[0:128, 512 * 4 + 4:512 * 4 + 5],
           [((), ONESR2[0:1, :], GSR[0:1, 6:7])], inc=mm)
    gv2 = kb.now(mm)
    kb.op("scalar", lambda E: E.copy(GCOL[:, 2:3], C.ps[0:128, 512 * 4:512 * 4 + 1]),
          waits=[(mm, gv2)], inc=ev)
    kb.op("scalar", lambda E: E.copy(GCOL[:, 3:4], C.ps[0:128, 512 * 4 + 4:512 * 4 + 5]), inc=ev)
    C.setfree(("bk", 4), ev, kb.now(ev))
    ggv = kb.now(ev)

    for c in range(NDT):
        for n in range(2):
            flat = 1024 * c + 512 * n
            t_ = WOSTA if flat < 4096 else WOSTB
            kb.dma("sync", t_[:, flat % 4096:flat % 4096 + 512],
                   CPWd[128 * c:128 * (c + 1), 512 * n:512 * (n + 1)], f"{tag}cw2",
                   [of_ready] if c == 0 and n == 0 else [])
    cwv2 = kb.now(f"{tag}cw2")
    for tc in range(T // 512):
        tsl = slice(512 * tc, 512 * (tc + 1))
        sv = ln_stats_rows([OF[c][:, tsl] for c in range(NDT)], 512, 4, 5,
                           CSQ2[:, 0:512], [of_ready, (ev, ggv)])
        mm_acc(C, C.ps[0:128, 512 * 4:512 * 4 + 512],
               [([(vx, sv)], ONESR2[0:1, :], GSR[0:1, 512:1024])], inc=mm,
               waits=[C.fwait(("bk", 4))])
        mm_acc(C, C.ps[0:128, 512 * 5:512 * 5 + 512],
               [((), ONESR2[0:1, :], GSR[0:1, 0:512])], inc=mm, waits=[C.fwait(("bk", 5))])
        bcv = kb.now(mm)
        kb.op("scalar", lambda E: E.copy(RSBG[:, 0:512], C.ps[0:128, 512 * 4:512 * 4 + 512]),
              waits=[(mm, bcv), C.fwait("rsbg")], inc=ev)
        kb.op("scalar", lambda E: E.copy(RSBG[:, 512:1024], C.ps[0:128, 512 * 5:512 * 5 + 512]),
              inc=ev)
        rbv = kb.now(ev)
        C.setfree(("bk", 4), ev, rbv)
        C.setfree(("bk", 5), ev, rbv)
        for c in range(NDT):
            cl = kb.dma("sync", COMPL[:, 512 * (c % 2):512 * (c % 2) + 512],
                        comp_hbm[c, :, tsl], f"{tag}cl", [comp_ready, C.fwait(("cpl", c % 2))])
            kb.op("vector", lambda E, c=c, tsl=tsl: E.tensor_tensor(
                CMB[:, 512 * c:512 * (c + 1)], OF[c][:, tsl], RSBG[:, 0:512], ALU.mult),
                waits=[(ev, rbv), C.fwait(("cmb", c))], inc=vx)
            kb.wait("vector", [(vx, kb.now(vx))])
            kb.op("vector", lambda E, c=c: E.tensor_tensor(
                CMB[:, 512 * c:512 * (c + 1)], CMB[:, 512 * c:512 * (c + 1)],
                RSBG[:, 512:1024], ALU.add), inc=vx)
            kb.wait("vector", [(vx, kb.now(vx))])
            kb.op("vector", lambda E, c=c: E.tensor_scalar(
                CMB[:, 512 * c:512 * (c + 1)], CMB[:, 512 * c:512 * (c + 1)],
                LNLW[:, c:c + 1], LNLW[:, 8 + c:8 + c + 1], ALU.mult, ALU.add),
                waits=[(ld, ldv)], inc=vx)
            kb.wait("vector", [(vx, kb.now(vx))])
            kb.op("vector", lambda E, c=c: E.tensor_scalar(
                CMB[:, 512 * c:512 * (c + 1)], CMB[:, 512 * c:512 * (c + 1)],
                GCOL[:, 2:3], None, ALU.mult), waits=[(ev, ggv)], inc=vx)
            kb.wait("vector", [(vx, kb.now(vx))])
            kb.op("vector", lambda E, c=c: E.scalar_tensor_tensor(
                CMB[:, 512 * c:512 * (c + 1)], COMPL[:, 512 * (c % 2):512 * (c % 2) + 512],
                GCOL[:, 3:4], CMB[:, 512 * c:512 * (c + 1)], ALU.mult, ALU.add),
                waits=[(f"{tag}cl", cl)], inc=vx)
            C.setfree(("cpl", c % 2), vx, kb.now(vx))
        cmv = kb.now(vx)
        for tt in range(4):
            i = 4 * tc + tt
            for n in range(2):
                b = n
                flatsl = lambda c, n2: (WOSTA if 1024 * c + 512 * n2 < 4096 else WOSTB,
                                        (1024 * c + 512 * n2) % 4096)
                mm_acc(C, C.bank(b),
                       [((), CMB[:, 512 * c + 128 * tt:512 * c + 128 * tt + 128],
                         flatsl(c, n)[0][:, flatsl(c, n)[1]:flatsl(c, n)[1] + 512])
                        for c in range(NDT)],
                       inc=mm, waits=[(vx, cmv), (f"{tag}cw2", cwv2), C.fwait(("bk", b))])
                mv = kb.now(mm)
                kb.op("vector", lambda E, b=b, n=n: E.tensor_tensor(
                    DELTA[:, 0:512], C.bank(b), OPB[:, 512 * n:512 * n + 512], ALU.add),
                    waits=[(mm, mv), (dc2, dcb + 16 * (2 * i + n)), (f"{tag}ob2", 16)],
                    inc=f"{tag}dv")
                C.setfree(("bk", b), f"{tag}dv", kb.now(f"{tag}dv"))
                sink(i, n, DELTA[:, 0:512], [(f"{tag}dv", kb.now(f"{tag}dv"))])
        for c in range(NDT):
            C.setfree(("cmb", c), mm, kb.now(mm))
    return dc2, kb.now(dc2)


def build_l1_attn_standalone(env=None):
    nc, kb, stack, pref, own = _env_open(env)
    if env is None:
        xn_hbm = nc.dram_tensor("xn", [T, D], BF16, kind="ExternalInput").ap()
        out = nc.dram_tensor("out", [T, D], F32, kind="ExternalOutput").ap()
    else:
        xn_hbm, out = env["xn"], env["out"]
    cattn = nc.dram_tensor("l1_cattn", [D, 3072], BF16, kind="ExternalInput").ap()
    WQd = nc.dram_tensor("l1_wq", [D, D], BF16, kind="ExternalInput").ap()
    WKd = nc.dram_tensor("l1_wk", [D, D], BF16, kind="ExternalInput").ap()
    WVAd = nc.dram_tensor("l1_wva", [D, 1040], BF16, kind="ExternalInput").ap()
    BQd = nc.dram_tensor("l1_bq", [128, 8], F32, kind="ExternalInput").ap()
    BKd = nc.dram_tensor("l1_bk", [128, 8], F32, kind="ExternalInput").ap()
    BVd = nc.dram_tensor("l1_bv", [8, 128, 130], F32, kind="ExternalInput").ap()
    BQK2 = nc.dram_tensor("l1_bqk2", [128, 16], F32, kind="ExternalInput").ap()
    CWd = nc.dram_tensor("l1_cw", [128, 2048], BF16, kind="ExternalInput").ap()
    CBd = nc.dram_tensor("l1_cb", [64, 1], F32, kind="ExternalInput").ap()
    CLNd = nc.dram_tensor("l1_cln", [64, 2], F32, kind="ExternalInput").ap()
    NBv = (T - 32) // 16 + 1
    CM0d = nc.dram_tensor("l1_cm0", [128, NBv], BF16, kind="ExternalInput").ap()
    CM1d = nc.dram_tensor("l1_cm1", [128, NBv], BF16, kind="ExternalInput").ap()
    CMT0d = nc.dram_tensor("l1_cmt0", [128, 256], BF16, kind="ExternalInput").ap()
    CMT1d = nc.dram_tensor("l1_cmt1", [128, 256], BF16, kind="ExternalInput").ap()
    CPWd = nc.dram_tensor("l1_cpw", [D, D], BF16, kind="ExternalInput").ap()
    CPBd = nc.dram_tensor("l1_cpb", [128, 1024], F32, kind="ExternalInput").ap()
    GW1d = nc.dram_tensor("l1_gw1", [8, 128, 2048], BF16, kind="ExternalInput").ap()
    GB1d = nc.dram_tensor("l1_gb1", [128, 8], F32, kind="ExternalInput").ap()
    GLWd = nc.dram_tensor("l1_glw", [128, 16], F32, kind="ExternalInput").ap()
    GW2d = nc.dram_tensor("l1_gw2", [128, 16], BF16, kind="ExternalInput").ap()
    GB2d = nc.dram_tensor("l1_gb2", [2, 1], F32, kind="ExternalInput").ap()
    LLWd = nc.dram_tensor("l1_llw", [128, 16], F32, kind="ExternalInput").ap()
    comp_hbm = nc.dram_tensor(pref + "l1_comp", [8, 128, T], BF16, kind="Internal").ap()

    P = host_consts_common()
    C = Ctx(nc, kb, stack, pref)
    C.l1_cattn = cattn
    for c in range(NDT):
        C.alloc(f"XN{c}", [128, T], BF16)
    alloc_mha(C)
    alloc_mha_bias(C)
    alloc_l1(C)
    alloc_l1_final(C)
    if T >= 4096:
        C.sb["WOSTA"] = C.sb["QA"]
        C.sb["WOSTB"] = C.sb["QB"]
        C.sb["DELTA"] = C.sb["KA"].bitcast(F32)
        C.sb["OPB"] = C.sb["KB_"]
    else:
        C.sb["WOSTA"] = C.alloc("WOSTAd", [128, 4096], BF16)
        C.sb["WOSTB"] = C.alloc("WOSTBd", [128, 4096], BF16)
        C.sb["DELTA"] = C.alloc("DELTAd", [128, 512], F32)
        C.sb["OPB"] = C.alloc("OPBd", [128, 1024], BF16)
    C.alloc("CBCr", [64, 1], F32)
    C.sb["CBC"] = C.sb["CBCr"]
    C.alloc("CLW", [64, 2], F32)
    C.alloc("onerow_c", [1, 256], BF16)
    C.alloc("onecol_c", [128, 1], BF16)
    C.alloc_ps()

    cs = emit_mha_consts(C, P)
    cld = "ccl"
    kb.dma("sync", C.sb["CWV"][:, :], CWd[:, :], cld)
    kb.dma("sync", C.sb["CBCr"][0:64, :], CBd[:, :], cld)
    kb.dma("sync", C.sb["CLW"][0:64, :], CLNd[:, :], cld)
    kb.dma("sync", C.sb["CM0"][:, :], CM0d[:, :], cld)
    kb.dma("sync", C.sb["CM1"][:, :], CM1d[:, :], cld)
    kb.dma("sync", C.sb["CMT0"][:, :], CMT0d[:, :], cld)
    kb.dma("sync", C.sb["CMT1"][:, :], CMT1d[:, :], cld)
    kb.dma("sync", C.sb["IDF3"][:, :], C.const("c_IF32", as_f32(np.eye(128))), cld)
    kb.dma("sync", C.sb["ONE64"][0:64, :], C.const("c_one64", as_f32(np.ones((64, 1)))), cld)
    kb.dma("sync", C.sb["onerow_c"][0:1, :], C.const("c_or256", as_bf16(np.ones((1, 256)))), cld)
    kb.dma("sync", C.sb["onecol_c"][:, :], C.const("c_onescol_bf", as_bf16(np.ones((128, 1)))), cld)
    kb.dma("sync", C.sb["BQK2"][:, :], BQK2[:, :], cld)
    kb.dma("sync", C.sb["ONESCB"][:, :], C.const("c_onescol_bf", as_bf16(np.ones((128, 1)))), cld)
    kb.dma("sync", C.sb["ONESR2"][0:1, :], C.const("c_onesrow", as_f32(np.ones((1, 128)))), cld)
    kb.dma("sync", C.sb["ONESCF"][:, :], C.const("c_onescol_f", as_f32(np.ones((128, 1)))), cld)
    conv_ready = (cld, kb.now(cld))

    xr = emit_xn_from_hbm(C, xn_hbm, "x0")
    hook, cst = emit_l1_comp(C, "a", CWd, CBd, CLNd, comp_hbm, conv_ready)
    ofs = emit_mha(C, "a", WQd, WKd, WVAd, xr, cs, causal=True, q_scale=0.125,
                   biases=dict(bq=BQd, bk=BKd, bv=BVd), rms=None, pair_hook=hook)
    comp_ready = ("acs", kb.now("acs"))

    def sink(i, n, ap, waits):
        kb.dma("gpsimd", out[128 * i:128 * (i + 1), 512 * n:512 * (n + 1)], ap, "l1dc", waits)

    emit_l1_final(C, "l1", CPWd, CPBd, GW1d, GB1d, GLWd, GW2d, GB2d, LLWd,
                  comp_hbm, comp_ready, ofs, sink)
    kb.wait("gpsimd", [("l1dc", kb.now("l1dc"))])
    if own:
        kb.finalize(nc, stack)
    return nc, C, stack


def prep_l1_attn_inputs(I):
    W = I["L1_attn_c_attn_w"].astype(np.float32)
    b = I["L1_attn_c_attn_b"].astype(np.float32)
    bv = b[2048:3072]
    bvs = np.zeros((8, 128, 130), np.float32)
    for p_ in range(8):
        for hh in range(2):
            h = 2 * p_ + hh
            bvs[p_, :, 65 * hh:65 * hh + 64] = bv[64 * h:64 * h + 64][None, :]
            bvs[p_, :, 65 * hh + 64] = 1.0
    NEGM = NEG
    NBv = (T - 32) // 16 + 1
    cm0 = np.zeros((128, NBv), np.float32)
    cm1 = np.zeros((128, NBv), np.float32)
    for qi in range(128):
        cm0[qi, qi + 1:] = NEGM
        if 128 + qi + 1 < NBv:
            cm1[qi, 128 + qi + 1:] = NEGM
    cmt0 = np.zeros((128, 256), np.float32)
    cmt1 = np.zeros((128, 256), np.float32)
    for k_ in range(128):
        cmt0[k_, :k_] = NEGM
        cmt1[k_, :min(128 + k_, 256)] = NEGM
    gw1 = I["L1_attn_gate_w1"].astype(np.float32)
    gw1s = np.zeros((8, 128, 2048), np.float32)
    for c2 in range(8):
        for kk in range(16):
            gw1s[c2, :, 128 * kk:128 * (kk + 1)] = gw1[128 * kk:128 * (kk + 1),
                                                       128 * c2:128 * (c2 + 1)]
    gw2 = I["L1_attn_gate_w2"].astype(np.float32)
    gw2s = np.zeros((128, 16), np.float32)
    for c2 in range(8):
        gw2s[:, 2 * c2:2 * c2 + 2] = gw2[128 * c2:128 * (c2 + 1), :]
    return {
        "l1_cattn": as_bf16(W),
        "l1_wq": as_bf16(W[:, 0:1024]),
        "l1_wk": as_bf16(W[:, 1024:2048]),
        "l1_wva": as_bf16(aug_v_weight(W[:, 2048:3072], 16)),
        "l1_bq": as_f32(b[0:1024].reshape(8, 128).T),
        "l1_bk": as_f32(b[1024:2048].reshape(8, 128).T),
        "l1_bv": as_f32(bvs),
        "l1_bqk2": as_f32(np.concatenate([b[1024:2048].reshape(8, 128).T,
                                          b[2048:3072].reshape(8, 128).T], axis=1)),
        "l1_cw": as_bf16(np.tile(I["L1_attn_conv_w"].astype(np.float32).transpose(1, 2, 0)
                         .reshape(64, 2048), (2, 1))),
        "l1_cb": as_f32(I["L1_attn_conv_b"][:, None]),
        "l1_cln": as_f32(np.stack([I["L1_attn_comp_ln_w"], I["L1_attn_comp_ln_b"]], axis=1)),
        "l1_cm0": as_bf16(cm0), "l1_cm1": as_bf16(cm1), "l1_cmt0": as_bf16(cmt0), "l1_cmt1": as_bf16(cmt1),
        "l1_cpw": as_bf16(I["L1_attn_c_proj_w"]),
        "l1_cpb": as_f32(np.tile(I["L1_attn_c_proj_b"][None, :], (128, 1))),
        "l1_gw1": as_bf16(gw1s),
        "l1_gb1": as_f32(I["L1_attn_gate_b1"].reshape(8, 128).T),
        "l1_glw": as_f32(np.concatenate([I["L1_attn_gate_ln_w"].reshape(8, 128).T,
                                         I["L1_attn_gate_ln_b"].reshape(8, 128).T], axis=1)),
        "l1_gw2": as_bf16(gw2s),
        "l1_gb2": as_f32(I["L1_attn_gate_b2"][:, None]),
        "l1_llw": as_f32(np.concatenate([I["L1_attn_ln_local_w"].reshape(8, 128).T,
                                         I["L1_attn_ln_local_b"].reshape(8, 128).T], axis=1)),
    }
